# version 03
# baseline (speedup 1.0000x reference)
import numpy as np
import ml_dtypes
from contextlib import ExitStack

import concourse.bass as bass
import concourse.mybir as mybir
from concourse.bass_utils import run_bass_kernel_spmd

F32 = mybir.dt.float32
BF16 = mybir.dt.bfloat16
I32 = mybir.dt.int32
AF = mybir.ActivationFunctionType
ALU = mybir.AluOpType
AX = mybir.AxisListType

NTOK, DM, TILE = 2048, 1024, 128
NT = NTOK // TILE
DK = DM // TILE
FFI = 2816
NKI = FFI // TILE
H, HD = 16, 64
MGS = [(0, 512), (512, 512), (1024, 512), (1536, 512), (2048, 512), (2560, 256)]
QT = 4
NQ = NT // QT
GT = 2
NGRP = NT // GT
GTOK = GT * TILE

INV6 = float(np.float32(1.0) / np.float32(6.0))
ANDM = float(np.int64(0xFFC00000) - (1 << 32))
RD = float(np.float32(1.0) / np.float32(DM))
EPS = 1e-5

PHASES = ("ffn1", "attn", "ffn2")


def build_program():
    nc = bass.Bass("TRN2", target_bir_lowering=False)

    dx = nc.dram_tensor("x", [NTOK, DM], F32, kind="ExternalInput").ap()
    dwq = nc.dram_tensor("wq", [DM, DM], F32, kind="ExternalInput").ap()
    dwk = nc.dram_tensor("wk", [DM, DM], F32, kind="ExternalInput").ap()
    dwv = nc.dram_tensor("wv", [DM, DM], F32, kind="ExternalInput").ap()
    dwo = nc.dram_tensor("wo", [DM, DM], F32, kind="ExternalInput").ap()
    dwg1 = nc.dram_tensor("wg1", [DM, HD], F32, kind="ExternalInput").ap()
    dwg2 = nc.dram_tensor("wg2", [HD, DM], F32, kind="ExternalInput").ap()
    dgam = nc.dram_tensor("d_gammaT", [128, DM], F32, kind="ExternalInput").ap()
    dbet = nc.dram_tensor("d_betaT", [128, DM], F32, kind="ExternalInput").ap()
    dwgate = nc.dram_tensor("wgate", [FFI, DM], F32, kind="ExternalInput").ap()
    dwup = nc.dram_tensor("wup", [FFI, DM], F32, kind="ExternalInput").ap()
    dwdown = nc.dram_tensor("wdown", [DM, FFI], F32, kind="ExternalInput").ap()
    dcos = nc.dram_tensor("d_cosr", [128, NTOK], BF16, kind="ExternalInput").ap()
    dsin = nc.dram_tensor("d_sinr", [128, NTOK], BF16, kind="ExternalInput").ap()
    dp128 = nc.dram_tensor("d_p128t", [128, 128], BF16, kind="ExternalInput").ap()
    dident = nc.dram_tensor("d_ident", [128, 128], BF16, kind="ExternalInput").ap()
    dmask = nc.dram_tensor("d_mask4", [128, 512], BF16, kind="ExternalInput").ap()
    dout = nc.dram_tensor("out", [NTOK, DM], F32, kind="ExternalOutput").ap()
    s_wgT = nc.dram_tensor("s_wgT", [DM, FFI], BF16).ap()
    s_wuT = nc.dram_tensor("s_wuT", [DM, FFI], BF16).ap()
    s_wdT = nc.dram_tensor("s_wdT", [FFI, DM], BF16).ap()
    s_att = {n: nc.dram_tensor(f"s_w{n}", [DM, DM], BF16).ap()
             for n in ("q", "k", "v", "o")}

    es = ExitStack()
    SB = lambda n, s, d: es.enter_context(nc.sbuf_tensor(n, s, d))

    R = SB("R", [128, NT * DM], F32)
    cosr = SB("cosr", [128, NTOK], BF16)
    sinr = SB("sinr", [128, NTOK], BF16)
    p128t = SB("p128t", [128, 128], BF16)
    identb = SB("identb", [128, 128], BF16)
    mask4 = SB("mask4", [128, 512], BF16)
    gammaT = SB("gammaT", [128, DM], F32)
    betaT = SB("betaT", [128, DM], F32)
    wg1b = SB("wg1b", [128, DK * HD], BF16)
    wg2b = SB("wg2b", [64, DM], BF16)
    st_s1 = SB("st_s1", [128, NT], F32)
    st_ss = SB("st_ss", [128, NT], F32)
    st_m = SB("st_m", [128, NT], F32)
    st_v = SB("st_v", [128, NT], F32)
    st_r = SB("st_r", [128, NT], F32)
    st_a = SB("st_a", [128, NT], F32)
    st_t1 = SB("st_t1", [128, NT], F32)
    st_t2 = SB("st_t2", [128, NT], F32)
    wkA = SB("wkA", [128, 2 * 512], F32)
    wkB = SB("wkB", [128, 2 * 512], F32)
    wkC = SB("wkC", [128, 2 * DM], F32)
    wkD = SB("wkD", [128, 2 * DM], F32)
    btA = SB("btA", [128, 2 * DM], BF16)
    btB = SB("btB", [128, 2 * DM], BF16)

    class Ctr:
        def __init__(self, name, auto=False):
            self.h = es.enter_context(nc.semaphore(name))
            self.n = 0
            self.auto = auto

        def inc(self, instr, k=1):
            if self.auto:
                return self.n
            instr.then_inc(self.h, k)
            self.n += k
            return self.n

    ssu = Ctr("ssu")
    slw = [Ctr("slw0"), Ctr("slw1")]
    sgw = Ctr("sgw")
    sgwd = Ctr("sgwd")
    sga = Ctr("sga")
    sfg = [Ctr("sfg0"), Ctr("sfg1")]
    sfd = [Ctr("sfd0"), Ctr("sfd1")]
    sat = [Ctr("sat0"), Ctr("sat1")]
    sgo = Ctr("sgo")
    spe = Ctr("spe")
    sdv = Ctr("sdv", auto=True)
    sac = Ctr("sac", auto=True)
    dma_ctrs = [ssu, slw[0], slw[1], sgw, sgwd, sga, sfg[0], sfg[1],
                sfd[0], sfd[1], sat[0], sat[1], sgo]

    class EngProxy:
        _PASS = ("wait_ge", "dma_start")

        def __init__(self, eng, ctr):
            object.__setattr__(self, "_eng", eng)
            object.__setattr__(self, "_ctr", ctr)

        def __getattr__(self, name):
            f = getattr(self._eng, name)
            if name in EngProxy._PASS or not callable(f):
                return f
            eng, ctr = self._eng, self._ctr

            def wrapped(*a, **kw):
                if ctr.n > 0:
                    eng.wait_ge(ctr.h, ctr.n)
                instr = f(*a, **kw)
                instr.then_inc(ctr.h, 1)
                ctr.n += 1
                return instr
            return wrapped

    blk = es.enter_context(nc.Block())
    T, Y, G = blk.tensor, blk.sync, blk.gpsimd
    V = lambda f: blk.vector(lambda v: f(EngProxy(v, sdv)))
    S = lambda f: blk.scalar(lambda s: f(EngProxy(s, sac)))

    wv3 = lambda ap: ap.rearrange("p (b w) -> p b w", w=16)

    def mkquant(qs):
        q_x, q_i, q_r, q_n, q_c, q_bm, q_sc, q_iv, q_sh = qs

        def bc3(tile, nb):
            return tile[:, 0:nb].unsqueeze(2).broadcast_to([128, nb, 16])

        def emit_quant(v, src_ap, F, dst_ap):
            nb = F // 16
            v.tensor_reduce(q_bm[:, 0:nb], wv3(src_ap), AX.X, ALU.max,
                            apply_absolute_value=True)
            v.tensor_scalar(q_sc[:, 0:nb], q_bm[:, 0:nb], INV6, 1e-12,
                            ALU.mult, ALU.max)
            sdv.inc(v.reciprocal(q_iv[:, 0:nb], q_sc[:, 0:nb]))
            v.wait_ge(sdv.h, sdv.n)
            v.tensor_scalar(q_iv[:, 0:nb], q_iv[:, 0:nb], 2.0, None, ALU.mult)
            v.tensor_scalar(q_sh[:, 0:nb], q_sc[:, 0:nb], 0.5, None, ALU.mult)
            v.tensor_tensor(wv3(q_x[:, 0:F]), wv3(src_ap), bc3(q_iv, nb), ALU.mult)
            v.tensor_scalar(q_i[:, 0:F].bitcast(F32), q_x[:, 0:F], 4.0, -4.0,
                            ALU.min, ALU.max)
            v.tensor_scalar(q_r[:, 0:F], q_i[:, 0:F].bitcast(F32),
                            12582912.0, -12582912.0, ALU.add, ALU.add)
            v.tensor_scalar(q_i[:, 0:F], q_x[:, 0:F].bitcast(I32),
                            float(0x00200000), None, ALU.add)
            v.tensor_scalar(q_i[:, 0:F], q_i[:, 0:F], int(-4194304), None,
                            ALU.bitwise_and)
            v.tensor_copy(q_n[:, 0:F], q_i[:, 0:F].bitcast(F32))
            v.tensor_scalar(q_c[:, 0:F], q_n[:, 0:F], -4.0, 4.0, ALU.max, ALU.min)
            v.tensor_tensor(q_c[:, 0:F], q_n[:, 0:F], q_c[:, 0:F], ALU.subtract)
            v.tensor_tensor(q_n[:, 0:F], q_r[:, 0:F], q_c[:, 0:F], ALU.add)
            return v.tensor_tensor(wv3(dst_ap), wv3(q_n[:, 0:F]), bc3(q_sh, nb),
                                   ALU.mult)
        return emit_quant

    def mk_qscratch(F, tag, alloc):
        return (alloc(f"q_x{tag}", [128, F], F32), alloc(f"q_i{tag}", [128, F], I32),
                alloc(f"q_r{tag}", [128, F], BF16), alloc(f"q_n{tag}", [128, F], BF16),
                alloc(f"q_c{tag}", [128, F], BF16), alloc(f"q_bm{tag}", [128, F // 16], F32),
                alloc(f"q_sc{tag}", [128, F // 16], F32), alloc(f"q_iv{tag}", [128, F // 16], F32),
                alloc(f"q_sh{tag}", [128, F // 16], F32))

    def emit_rsqrt(v, dst, src, t1, t2):
        v.tensor_scalar(t1.bitcast(I32), src.bitcast(I32), int(1), None,
                        ALU.logical_shift_right)
        v.tensor_scalar(t1.bitcast(I32), t1.bitcast(I32), -1.0,
                        float(0x5F3759DF), ALU.mult, ALU.add)
        for _i in range(3):
            v.tensor_tensor(t2, t1, t1, ALU.mult)
            v.tensor_tensor(t2, t2, src, ALU.mult)
            v.tensor_scalar(t2, t2, -0.5, 1.5, ALU.mult, ALU.add)
            v.tensor_tensor(t1, t1, t2, ALU.mult)
        return v.tensor_copy(dst, t1)

    def Rt(t):
        return R[:, t * DM:(t + 1) * DM]

    def emit_ln_stats(tiles, rwaits):
        def _(v):
            for ctr, m in rwaits:
                v.wait_ge(ctr.h, m)
            last = None
            for t in tiles:
                last = v.tensor_reduce(st_s1[:, t:t + 1], Rt(t), AX.X, ALU.add)
            sdv.inc(last)
        V(_)

        def _(s):
            for ctr, m in rwaits:
                s.wait_ge(ctr.h, m)
            s.wait_ge(sdv.h, sdv.n)
            last = None
            for j, t in enumerate(tiles):
                last = s.activation(wkC[:, (j % 2) * DM:(j % 2) * DM + DM],
                                    Rt(t), AF.Square,
                                    accum_out=st_ss[:, t:t + 1])
            sac.inc(last)
        S(_)
        acm = sac.n

        def _(v):
            v.wait_ge(sac.h, acm)
            c0, c1 = min(tiles), max(tiles) + 1
            sl = lambda tt: tt[:, c0:c1]
            v.tensor_scalar(sl(st_m), sl(st_s1), RD, None, ALU.mult)
            v.tensor_scalar(sl(st_v), sl(st_ss), RD, None, ALU.mult)
            v.tensor_tensor(sl(st_t1), sl(st_m), sl(st_m), ALU.mult)
            v.tensor_tensor(sl(st_v), sl(st_v), sl(st_t1), ALU.subtract)
            v.tensor_scalar(sl(st_v), sl(st_v), EPS, None, ALU.add)
            emit_rsqrt(v, sl(st_r), sl(st_v), sl(st_t1), sl(st_t2))
            sdv.inc(v.scalar_tensor_tensor(sl(st_a), sl(st_m), -1.0, sl(st_r),
                                           ALU.mult, ALU.mult))
        V(_)
        return sdv.n

    def _(g):
        for t in range(NT):
            g.dma_start(out=Rt(t), in_=dx[t * TILE:(t + 1) * TILE, :]).then_inc(ssu.h, 16)
        for dst, src in [(cosr, dcos), (sinr, dsin), (p128t, dp128),
                         (identb, dident), (mask4, dmask), (gammaT, dgam),
                         (betaT, dbet)]:
            g.dma_start(out=dst[:], in_=src).then_inc(ssu.h, 16)
        g.dma_start(out=wkC[:, 0:DK * HD].rearrange("p (k d) -> p k d", d=HD),
                    in_=dwg1.rearrange("(k p) d -> p k d", p=128)).then_inc(ssu.h, 16)
        g.dma_start(out=wkD[0:64, 0:DM], in_=dwg2).then_inc(ssu.h, 16)
        ssu.n += 16 * (NT + 7 + 2)
    G(_)
    setup_m = ssu.n

    def _(s):
        s.wait_ge(ssu.h, setup_m)
        s.copy(wg1b[:], wkC[:, 0:DK * HD])
        sac.inc(s.copy(wg2b[:], wkD[0:64, 0:DM]))
    S(_)
    wgsmall_m = sac.n

    wprep_marks = {}
    def wait_stores(e):
        for ctr in (sgw, sgwd, sga):
            if ctr.n > 0:
                e.wait_ge(ctr.h, ctr.n)

    with ExitStack() as wes:
        WSB = lambda n, s, d: wes.enter_context(nc.sbuf_tensor(n, s, d))
        wldA = WSB("wldA", [128, FFI], F32)
        wldB = WSB("wldB", [128, FFI], F32)
        wqb = WSB("wqb", [128, FFI], BF16)
        evacW = WSB("evacW", [128, 2048], BF16)
        qsW = mk_qscratch(FFI, "W", WSB)
        quantW = mkquant(qsW)

        with nc.psum_tensor("psW", [128, 2048], BF16) as psW:
            for name, dsrc, ddstT, nchunk, F in [
                ("wg", dwgate, s_wgT, NKI, DM),
                ("wu", dwup, s_wuT, NKI, DM),
                ("wd", dwdown, s_wdT, DK, FFI),
            ]:
                sst = sgw if name in ("wg", "wu") else sgwd
                for j in range(nchunk):
                    buf = wldA if (j % 2 == 0) else wldB
                    slot = slw[j % 2]

                    def _(y, j=j, buf=buf, dsrc=dsrc, F=F):
                        y.wait_ge(sdv.h, sdv.n)
                        y.dma_start(out=buf[:, 0:F],
                                    in_=dsrc[j * 128:(j + 1) * 128, 0:F]).then_inc(slot.h, 16)
                    Y(_)
                    slot.n += 16
                    ldm = slot.n

                    def _(v, buf=buf, F=F, ldm=ldm, slot=slot):
                        v.wait_ge(slot.h, ldm)
                        v.wait_ge(spe.h, spe.n)
                        sdv.inc(quantW(v, buf[:, 0:F], F, wqb[:, 0:F]))
                    V(_)
                    qm = sdv.n

                    nsb = F // 128
                    for g0 in range(0, nsb, 8):
                        gn = min(8, nsb - g0)
                        pp = ((g0 // 8) + j) % 2

                        def _(t, g0=g0, gn=gn, pp=pp, qm=qm):
                            t.wait_ge(sdv.h, qm)
                            t.wait_ge(sac.h, sac.n)
                            last = None
                            for s in range(gn):
                                last = t.transpose(
                                    psW[:, pp * 1024 + s * 128: pp * 1024 + (s + 1) * 128],
                                    wqb[:, (g0 + s) * 128:(g0 + s + 1) * 128],
                                    identb[:])
                            spe.inc(last)
                        T(_)
                        tm = spe.n

                        def _(s, gn=gn, pp=pp, tm=tm):
                            s.wait_ge(spe.h, tm)
                            wait_stores(s)
                            sac.inc(s.copy(evacW[:, pp * 1024: pp * 1024 + gn * 128],
                                           psW[:, pp * 1024: pp * 1024 + gn * 128]))
                        S(_)
                        am = sac.n

                        def _(g, j=j, g0=g0, gn=gn, pp=pp, am=am, ddstT=ddstT,
                              sst=sst):
                            g.wait_ge(sac.h, am)
                            for s in range(gn):
                                g.dma_start(
                                    out=ddstT[(g0 + s) * 128:(g0 + s + 1) * 128,
                                              j * 128:(j + 1) * 128],
                                    in_=evacW[:, pp * 1024 + s * 128: pp * 1024 + (s + 1) * 128],
                                ).then_inc(sst.h, 16)
                        G(_)
                        sst.n += 16 * gn
                wprep_marks[name] = sst.n

            for nm in ("q", "k", "v", "o"):
                dsrc, ddst = {"q": (dwq, s_att["q"]), "k": (dwk, s_att["k"]),
                              "v": (dwv, s_att["v"]), "o": (dwo, s_att["o"])}[nm]
                for k in range(DK):
                    buf = wldA if (k % 2 == 0) else wldB
                    pp = k % 2
                    slot = slw[k % 2]

                    def _(y, k=k, buf=buf, dsrc=dsrc):
                        y.wait_ge(sac.h, sac.n)
                        y.dma_start(out=buf[:, 0:DM],
                                    in_=dsrc[k * 128:(k + 1) * 128, :]).then_inc(slot.h, 16)
                    Y(_)
                    slot.n += 16
                    ldm = slot.n

                    def _(s, buf=buf, ldm=ldm, pp=pp, slot=slot):
                        s.wait_ge(slot.h, ldm)
                        wait_stores(s)
                        sac.inc(s.copy(evacW[:, pp * 1024:(pp + 1) * 1024],
                                       buf[:, 0:DM]))
                    S(_)
                    am = sac.n

                    def _(g, k=k, am=am, pp=pp, ddst=ddst):
                        g.wait_ge(sac.h, am)
                        g.dma_start(out=ddst[k * 128:(k + 1) * 128, :],
                                    in_=evacW[:, pp * 1024:(pp + 1) * 1024]).then_inc(sga.h, 16)
                    G(_)
                    sga.n += 16
            wprep_marks["attn"] = sga.n

    def emit_phase_barrier():
        marks = [(c, c.n) for c in dma_ctrs if c.n > 0]
        marks += [(spe, spe.n), (sdv, sdv.n), (sac, sac.n)]

        def mk(own):
            def _(e):
                for ctr, m in marks:
                    if ctr is not own:
                        e.wait_ge(ctr.h, m)
            return _
        V(mk(None))
        S(mk(None))
        T(mk(None))
        Y(mk(None))
        G(mk(None))

    def emit_ffn_call(ci, rwaits):
        with ExitStack() as fes:
            FSB = lambda n, s, d: fes.enter_context(nc.sbuf_tensor(n, s, d))
            xq_fm = FSB(f"xq{ci}", [128, DK * 512], BF16)
            hq_fm = FSB(f"hq{ci}", [128, NKI * 512], BF16)
            wgbuf = FSB(f"wgb{ci}", [128, 2 * 4096], BF16)
            wubuf = FSB(f"wub{ci}", [128, 2 * 4096], BF16)
            wdbuf = FSB(f"wdb{ci}", [128, 2 * DM], BF16)
            qsF = mk_qscratch(1024, f"F{ci}", FSB)
            quantF = mkquant(qsF)

            for iq in range(NQ):
                tiles = [iq * QT + j for j in range(QT)]
                stats_m = emit_ln_stats(tiles, rwaits)

                with nc.psum_tensor(f"pstA{ci}_{iq}", [128, 2048], BF16) as pstA:
                    for j, t in enumerate(tiles):
                        pp = j % 2

                        def _(s, t=t, pp=pp, stats_m=stats_m):
                            s.wait_ge(sdv.h, sdv.n)
                            s.wait_ge(sdv.h, stats_m)
                            sac.inc(s.activation(wkD[:, pp * DM:(pp + 1) * DM],
                                                 Rt(t), AF.Identity,
                                                 bias=st_a[:, t:t + 1],
                                                 scale=st_r[:, t:t + 1]))
                        S(_)
                        am = sac.n

                        def _(v, pp=pp, am=am):
                            v.wait_ge(sac.h, am)
                            v.wait_ge(spe.h, spe.n)
                            sdv.inc(quantF(v, wkD[:, pp * DM:(pp + 1) * DM], DM,
                                           btA[:, pp * DM:(pp + 1) * DM]))
                        V(_)
                        qm = sdv.n

                        def _(t_, pp=pp, qm=qm):
                            t_.wait_ge(sdv.h, qm)
                            t_.wait_ge(sac.h, sac.n)
                            last = None
                            for k in range(DK):
                                last = t_.transpose(
                                    pstA[:, pp * 1024 + k * 128: pp * 1024 + (k + 1) * 128],
                                    btA[:, pp * DM + k * 128: pp * DM + (k + 1) * 128],
                                    identb[:])
                            spe.inc(last)
                        T(_)
                        tm = spe.n

                        def _(s, j=j, pp=pp, tm=tm):
                            s.wait_ge(spe.h, tm)
                            dst = xq_fm[:].rearrange("p (k q) -> p k q", q=512)[
                                :, :, j * 128:(j + 1) * 128]
                            sac.inc(s.copy(dst,
                                           pstA[:, pp * 1024:(pp + 1) * 1024]
                                           .rearrange("p (k q) -> p k q", q=128)))
                        S(_)
                    xq_m = sac.n

                with nc.psum_tensor(f"psg{ci}_{iq}", [128, 1024], F32) as psg, \
                     nc.psum_tensor(f"psu{ci}_{iq}", [128, 1024], F32) as psu, \
                     nc.psum_tensor(f"pstB{ci}_{iq}", [128, 2048], BF16) as pstB:
                    it = 0
                    for mgi, (m0, mw) in enumerate(MGS):
                        wp = mgi % 2

                        slot = sfg[wp]

                        def _(y, m0=m0, mw=mw, wp=wp, ci=ci, slot=slot):
                            y.wait_ge(spe.h, spe.n)
                            if ci == 0:
                                y.wait_ge(sgw.h, wprep_marks["wu"])
                            for k in range(DK):
                                y.dma_start(out=wgbuf[:, wp * 4096 + k * 512: wp * 4096 + k * 512 + mw],
                                            in_=s_wgT[k * 128:(k + 1) * 128, m0:m0 + mw]).then_inc(slot.h, 16)
                                y.dma_start(out=wubuf[:, wp * 4096 + k * 512: wp * 4096 + k * 512 + mw],
                                            in_=s_wuT[k * 128:(k + 1) * 128, m0:m0 + mw]).then_inc(slot.h, 16)
                        Y(_)
                        slot.n += 16 * 2 * DK
                        wlm = slot.n

                        for j in range(QT):
                            pp = it % 2
                            it += 1

                            def _(t_, j=j, pp=pp, mw=mw, wp=wp, wlm=wlm, xq_m=xq_m,
                                  slot=slot):
                                t_.wait_ge(slot.h, wlm)
                                t_.wait_ge(sac.h, xq_m)
                                t_.wait_ge(sdv.h, sdv.n)
                                last = None
                                for k in range(DK):
                                    lhs = xq_fm[:, k * 512 + j * 128: k * 512 + (j + 1) * 128]
                                    t_.matmul(psg[:, pp * 512: pp * 512 + mw], lhs,
                                              wgbuf[:, wp * 4096 + k * 512: wp * 4096 + k * 512 + mw],
                                              start=(k == 0), stop=(k == DK - 1))
                                    last = t_.matmul(psu[:, pp * 512: pp * 512 + mw], lhs,
                                                     wubuf[:, wp * 4096 + k * 512: wp * 4096 + k * 512 + mw],
                                                     start=(k == 0), stop=(k == DK - 1))
                                spe.inc(last)
                            T(_)
                            mmm = spe.n

                            def _(s, pp=pp, mw=mw, mmm=mmm):
                                s.wait_ge(spe.h, mmm)
                                s.wait_ge(sdv.h, sdv.n)
                                s.activation(wkA[:, pp * 512: pp * 512 + mw],
                                             psg[:, pp * 512: pp * 512 + mw], AF.Silu)
                                sac.inc(s.copy(wkB[:, pp * 512: pp * 512 + mw],
                                               psu[:, pp * 512: pp * 512 + mw]))
                            S(_)
                            em = sac.n

                            def _(v, pp=pp, mw=mw, em=em):
                                v.wait_ge(sac.h, em)
                                v.tensor_tensor(wkC[:, 0:mw],
                                                wkA[:, pp * 512: pp * 512 + mw],
                                                wkB[:, pp * 512: pp * 512 + mw], ALU.mult)
                                v.wait_ge(spe.h, spe.n)
                                sdv.inc(quantF(v, wkC[:, 0:mw], mw,
                                               btB[:, pp * DM: pp * DM + mw]))
                            V(_)
                            hqm = sdv.n

                            def _(t_, pp=pp, mw=mw, hqm=hqm):
                                t_.wait_ge(sdv.h, hqm)
                                t_.wait_ge(sac.h, sac.n)
                                last = None
                                for sb in range(mw // 128):
                                    last = t_.transpose(
                                        pstB[:, pp * 1024 + sb * 128: pp * 1024 + (sb + 1) * 128],
                                        btB[:, pp * DM + sb * 128: pp * DM + (sb + 1) * 128],
                                        identb[:])
                                spe.inc(last)
                            T(_)
                            trm = spe.n

                            def _(s, j=j, pp=pp, m0=m0, mw=mw, trm=trm):
                                s.wait_ge(spe.h, trm)
                                kk0 = m0 // 128
                                dst = hq_fm[:].rearrange("p (k q) -> p k q", q=512)[
                                    :, kk0:kk0 + mw // 128, j * 128:(j + 1) * 128]
                                sac.inc(s.copy(dst,
                                               pstB[:, pp * 1024: pp * 1024 + mw]
                                               .rearrange("p (k q) -> p k q", q=128)))
                            S(_)
                    hq_m = sac.n

                with nc.psum_tensor(f"psd{ci}_{iq}", [128, 4096], F32) as psd:
                    for kp in range(NKI):
                        wp = kp % 2

                        slot = sfd[wp]

                        def _(y, kp=kp, wp=wp, ci=ci, slot=slot):
                            y.wait_ge(spe.h, spe.n)
                            if ci == 0 and kp < 2:
                                y.wait_ge(sgwd.h, wprep_marks["wd"])
                            y.dma_start(out=wdbuf[:, wp * DM:(wp + 1) * DM],
                                        in_=s_wdT[kp * 128:(kp + 1) * 128, :]).then_inc(slot.h, 16)
                        Y(_)
                        slot.n += 16
                        wlm = slot.n

                        def _(t_, kp=kp, wp=wp, wlm=wlm, hq_m=hq_m, slot=slot):
                            t_.wait_ge(slot.h, wlm)
                            t_.wait_ge(sac.h, hq_m)
                            if kp == 0:
                                t_.wait_ge(sdv.h, sdv.n)
                            last = None
                            for j in range(QT):
                                lhs = hq_fm[:, kp * 512 + j * 128: kp * 512 + (j + 1) * 128]
                                for hf in range(2):
                                    last = t_.matmul(
                                        psd[:, j * 1024 + hf * 512: j * 1024 + (hf + 1) * 512],
                                        lhs,
                                        wdbuf[:, wp * DM + hf * 512: wp * DM + (hf + 1) * 512],
                                        start=(kp == 0), stop=(kp == NKI - 1))
                            spe.inc(last)
                        T(_)
                    down_m = spe.n

                    def _(v, tiles=tiles, down_m=down_m):
                        v.wait_ge(spe.h, down_m)
                        last = None
                        for j, t in enumerate(tiles):
                            last = v.scalar_tensor_tensor(
                                Rt(t), psd[:, j * 1024:(j + 1) * 1024], 0.5, Rt(t),
                                ALU.mult, ALU.add)
                        sdv.inc(last)
                    V(_)
        return sdv.n

    def emit_attn(rwaits):
        with ExitStack() as aes:
            ASB = lambda n, s, d: aes.enter_context(nc.sbuf_tensor(n, s, d))
            xn_fm = ASB("xn_fm", [128, DK * GTOK], BF16)
            qs_b = ASB("qs_b", [128, 2 * GTOK], BF16)
            qp_b = ASB("qp_b", [128, 2 * GTOK], BF16)
            qh = ASB("qh", [64, H * GTOK], BF16)
            kh = ASB("kh", [64, H * GTOK], BF16)
            v_g = ASB("v_g", [128, GT * DM], BF16)
            xg1g = ASB("xg1g", [64, GTOK], BF16)
            sTm = ASB("sTm", [128, 2 * 512], BF16)
            ktm_sb = ASB("ktm_sb", [128, DM], BF16)
            S32 = ASB("S32", [64, H * HD], F32)
            Sb = ASB("Sb", [64, H * HD], BF16)
            wbufA = ASB("wbufA", [128, 2 * 8192], BF16)
            y_fm = ASB("y_fm", [128, DM], BF16)

            stats_m = emit_ln_stats(list(range(NT)), rwaits)

            def _(v):
                v.memset(S32[:], 0.0)
                sdv.inc(v.memset(Sb[:], 0.0))
            V(_)

            with nc.psum_tensor("psA", [128, 1024], F32) as psA, \
                 nc.psum_tensor("psB", [128, 1024], F32) as psB, \
                 nc.psum_tensor("psC", [128, 512], F32) as psC, \
                 nc.psum_tensor("psD", [128, 1024], BF16) as psD:
                for g in range(NGRP):
                    tiles = [g * GT + j for j in range(GT)]
                    gs = slice(g * GTOK, (g + 1) * GTOK)

                    for j, t in enumerate(tiles):
                        pp = j % 2

                        def _(s, t=t, pp=pp, stats_m=stats_m):
                            s.wait_ge(sdv.h, stats_m)
                            s.wait_ge(spe.h, spe.n)
                            sac.inc(s.activation(btA[:, pp * DM:(pp + 1) * DM],
                                                 Rt(t), AF.Identity,
                                                 bias=st_a[:, t:t + 1],
                                                 scale=st_r[:, t:t + 1]))
                        S(_)
                        am = sac.n

                        def _(t_, pp=pp, am=am):
                            t_.wait_ge(sac.h, am)
                            t_.wait_ge(sac.h, sac.n)
                            t_.wait_ge(sdv.h, sdv.n)
                            last = None
                            for k in range(DK):
                                last = t_.transpose(
                                    psD[:, k * 128:(k + 1) * 128],
                                    btA[:, pp * DM + k * 128: pp * DM + (k + 1) * 128],
                                    identb[:])
                            spe.inc(last)
                        T(_)
                        tm = spe.n

                        def _(s, j=j, tm=tm):
                            s.wait_ge(spe.h, tm)
                            s.wait_ge(spe.h, spe.n)
                            dst = xn_fm[:].rearrange("p (k q) -> p k q", q=GTOK)[
                                :, :, j * 128:(j + 1) * 128]
                            sac.inc(s.copy(dst, psD[:].rearrange("p (k q) -> p k q", q=128)))
                        S(_)
                    xn_m = sac.n

                    wmarks = {}
                    for nm, wp in (("q", 0), ("k", 1)):
                        def _(y, nm=nm, wp=wp, g=g, slot=sat[wp]):
                            y.wait_ge(spe.h, spe.n)
                            if g == 0:
                                y.wait_ge(sga.h, wprep_marks["attn"])
                            for k in range(DK):
                                y.dma_start(out=wbufA[:, wp * 8192 + k * 1024: wp * 8192 + (k + 1) * 1024],
                                            in_=s_att[nm][k * 128:(k + 1) * 128, :]).then_inc(slot.h, 16)
                        Y(_)
                        sat[wp].n += 16 * DK
                        wmarks[nm] = (wp, sat[wp].n)

                    projpe = {}
                    for nm, htile in (("q", qh), ("k", kh)):
                        wp, wlm = wmarks[nm]
                        for m in range(DK):
                            pp = m % 2

                            def _(t_, m=m, pp=pp, wp=wp, wlm=wlm, xn_m=xn_m):
                                t_.wait_ge(sat[wp].h, wlm)
                                t_.wait_ge(sac.h, xn_m)
                                t_.wait_ge(sac.h, sac.n)
                                last = None
                                for k in range(DK):
                                    last = t_.matmul(
                                        psA[:, pp * 512: pp * 512 + GTOK],
                                        wbufA[:, wp * 8192 + k * 1024 + m * 128: wp * 8192 + k * 1024 + (m + 1) * 128],
                                        xn_fm[:, k * GTOK:(k + 1) * GTOK],
                                        start=(k == 0), stop=(k == DK - 1))
                                spe.inc(last)
                            T(_)
                            pm = spe.n

                            def _(s, pp=pp, pm=pm):
                                s.wait_ge(spe.h, pm)
                                s.wait_ge(spe.h, spe.n)
                                sac.inc(s.activation(qs_b[:, pp * GTOK:(pp + 1) * GTOK],
                                                     psA[:, pp * 512: pp * 512 + GTOK],
                                                     AF.Silu))
                            S(_)
                            sm = sac.n

                            def _(t_, pp=pp, sm=sm):
                                t_.wait_ge(sac.h, sm)
                                t_.wait_ge(sac.h, sac.n)
                                spe.inc(t_.matmul(psB[:, pp * 512: pp * 512 + GTOK],
                                                  p128t[:],
                                                  qs_b[:, pp * GTOK:(pp + 1) * GTOK],
                                                  start=True, stop=True))
                            T(_)
                            qpm = spe.n

                            def _(s, pp=pp, qpm=qpm):
                                s.wait_ge(spe.h, qpm)
                                s.wait_ge(sdv.h, sdv.n)
                                sac.inc(s.copy(qp_b[:, pp * GTOK:(pp + 1) * GTOK],
                                               psB[:, pp * 512: pp * 512 + GTOK]))
                            S(_)
                            qem = sac.n

                            def _(v, htile=htile, m=m, pp=pp, qem=qem, gs=gs):
                                v.wait_ge(sac.h, qem)
                                v.wait_ge(spe.h, spe.n)
                                v.tensor_tensor(btB[:, 0:GTOK],
                                                qs_b[:, pp * GTOK:(pp + 1) * GTOK],
                                                cosr[:, gs], ALU.mult)
                                v.tensor_tensor(btB[:, GTOK:2 * GTOK],
                                                qp_b[:, pp * GTOK:(pp + 1) * GTOK],
                                                sinr[:, gs], ALU.mult)
                                h0, h1 = 2 * m, 2 * m + 1
                                v.tensor_tensor(htile[:, h0 * GTOK:(h0 + 1) * GTOK],
                                                btB[0:64, 0:GTOK],
                                                btB[0:64, GTOK:2 * GTOK], ALU.add)
                                sdv.inc(v.tensor_tensor(htile[:, h1 * GTOK:(h1 + 1) * GTOK],
                                                        btB[64:128, 0:GTOK],
                                                        btB[64:128, GTOK:2 * GTOK],
                                                        ALU.add))
                            V(_)
                        projpe[nm] = spe.n
                    qk_m = sdv.n

                    for nm, wp, pmark in (("v", 0, projpe["q"]), ("o", 1, projpe["k"])):
                        def _(y, nm=nm, wp=wp, pmark=pmark, slot=sat[wp]):
                            y.wait_ge(spe.h, spe.n)
                            for k in range(DK):
                                y.dma_start(out=wbufA[:, wp * 8192 + k * 1024: wp * 8192 + (k + 1) * 1024],
                                            in_=s_att[nm][k * 128:(k + 1) * 128, :]).then_inc(slot.h, 16)
                        Y(_)
                        sat[wp].n += 16 * DK
                        wmarks[nm] = (wp, sat[wp].n)

                    wp, wlm = wmarks["v"]
                    for c in range(GT):
                        def _(t_, c=c, wp=wp, wlm=wlm, xn_m=xn_m):
                            t_.wait_ge(sat[wp].h, wlm)
                            t_.wait_ge(sac.h, xn_m)
                            t_.wait_ge(sac.h, sac.n)
                            t_.wait_ge(sdv.h, sdv.n)
                            last = None
                            for k in range(DK):
                                lhs = xn_fm[:, k * GTOK + c * 128: k * GTOK + (c + 1) * 128]
                                for hf in range(2):
                                    last = t_.matmul(
                                        psA[:, hf * 512:(hf + 1) * 512], lhs,
                                        wbufA[:, wp * 8192 + k * 1024 + hf * 512: wp * 8192 + k * 1024 + (hf + 1) * 512],
                                        start=(k == 0), stop=(k == DK - 1))
                            spe.inc(last)
                        T(_)
                        vm = spe.n

                        def _(s, c=c, vm=vm):
                            s.wait_ge(spe.h, vm)
                            s.wait_ge(spe.h, spe.n)
                            sac.inc(s.copy(v_g[:, c * DM:(c + 1) * DM], psA[:]))
                        S(_)
                    v_m = sac.n

                    def _(t_, xn_m=xn_m):
                        t_.wait_ge(sac.h, xn_m)
                        t_.wait_ge(sac.h, wgsmall_m)
                        t_.wait_ge(sac.h, sac.n)
                        t_.wait_ge(sdv.h, sdv.n)
                        last = None
                        for k in range(DK):
                            last = t_.matmul(psC[0:64, 0:GTOK],
                                             wg1b[:, k * HD:(k + 1) * HD],
                                             xn_fm[:, k * GTOK:(k + 1) * GTOK],
                                             start=(k == 0), stop=(k == DK - 1))
                        spe.inc(last)
                    T(_)
                    x1m = spe.n

                    def _(s, x1m=x1m):
                        s.wait_ge(spe.h, x1m)
                        s.wait_ge(spe.h, spe.n)
                        sac.inc(s.copy(xg1g[:, 0:GTOK], psC[0:64, 0:GTOK]))
                    S(_)
                    xg1_m = sac.n

                    for c in range(GT):
                        tglob = g * GT + c

                        out_batches = []
                        for b in range(4):
                            pp = b % 2

                            def _(t_, b=b, c=c, qk_m=qk_m, xg1_m=xg1_m):
                                t_.wait_ge(sdv.h, qk_m)
                                t_.wait_ge(sac.h, xg1_m)
                                t_.wait_ge(sdv.h, sdv.n)
                                last = None
                                for hh in range(4):
                                    hc = b * 4 + hh
                                    last = t_.matmul(
                                        psC[:, hh * 128:(hh + 1) * 128],
                                        kh[:, hc * GTOK + c * 128: hc * GTOK + (c + 1) * 128],
                                        qh[:, hc * GTOK + c * 128: hc * GTOK + (c + 1) * 128],
                                        start=True, stop=True)
                                spe.inc(last)
                            T(_)
                            stm = spe.n

                            def _(v, pp=pp, stm=stm):
                                v.wait_ge(spe.h, stm)
                                v.wait_ge(spe.h, spe.n)
                                sdv.inc(v.tensor_tensor(sTm[:, pp * 512:(pp + 1) * 512],
                                                        psC[:], mask4[:], ALU.mult))
                            V(_)
                            evm = sdv.n

                            def _(t_, b=b, c=c, pp=pp, evm=evm, v_m=v_m):
                                t_.wait_ge(sdv.h, evm)
                                t_.wait_ge(sac.h, v_m)
                                last = None
                                for hh in range(4):
                                    hc = b * 4 + hh
                                    t_.matmul(psA[:, hc * 64:(hc + 1) * 64],
                                              sTm[:, pp * 512 + hh * 128: pp * 512 + (hh + 1) * 128],
                                              v_g[:, c * DM + hc * 64: c * DM + (hc + 1) * 64],
                                              start=True, stop=False)
                                    last = t_.matmul(psA[:, hc * 64:(hc + 1) * 64],
                                                     qh[:, hc * GTOK + c * 128: hc * GTOK + (c + 1) * 128],
                                                     Sb[:, hc * 64:(hc + 1) * 64],
                                                     start=False, stop=True)
                                spe.inc(last)
                            T(_)
                        out_m = spe.n

                        def _(t_, c=c, qk_m=qk_m):
                            t_.wait_ge(sdv.h, qk_m)
                            t_.wait_ge(sdv.h, sdv.n)
                            t_.wait_ge(sac.h, sac.n)
                            last = None
                            for hc in range(H):
                                last = t_.transpose(
                                    psD[:, hc * 64:(hc + 1) * 64],
                                    kh[:, hc * GTOK + c * 128: hc * GTOK + (c + 1) * 128],
                                    identb[0:64, 0:64])
                            spe.inc(last)
                        T(_)
                        ktm_m = spe.n

                        def _(v, ktm_m=ktm_m):
                            v.wait_ge(spe.h, ktm_m)
                            v.wait_ge(spe.h, spe.n)
                            sdv.inc(v.tensor_copy(ktm_sb[:], psD[:]))
                        V(_)
                        ktme_m = sdv.n

                        def _(t_, c=c, ktme_m=ktme_m, v_m=v_m):
                            t_.wait_ge(sdv.h, ktme_m)
                            t_.wait_ge(sac.h, v_m)
                            t_.wait_ge(sac.h, sac.n)
                            t_.wait_ge(sdv.h, sdv.n)
                            last = None
                            for hc in range(H):
                                last = t_.matmul(psB[0:64, hc * 64:(hc + 1) * 64],
                                                 ktm_sb[:, hc * 64:(hc + 1) * 64],
                                                 v_g[:, c * DM + hc * 64: c * DM + (hc + 1) * 64],
                                                 start=True, stop=True)
                            spe.inc(last)
                        T(_)
                        sup_m = spe.n

                        def _(v, sup_m=sup_m, out_m=out_m):
                            v.wait_ge(spe.h, sup_m)
                            v.wait_ge(spe.h, out_m)
                            v.tensor_tensor(S32[:], S32[:], psB[0:64, :], ALU.add)
                            sdv.inc(v.tensor_copy(Sb[:], S32[:]))
                        V(_)
                        supd_m = sdv.n

                        def _(t_, c=c, xg1_m=xg1_m, supd_m=supd_m):
                            t_.wait_ge(sac.h, xg1_m)
                            t_.wait_ge(sdv.h, supd_m)
                            t_.wait_ge(sac.h, sac.n)
                            last = None
                            for hf in range(2):
                                last = t_.matmul(psB[:, hf * 512:(hf + 1) * 512],
                                                 xg1g[:, c * 128:(c + 1) * 128],
                                                 wg2b[:, hf * 512:(hf + 1) * 512],
                                                 start=True, stop=True)
                            spe.inc(last)
                        T(_)
                        gz_m = spe.n

                        def _(s, gz_m=gz_m):
                            s.wait_ge(spe.h, gz_m)
                            s.wait_ge(sdv.h, sdv.n)
                            sac.inc(s.activation(wkC[:, 0:DM], psB[:], AF.Tanh,
                                                 scale=0.5))
                        S(_)
                        tg_m = sac.n

                        def _(v, tg_m=tg_m, out_m=out_m):
                            v.wait_ge(sac.h, tg_m)
                            v.wait_ge(spe.h, out_m)
                            v.tensor_scalar(wkC[:, 0:DM], wkC[:, 0:DM], 0.5, 0.5,
                                            ALU.mult, ALU.add)
                            v.tensor_tensor(wkC[:, DM:2 * DM], wkC[:, 0:DM],
                                            psA[:], ALU.mult)
                            sdv.inc(v.tensor_reduce(st_s1[:, 0:1],
                                                    wkC[:, DM:2 * DM], AX.X, ALU.add))
                        V(_)
                        gd_m = sdv.n

                        def _(s, gd_m=gd_m):
                            s.wait_ge(sdv.h, gd_m)
                            sac.inc(s.activation(wkA[:, 0:512], wkC[:, DM:DM + 512],
                                                 AF.Square, accum_out=st_t1[:, 0:1]))
                            sac.inc(s.activation(wkA[:, 512:1024], wkC[:, DM + 512:2 * DM],
                                                 AF.Square, accum_out=st_t2[:, 0:1]))
                        S(_)
                        sq_m = sac.n

                        def _(v, sq_m=sq_m):
                            v.wait_ge(sac.h, sq_m)
                            sl = lambda tt: tt[:, 0:1]
                            v.tensor_tensor(sl(st_ss), sl(st_t1), sl(st_t2), ALU.add)
                            v.tensor_scalar(sl(st_m), sl(st_s1), RD, None, ALU.mult)
                            v.tensor_scalar(sl(st_v), sl(st_ss), RD, None, ALU.mult)
                            v.tensor_tensor(sl(st_t1), sl(st_m), sl(st_m), ALU.mult)
                            v.tensor_tensor(sl(st_v), sl(st_v), sl(st_t1), ALU.subtract)
                            v.tensor_scalar(sl(st_v), sl(st_v), EPS, None, ALU.add)
                            emit_rsqrt(v, sl(st_r), sl(st_v), sl(st_t1), sl(st_t2))
                            sdv.inc(v.scalar_tensor_tensor(sl(st_a), sl(st_m), -1.0,
                                                           sl(st_r), ALU.mult, ALU.mult))
                        V(_)
                        lns_m = sdv.n

                        def _(s, lns_m=lns_m):
                            s.wait_ge(sdv.h, lns_m)
                            sac.inc(s.activation(wkD[:, 0:DM], wkC[:, DM:2 * DM],
                                                 AF.Identity, bias=st_a[:, 0:1],
                                                 scale=st_r[:, 0:1]))
                        S(_)
                        ya_m = sac.n

                        def _(v, ya_m=ya_m):
                            v.wait_ge(sac.h, ya_m)
                            v.wait_ge(spe.h, spe.n)
                            v.tensor_tensor(wkD[:, DM:2 * DM], wkD[:, 0:DM],
                                            gammaT[:], ALU.mult)
                            sdv.inc(v.tensor_tensor(btA[:, 0:DM], wkD[:, DM:2 * DM],
                                                    betaT[:], ALU.add))
                        V(_)
                        y2_m = sdv.n

                        def _(t_, y2_m=y2_m):
                            t_.wait_ge(sdv.h, y2_m)
                            t_.wait_ge(sdv.h, sdv.n)
                            last = None
                            for k in range(DK):
                                last = t_.transpose(psD[:, k * 128:(k + 1) * 128],
                                                    btA[:, k * 128:(k + 1) * 128],
                                                    identb[:])
                            spe.inc(last)
                        T(_)
                        ytr_m = spe.n

                        def _(v, ytr_m=ytr_m):
                            v.wait_ge(spe.h, ytr_m)
                            v.wait_ge(spe.h, spe.n)
                            sdv.inc(v.tensor_copy(y_fm[:], psD[:]))
                        V(_)
                        yfm_m = sdv.n

                        wp_o, wlm_o = wmarks["o"]

                        def _(t_, yfm_m=yfm_m, wp_o=wp_o, wlm_o=wlm_o):
                            t_.wait_ge(sdv.h, yfm_m)
                            t_.wait_ge(sat[wp_o].h, wlm_o)
                            t_.wait_ge(sdv.h, sdv.n)
                            last = None
                            for k in range(DK):
                                for hf in range(2):
                                    last = t_.matmul(
                                        psA[:, hf * 512:(hf + 1) * 512],
                                        y_fm[:, k * 128:(k + 1) * 128],
                                        wbufA[:, wp_o * 8192 + k * 1024 + hf * 512: wp_o * 8192 + k * 1024 + (hf + 1) * 512],
                                        start=(k == 0), stop=(k == DK - 1))
                            spe.inc(last)
                        T(_)
                        wo_m = spe.n

                        def _(v, tglob=tglob, wo_m=wo_m):
                            v.wait_ge(spe.h, wo_m)
                            sdv.inc(v.scalar_tensor_tensor(Rt(tglob), psA[:], 1.0,
                                                           Rt(tglob), ALU.mult, ALU.add))
                        V(_)
        return sdv.n

    emit_phase_barrier()
    r_mark = [(ssu, setup_m)]
    if "ffn1" in PHASES:
        r_mark = [(sdv, emit_ffn_call(0, r_mark))]
        emit_phase_barrier()
    if "attn" in PHASES:
        r_mark = [(sdv, emit_attn(r_mark))]
        emit_phase_barrier()
    if "ffn2" in PHASES:
        r_mark = [(sdv, emit_ffn_call(1, r_mark))]

    def _(g):
        for ctr, m in r_mark:
            g.wait_ge(ctr.h, m)
        for t in range(NT):
            g.dma_start(out=dout[t * TILE:(t + 1) * TILE, :], in_=Rt(t)).then_inc(sgo.h, 16)
        sgo.n += 16 * NT
        g.wait_ge(sgo.h, sgo.n)
    G(_)

    es.close()
    return nc


def _host_consts():
    BASE = 10000.0
    inv_freq = BASE ** (-(np.arange(32, dtype=np.float64) * 2.0) / 64.0)
    pos = np.arange(NTOK, dtype=np.float64)
    ang = pos[:, None] * inv_freq[None, :]
    cos32 = np.cos(ang).T
    sin32 = np.sin(ang).T
    cosr = np.tile(cos32, (4, 1)).astype(ml_dtypes.bfloat16)
    sinr = np.tile(sin32, (4, 1)).astype(ml_dtypes.bfloat16)
    P64 = np.zeros((64, 64), np.float32)
    for i in range(32):
        P64[i, i + 32] = -1.0
        P64[i + 32, i] = 1.0
    P128 = np.zeros((128, 128), np.float32)
    P128[0:64, 0:64] = P64
    P128[64:128, 64:128] = P64
    p128t = P128.T.astype(ml_dtypes.bfloat16)
    ident = np.eye(128, dtype=np.float32).astype(ml_dtypes.bfloat16)
    mask = np.triu(np.ones((128, 128), np.float32))
    mask4 = np.tile(mask, (1, 4)).astype(ml_dtypes.bfloat16)
    return cosr, sinr, p128t, ident, mask4


def kernel(L0_attn_x, L0_attn_Wq, L0_attn_Wk, L0_attn_Wv, L0_attn_Wo,
           L0_attn_Wg1, L0_attn_Wg2, L0_attn_gamma, L0_attn_beta,
           L0_ffn_w_gate, L0_ffn_w_up, L0_ffn_w_down):
    x = np.ascontiguousarray(np.asarray(L0_attn_x, dtype=np.float32))
    B = x.shape[0]
    cosr, sinr, p128t, ident, mask4 = _host_consts()
    gammaT = np.ascontiguousarray(
        np.broadcast_to(np.asarray(L0_attn_gamma, np.float32), (128, DM)))
    betaT = np.ascontiguousarray(
        np.broadcast_to(np.asarray(L0_attn_beta, np.float32), (128, DM)))
    common = {
        "wq": np.ascontiguousarray(np.asarray(L0_attn_Wq, np.float32)),
        "wk": np.ascontiguousarray(np.asarray(L0_attn_Wk, np.float32)),
        "wv": np.ascontiguousarray(np.asarray(L0_attn_Wv, np.float32)),
        "wo": np.ascontiguousarray(np.asarray(L0_attn_Wo, np.float32)),
        "wg1": np.ascontiguousarray(np.asarray(L0_attn_Wg1, np.float32)),
        "wg2": np.ascontiguousarray(np.asarray(L0_attn_Wg2, np.float32)),
        "d_gammaT": gammaT, "d_betaT": betaT,
        "wgate": np.ascontiguousarray(np.asarray(L0_ffn_w_gate, np.float32)),
        "wup": np.ascontiguousarray(np.asarray(L0_ffn_w_up, np.float32)),
        "wdown": np.ascontiguousarray(np.asarray(L0_ffn_w_down, np.float32)),
        "d_cosr": cosr, "d_sinr": sinr, "d_p128t": p128t, "d_ident": ident,
        "d_mask4": mask4,
    }
    nc = build_program()
    in_maps = [dict(common, x=np.ascontiguousarray(x[b])) for b in range(B)]
    res = run_bass_kernel_spmd(nc, in_maps, core_ids=list(range(B)))
    return np.stack([res.results[b]["out"] for b in range(B)], axis=0)


# version 04
# speedup vs baseline: 1.1933x; 1.1933x over previous
import numpy as np
import ml_dtypes
from contextlib import ExitStack

import concourse.bass as bass
import concourse.mybir as mybir
from concourse.bass_utils import run_bass_kernel_spmd

F32 = mybir.dt.float32
BF16 = mybir.dt.bfloat16
I32 = mybir.dt.int32
AF = mybir.ActivationFunctionType
ALU = mybir.AluOpType
AX = mybir.AxisListType

NTOK, DM, TILE = 2048, 1024, 128
NT = NTOK // TILE
DK = DM // TILE
FFI = 2816
NKI = FFI // TILE
H, HD = 16, 64
MGS = [(0, 512), (512, 512), (1024, 512), (1536, 512), (2048, 512), (2560, 256)]
MGP = [(0, 1024), (1024, 1024), (2048, 768)]
QT = 4
NQ = NT // QT
GT = 2
NGRP = NT // GT
GTOK = GT * TILE

INV6 = float(np.float32(1.0) / np.float32(6.0))
ANDM = float(np.int64(0xFFC00000) - (1 << 32))
RD = float(np.float32(1.0) / np.float32(DM))
EPS = 1e-5

PHASES = ("ffn1", "attn", "ffn2")


def build_program():
    nc = bass.Bass("TRN2", target_bir_lowering=False)

    dx = nc.dram_tensor("x", [NTOK, DM], F32, kind="ExternalInput").ap()
    dwq = nc.dram_tensor("wq", [DM, DM], F32, kind="ExternalInput").ap()
    dwk = nc.dram_tensor("wk", [DM, DM], F32, kind="ExternalInput").ap()
    dwv = nc.dram_tensor("wv", [DM, DM], F32, kind="ExternalInput").ap()
    dwo = nc.dram_tensor("wo", [DM, DM], F32, kind="ExternalInput").ap()
    dwg1 = nc.dram_tensor("wg1", [DM, HD], F32, kind="ExternalInput").ap()
    dwg2 = nc.dram_tensor("wg2", [HD, DM], F32, kind="ExternalInput").ap()
    dgam = nc.dram_tensor("d_gammaT", [128, DM], F32, kind="ExternalInput").ap()
    dbet = nc.dram_tensor("d_betaT", [128, DM], F32, kind="ExternalInput").ap()
    dwgate = nc.dram_tensor("wgate", [FFI, DM], F32, kind="ExternalInput").ap()
    dwup = nc.dram_tensor("wup", [FFI, DM], F32, kind="ExternalInput").ap()
    dwdown = nc.dram_tensor("wdown", [DM, FFI], F32, kind="ExternalInput").ap()
    dcos = nc.dram_tensor("d_cosr", [128, NTOK], BF16, kind="ExternalInput").ap()
    dsin = nc.dram_tensor("d_sinr", [128, NTOK], BF16, kind="ExternalInput").ap()
    dp128 = nc.dram_tensor("d_p128t", [128, 128], BF16, kind="ExternalInput").ap()
    dident = nc.dram_tensor("d_ident", [128, 128], BF16, kind="ExternalInput").ap()
    dmask = nc.dram_tensor("d_mask4", [128, 512], BF16, kind="ExternalInput").ap()
    dout = nc.dram_tensor("out", [NTOK, DM], F32, kind="ExternalOutput").ap()
    s_wgT = nc.dram_tensor("s_wgT", [DM, FFI], BF16).ap()
    s_wuT = nc.dram_tensor("s_wuT", [DM, FFI], BF16).ap()
    s_wdT = nc.dram_tensor("s_wdT", [FFI, DM], BF16).ap()
    s_att = {n: nc.dram_tensor(f"s_w{n}", [DM, DM], BF16).ap()
             for n in ("q", "k", "v", "o")}

    es = ExitStack()
    SB = lambda n, s, d: es.enter_context(nc.sbuf_tensor(n, s, d))

    R = SB("R", [128, NT * DM], F32)
    cosr = SB("cosr", [128, NTOK], BF16)
    sinr = SB("sinr", [128, NTOK], BF16)
    p128t = SB("p128t", [128, 128], BF16)
    identb = SB("identb", [128, 128], BF16)
    mask4 = SB("mask4", [128, 512], BF16)
    gammaT = SB("gammaT", [128, DM], F32)
    betaT = SB("betaT", [128, DM], F32)
    wg1b = SB("wg1b", [128, DK * HD], BF16)
    wg2b = SB("wg2b", [64, DM], BF16)
    st_s1 = SB("st_s1", [128, NT], F32)
    st_ss = SB("st_ss", [128, NT], F32)
    st_m = SB("st_m", [128, NT], F32)
    st_v = SB("st_v", [128, NT], F32)
    st_r = SB("st_r", [128, NT], F32)
    st_a = SB("st_a", [128, NT], F32)
    st_t1 = SB("st_t1", [128, NT], F32)
    st_t2 = SB("st_t2", [128, NT], F32)
    wkA = SB("wkA", [128, 2 * 512], F32)
    wkC = SB("wkC", [128, 2 * DM], F32)
    wkD = SB("wkD", [128, 2 * DM], F32)
    btA = SB("btA", [128, 2 * DM], BF16)
    btB = SB("btB", [128, 2 * DM], BF16)

    class Ctr:
        def __init__(self, name, auto=False):
            self.h = es.enter_context(nc.semaphore(name))
            self.n = 0
            self.auto = auto

        def inc(self, instr, k=1):
            if self.auto:
                return self.n
            instr.then_inc(self.h, k)
            self.n += k
            return self.n

    ssu = Ctr("ssu")
    slw = [Ctr("slw0"), Ctr("slw1")]
    sgw = Ctr("sgw")
    sgwd = Ctr("sgwd")
    sga = Ctr("sga")
    sfg = [Ctr("sfg0"), Ctr("sfg1")]
    sfd = [Ctr("sfd0"), Ctr("sfd1")]
    sat = [Ctr("sat0"), Ctr("sat1")]
    sgo = Ctr("sgo")
    spe = Ctr("spe")
    sdv = Ctr("sdv", auto=True)
    sac = Ctr("sac", auto=True)
    dma_ctrs = [ssu, slw[0], slw[1], sgw, sgwd, sga, sfg[0], sfg[1],
                sfd[0], sfd[1], sat[0], sat[1], sgo]

    class EngProxy:
        _PASS = ("wait_ge", "dma_start")

        def __init__(self, eng, ctr):
            object.__setattr__(self, "_eng", eng)
            object.__setattr__(self, "_ctr", ctr)

        def __getattr__(self, name):
            f = getattr(self._eng, name)
            if name in EngProxy._PASS or not callable(f):
                return f
            eng, ctr = self._eng, self._ctr

            def wrapped(*a, **kw):
                if ctr.n > 0:
                    eng.wait_ge(ctr.h, ctr.n)
                instr = f(*a, **kw)
                instr.then_inc(ctr.h, 1)
                ctr.n += 1
                return instr
            return wrapped

    blk = es.enter_context(nc.Block())
    T, Y, G = blk.tensor, blk.sync, blk.gpsimd
    V = lambda f: blk.vector(lambda v: f(EngProxy(v, sdv)))
    S = lambda f: blk.scalar(lambda s: f(EngProxy(s, sac)))

    wv3 = lambda ap: ap.rearrange("p (b w) -> p b w", w=16)

    def mkquant(qs):
        q_i, q_r, q_c, q_bm, q_sc, q_iv, q_sh = qs

        def bc3(tile, nb):
            return tile[:, 0:nb].unsqueeze(2).broadcast_to([128, nb, 16])

        def emit_quant(v, src_ap, F, dst_ap):
            nb = F // 16
            v.tensor_reduce(q_bm[:, 0:nb], wv3(src_ap), AX.X, ALU.max,
                            apply_absolute_value=True)
            v.tensor_scalar(q_sc[:, 0:nb], q_bm[:, 0:nb], INV6, 1e-12,
                            ALU.mult, ALU.max)
            v.reciprocal(q_iv[:, 0:nb], q_sc[:, 0:nb])
            v.tensor_scalar(q_iv[:, 0:nb], q_iv[:, 0:nb], 2.0, None, ALU.mult)
            v.tensor_scalar(q_sh[:, 0:nb], q_sc[:, 0:nb], 0.5, None, ALU.mult)
            v.tensor_tensor(wv3(src_ap), wv3(src_ap), bc3(q_iv, nb), ALU.mult)
            v.tensor_scalar(q_i[:, 0:F], src_ap.bitcast(I32),
                            float(0x00200000), None, ALU.add)
            v.tensor_scalar(q_i[:, 0:F], q_i[:, 0:F], int(-4194304), None,
                            ALU.bitwise_and)
            v.tensor_scalar(q_c[:, 0:F], q_i[:, 0:F].bitcast(F32), -4.0, 4.0,
                            ALU.max, ALU.min)
            v.tensor_tensor(q_c[:, 0:F], q_i[:, 0:F].bitcast(F32), q_c[:, 0:F],
                            ALU.subtract)
            v.tensor_scalar(src_ap, src_ap, 4.0, -4.0, ALU.min, ALU.max)
            v.tensor_scalar(q_r[:, 0:F], src_ap, 12582912.0, -12582912.0,
                            ALU.add, ALU.add)
            v.tensor_tensor(q_r[:, 0:F], q_r[:, 0:F], q_c[:, 0:F], ALU.add)
            return v.tensor_tensor(wv3(dst_ap), wv3(q_r[:, 0:F]), bc3(q_sh, nb),
                                   ALU.mult)
        return emit_quant

    def mk_qscratch(F, tag, alloc):
        return (alloc(f"q_i{tag}", [128, F], I32),
                alloc(f"q_r{tag}", [128, F], BF16),
                alloc(f"q_c{tag}", [128, F], BF16), alloc(f"q_bm{tag}", [128, F // 16], F32),
                alloc(f"q_sc{tag}", [128, F // 16], F32), alloc(f"q_iv{tag}", [128, F // 16], F32),
                alloc(f"q_sh{tag}", [128, F // 16], F32))

    def Rt(t):
        return R[:, t * DM:(t + 1) * DM]

    def emit_ln_stats(tiles, rwaits):
        c0, c1 = min(tiles), max(tiles) + 1
        assert tiles == list(range(c0, c1))
        sl = lambda tt: tt[:, c0:c1]

        def _(v):
            for ctr, m in rwaits:
                v.wait_ge(ctr.h, m)
            sdv.inc(v.tensor_reduce(
                sl(st_s1),
                R[:, c0 * DM:c1 * DM].rearrange("p (t c) -> p t c", c=DM),
                AX.X, ALU.add))
        V(_)

        def _(s):
            for ctr, m in rwaits:
                s.wait_ge(ctr.h, m)
            s.wait_ge(sdv.h, sdv.n)
            last = None
            for j, t in enumerate(tiles):
                last = s.activation(wkC[:, (j % 2) * DM:(j % 2) * DM + DM],
                                    Rt(t), AF.Square,
                                    accum_out=st_ss[:, t:t + 1])
            sac.inc(last)
        S(_)
        acm = sac.n

        def _(v):
            v.wait_ge(sac.h, acm)
            v.tensor_scalar(sl(st_m), sl(st_s1), RD, None, ALU.mult)
            v.tensor_scalar(sl(st_v), sl(st_ss), RD, None, ALU.mult)
            v.tensor_tensor(sl(st_t1), sl(st_m), sl(st_m), ALU.mult)
            v.tensor_tensor(sl(st_v), sl(st_v), sl(st_t1), ALU.subtract)
            sdv.inc(v.tensor_scalar(sl(st_v), sl(st_v), EPS, None, ALU.add))
        V(_)
        vm = sdv.n

        def _(s):
            s.wait_ge(sdv.h, vm)
            sac.inc(s.activation(sl(st_t2), sl(st_v), AF.Sqrt))
        S(_)
        sqm = sac.n

        def _(v):
            v.wait_ge(sac.h, sqm)
            v.reciprocal(sl(st_r), sl(st_t2))
            sdv.inc(v.scalar_tensor_tensor(sl(st_a), sl(st_m), -1.0, sl(st_r),
                                           ALU.mult, ALU.mult))
        V(_)
        return sdv.n

    def _(g):
        for t in range(NT):
            g.dma_start(out=Rt(t), in_=dx[t * TILE:(t + 1) * TILE, :]).then_inc(ssu.h, 16)
        for dst, src in [(cosr, dcos), (sinr, dsin), (p128t, dp128),
                         (identb, dident), (mask4, dmask), (gammaT, dgam),
                         (betaT, dbet)]:
            g.dma_start(out=dst[:], in_=src).then_inc(ssu.h, 16)
        g.dma_start(out=wkC[:, 0:DK * HD].rearrange("p (k d) -> p k d", d=HD),
                    in_=dwg1.rearrange("(k p) d -> p k d", p=128)).then_inc(ssu.h, 16)
        g.dma_start(out=wkD[0:64, 0:DM], in_=dwg2).then_inc(ssu.h, 16)
        ssu.n += 16 * (NT + 7 + 2)
    G(_)
    setup_m = ssu.n

    def _(s):
        s.wait_ge(ssu.h, setup_m)
        s.copy(wg1b[:], wkC[:, 0:DK * HD])
        sac.inc(s.copy(wg2b[:], wkD[0:64, 0:DM]))
    S(_)
    wgsmall_m = sac.n

    wprep_marks = {}
    def wait_stores(e):
        for ctr in (sgw, sgwd, sga):
            if ctr.n > 0:
                e.wait_ge(ctr.h, ctr.n)

    with ExitStack() as wes:
        WSB = lambda n, s, d: wes.enter_context(nc.sbuf_tensor(n, s, d))
        wldA = WSB("wldA", [128, FFI], F32)
        wldB = WSB("wldB", [128, FFI], F32)
        wqb = WSB("wqb", [128, FFI], BF16)
        evacW = WSB("evacW", [128, 2048], BF16)
        qsW = mk_qscratch(FFI, "W", WSB)
        quantW = mkquant(qsW)

        with nc.psum_tensor("psW", [128, 2048], BF16) as psW:
            for name, dsrc, ddstT, nchunk, F in [
                ("wg", dwgate, s_wgT, NKI, DM),
                ("wu", dwup, s_wuT, NKI, DM),
                ("wd", dwdown, s_wdT, DK, FFI),
            ]:
                sst = sgw if name in ("wg", "wu") else sgwd
                for j in range(nchunk):
                    buf = wldA if (j % 2 == 0) else wldB
                    slot = slw[j % 2]

                    def _(y, j=j, buf=buf, dsrc=dsrc, F=F):
                        y.wait_ge(sdv.h, sdv.n)
                        y.dma_start(out=buf[:, 0:F],
                                    in_=dsrc[j * 128:(j + 1) * 128, 0:F]).then_inc(slot.h, 16)
                    Y(_)
                    slot.n += 16
                    ldm = slot.n

                    def _(v, buf=buf, F=F, ldm=ldm, slot=slot):
                        v.wait_ge(slot.h, ldm)
                        v.wait_ge(spe.h, spe.n)
                        sdv.inc(quantW(v, buf[:, 0:F], F, wqb[:, 0:F]))
                    V(_)
                    qm = sdv.n

                    nsb = F // 128
                    for g0 in range(0, nsb, 8):
                        gn = min(8, nsb - g0)
                        pp = ((g0 // 8) + j) % 2

                        def _(t, g0=g0, gn=gn, pp=pp, qm=qm):
                            t.wait_ge(sdv.h, qm)
                            t.wait_ge(sac.h, sac.n)
                            last = None
                            for s in range(gn):
                                last = t.transpose(
                                    psW[:, pp * 1024 + s * 128: pp * 1024 + (s + 1) * 128],
                                    wqb[:, (g0 + s) * 128:(g0 + s + 1) * 128],
                                    identb[:])
                            spe.inc(last)
                        T(_)
                        tm = spe.n

                        def _(s, gn=gn, pp=pp, tm=tm):
                            s.wait_ge(spe.h, tm)
                            wait_stores(s)
                            sac.inc(s.copy(evacW[:, pp * 1024: pp * 1024 + gn * 128],
                                           psW[:, pp * 1024: pp * 1024 + gn * 128]))
                        S(_)
                        am = sac.n

                        def _(g, j=j, g0=g0, gn=gn, pp=pp, am=am, ddstT=ddstT,
                              sst=sst):
                            g.wait_ge(sac.h, am)
                            for s in range(gn):
                                g.dma_start(
                                    out=ddstT[(g0 + s) * 128:(g0 + s + 1) * 128,
                                              j * 128:(j + 1) * 128],
                                    in_=evacW[:, pp * 1024 + s * 128: pp * 1024 + (s + 1) * 128],
                                ).then_inc(sst.h, 16)
                        G(_)
                        sst.n += 16 * gn
                wprep_marks[name] = sst.n

            for nm in ("q", "k", "v", "o"):
                dsrc, ddst = {"q": (dwq, s_att["q"]), "k": (dwk, s_att["k"]),
                              "v": (dwv, s_att["v"]), "o": (dwo, s_att["o"])}[nm]
                for k in range(DK):
                    buf = wldA if (k % 2 == 0) else wldB
                    pp = k % 2
                    slot = slw[k % 2]

                    def _(y, k=k, buf=buf, dsrc=dsrc):
                        y.wait_ge(sac.h, sac.n)
                        y.dma_start(out=buf[:, 0:DM],
                                    in_=dsrc[k * 128:(k + 1) * 128, :]).then_inc(slot.h, 16)
                    Y(_)
                    slot.n += 16
                    ldm = slot.n

                    def _(s, buf=buf, ldm=ldm, pp=pp, slot=slot):
                        s.wait_ge(slot.h, ldm)
                        wait_stores(s)
                        sac.inc(s.copy(evacW[:, pp * 1024:(pp + 1) * 1024],
                                       buf[:, 0:DM]))
                    S(_)
                    am = sac.n

                    def _(g, k=k, am=am, pp=pp, ddst=ddst):
                        g.wait_ge(sac.h, am)
                        g.dma_start(out=ddst[k * 128:(k + 1) * 128, :],
                                    in_=evacW[:, pp * 1024:(pp + 1) * 1024]).then_inc(sga.h, 16)
                    G(_)
                    sga.n += 16
            wprep_marks["attn"] = sga.n

    def emit_phase_barrier():
        marks = [(c, c.n) for c in dma_ctrs if c.n > 0]
        marks += [(spe, spe.n), (sdv, sdv.n), (sac, sac.n)]

        def mk(own):
            def _(e):
                for ctr, m in marks:
                    if ctr is not own:
                        e.wait_ge(ctr.h, m)
            return _
        V(mk(None))
        S(mk(None))
        T(mk(None))
        Y(mk(None))
        G(mk(None))

    def emit_ffn_call(ci, rwaits):
        with ExitStack() as fes:
            FSB = lambda n, s, d: fes.enter_context(nc.sbuf_tensor(n, s, d))
            xq_fm = FSB(f"xq{ci}", [128, DK * 512], BF16)
            hq_fm = FSB(f"hq{ci}", [128, NKI * 512], BF16)
            wgbuf = FSB(f"wgb{ci}", [128, 2 * 4096], BF16)
            wubuf = FSB(f"wub{ci}", [128, 2 * 4096], BF16)
            wdbuf = FSB(f"wdb{ci}", [128, 2 * DM], BF16)
            hb = FSB(f"hb{ci}", [128, QT * 1024], F32)
            qsF = mk_qscratch(1024, f"F{ci}", FSB)
            quantF = mkquant(qsF)
            resid_m = [0]

            for iq in range(NQ):
                tiles = [iq * QT + j for j in range(QT)]
                stats_m = emit_ln_stats(tiles, rwaits)

                with nc.psum_tensor(f"pstA{ci}_{iq}", [128, 2048], BF16) as pstA:
                    for j, t in enumerate(tiles):
                        pp = j % 2

                        def _(s, t=t, pp=pp, stats_m=stats_m):
                            s.wait_ge(sdv.h, sdv.n)
                            s.wait_ge(sdv.h, stats_m)
                            sac.inc(s.activation(wkD[:, pp * DM:(pp + 1) * DM],
                                                 Rt(t), AF.Identity,
                                                 bias=st_a[:, t:t + 1],
                                                 scale=st_r[:, t:t + 1]))
                        S(_)
                        am = sac.n

                        def _(v, pp=pp, am=am):
                            v.wait_ge(sac.h, am)
                            v.wait_ge(spe.h, spe.n)
                            sdv.inc(quantF(v, wkD[:, pp * DM:(pp + 1) * DM], DM,
                                           btA[:, pp * DM:(pp + 1) * DM]))
                        V(_)
                        qm = sdv.n

                        def _(t_, pp=pp, qm=qm):
                            t_.wait_ge(sdv.h, qm)
                            t_.wait_ge(sac.h, sac.n)
                            last = None
                            for k in range(DK):
                                last = t_.transpose(
                                    pstA[:, pp * 1024 + k * 128: pp * 1024 + (k + 1) * 128],
                                    btA[:, pp * DM + k * 128: pp * DM + (k + 1) * 128],
                                    identb[:])
                            spe.inc(last)
                        T(_)
                        tm = spe.n

                        def _(s, j=j, pp=pp, tm=tm):
                            s.wait_ge(spe.h, tm)
                            dst = xq_fm[:].rearrange("p (k q) -> p k q", q=512)[
                                :, :, j * 128:(j + 1) * 128]
                            sac.inc(s.copy(dst,
                                           pstA[:, pp * 1024:(pp + 1) * 1024]
                                           .rearrange("p (k q) -> p k q", q=128)))
                        S(_)
                    xq_m = sac.n

                with nc.psum_tensor(f"psg{ci}_{iq}", [128, 1024], F32) as psg, \
                     nc.psum_tensor(f"psu{ci}_{iq}", [128, 1024], F32) as psu, \
                     nc.psum_tensor(f"pstB{ci}_{iq}", [128, 2048], BF16) as pstB:
                    it = 0
                    qp = 0
                    bmark = {}
                    for pi, (p0, pw) in enumerate(MGP):
                        for sub in range(2):
                            mgi = pi * 2 + sub
                            m0, mw = MGS[mgi]
                            wp = mgi % 2
                            slot = sfg[wp]

                            def _(y, m0=m0, mw=mw, wp=wp, ci=ci, slot=slot):
                                y.wait_ge(spe.h, spe.n)
                                if ci == 0:
                                    y.wait_ge(sgw.h, wprep_marks["wu"])
                                for k in range(DK):
                                    y.dma_start(out=wgbuf[:, wp * 4096 + k * 512: wp * 4096 + k * 512 + mw],
                                                in_=s_wgT[k * 128:(k + 1) * 128, m0:m0 + mw]).then_inc(slot.h, 16)
                                    y.dma_start(out=wubuf[:, wp * 4096 + k * 512: wp * 4096 + k * 512 + mw],
                                                in_=s_wuT[k * 128:(k + 1) * 128, m0:m0 + mw]).then_inc(slot.h, 16)
                            Y(_)
                            slot.n += 16 * 2 * DK
                            wlm = slot.n

                            for j in range(QT):
                                pp = it % 2

                                def _(t_, j=j, pp=pp, mw=mw, wp=wp, wlm=wlm,
                                      xq_m=xq_m, slot=slot, it=it):
                                    t_.wait_ge(slot.h, wlm)
                                    t_.wait_ge(sac.h, xq_m)
                                    if it - 2 in bmark:
                                        t_.wait_ge(sdv.h, bmark[it - 2])
                                    elif resid_m[0]:
                                        t_.wait_ge(sdv.h, resid_m[0])
                                    last = None
                                    for k in range(DK):
                                        lhs = xq_fm[:, k * 512 + j * 128: k * 512 + (j + 1) * 128]
                                        t_.matmul(psg[:, pp * 512: pp * 512 + mw], lhs,
                                                  wgbuf[:, wp * 4096 + k * 512: wp * 4096 + k * 512 + mw],
                                                  start=(k == 0), stop=(k == DK - 1))
                                        last = t_.matmul(psu[:, pp * 512: pp * 512 + mw], lhs,
                                                         wubuf[:, wp * 4096 + k * 512: wp * 4096 + k * 512 + mw],
                                                         start=(k == 0), stop=(k == DK - 1))
                                    spe.inc(last)
                                T(_)
                                mmm = spe.n

                                def _(s, pp=pp, mw=mw, mmm=mmm, it=it):
                                    s.wait_ge(spe.h, mmm)
                                    if it - 2 in bmark:
                                        s.wait_ge(sdv.h, bmark[it - 2])
                                    sac.inc(s.activation(wkA[:, pp * 512: pp * 512 + mw],
                                                         psg[:, pp * 512: pp * 512 + mw],
                                                         AF.Silu))
                                S(_)
                                em = sac.n

                                def _(v, j=j, pp=pp, m0=m0, mw=mw, p0=p0, em=em):
                                    v.wait_ge(sac.h, em)
                                    off = j * 1024 + (m0 - p0)
                                    sdv.inc(v.tensor_tensor(
                                        hb[:, off:off + mw],
                                        wkA[:, pp * 512: pp * 512 + mw],
                                        psu[:, pp * 512: pp * 512 + mw], ALU.mult))
                                V(_)
                                bmark[it] = sdv.n
                                it += 1

                        for j in range(QT):
                            qpp = qp % 2
                            qp += 1

                            def _(v, j=j, pw=pw, qpp=qpp):
                                v.wait_ge(spe.h, spe.n)
                                sdv.inc(quantF(v, hb[:, j * 1024: j * 1024 + pw],
                                               pw, btB[:, qpp * DM: qpp * DM + pw]))
                            V(_)
                            hqm = sdv.n

                            def _(t_, pw=pw, qpp=qpp, hqm=hqm):
                                t_.wait_ge(sdv.h, hqm)
                                t_.wait_ge(sac.h, sac.n)
                                last = None
                                for sb in range(pw // 128):
                                    last = t_.transpose(
                                        pstB[:, qpp * 1024 + sb * 128: qpp * 1024 + (sb + 1) * 128],
                                        btB[:, qpp * DM + sb * 128: qpp * DM + (sb + 1) * 128],
                                        identb[:])
                                spe.inc(last)
                            T(_)
                            trm = spe.n

                            def _(s, j=j, p0=p0, pw=pw, qpp=qpp, trm=trm):
                                s.wait_ge(spe.h, trm)
                                kk0 = p0 // 128
                                dst = hq_fm[:].rearrange("p (k q) -> p k q", q=512)[
                                    :, kk0:kk0 + pw // 128, j * 128:(j + 1) * 128]
                                sac.inc(s.copy(dst,
                                               pstB[:, qpp * 1024: qpp * 1024 + pw]
                                               .rearrange("p (k q) -> p k q", q=128)))
                            S(_)
                    hq_m = sac.n

                with nc.psum_tensor(f"psd{ci}_{iq}", [128, 4096], F32) as psd:
                    for kp in range(NKI):
                        wp = kp % 2

                        slot = sfd[wp]

                        def _(y, kp=kp, wp=wp, ci=ci, slot=slot):
                            y.wait_ge(spe.h, spe.n)
                            if ci == 0 and kp < 2:
                                y.wait_ge(sgwd.h, wprep_marks["wd"])
                            y.dma_start(out=wdbuf[:, wp * DM:(wp + 1) * DM],
                                        in_=s_wdT[kp * 128:(kp + 1) * 128, :]).then_inc(slot.h, 16)
                        Y(_)
                        slot.n += 16
                        wlm = slot.n

                        def _(t_, kp=kp, wp=wp, wlm=wlm, hq_m=hq_m, slot=slot):
                            t_.wait_ge(slot.h, wlm)
                            t_.wait_ge(sac.h, hq_m)
                            if kp == 0:
                                t_.wait_ge(sdv.h, sdv.n)
                            last = None
                            for j in range(QT):
                                lhs = hq_fm[:, kp * 512 + j * 128: kp * 512 + (j + 1) * 128]
                                for hf in range(2):
                                    last = t_.matmul(
                                        psd[:, j * 1024 + hf * 512: j * 1024 + (hf + 1) * 512],
                                        lhs,
                                        wdbuf[:, wp * DM + hf * 512: wp * DM + (hf + 1) * 512],
                                        start=(kp == 0), stop=(kp == NKI - 1))
                            spe.inc(last)
                        T(_)
                    down_m = spe.n

                    def _(v, tiles=tiles, down_m=down_m):
                        v.wait_ge(spe.h, down_m)
                        t0 = tiles[0]
                        rsl = R[:, t0 * DM:(t0 + QT) * DM]
                        sdv.inc(v.scalar_tensor_tensor(rsl, psd[:], 0.5, rsl,
                                                       ALU.mult, ALU.add))
                    V(_)
                    resid_m[0] = sdv.n
        return sdv.n

    def emit_attn(rwaits):
        with ExitStack() as aes:
            ASB = lambda n, s, d: aes.enter_context(nc.sbuf_tensor(n, s, d))
            xn_fm = ASB("xn_fm", [128, DK * GTOK], BF16)
            qs_b = ASB("qs_b", [128, 2 * GTOK], BF16)
            qp_b = ASB("qp_b", [128, 2 * GTOK], BF16)
            qh = ASB("qh", [64, H * GTOK], BF16)
            kh = ASB("kh", [64, H * GTOK], BF16)
            v_g = ASB("v_g", [128, GT * DM], BF16)
            xg1g = ASB("xg1g", [64, GTOK], BF16)
            sTm = ASB("sTm", [128, 2 * 512], BF16)
            ktm_sb = ASB("ktm_sb", [128, DM], BF16)
            S32 = ASB("S32", [64, H * HD], F32)
            Sb = ASB("Sb", [64, H * HD], BF16)
            wbufA = ASB("wbufA", [128, 2 * 8192], BF16)
            y_fm = ASB("y_fm", [128, DM], BF16)

            stats_m = emit_ln_stats(list(range(NT)), rwaits)

            def _(v):
                v.memset(S32[:], 0.0)
                sdv.inc(v.memset(Sb[:], 0.0))
            V(_)

            with nc.psum_tensor("psA", [128, 1024], F32) as psA, \
                 nc.psum_tensor("psB", [128, 1024], F32) as psB, \
                 nc.psum_tensor("psC", [128, 512], F32) as psC, \
                 nc.psum_tensor("psD", [128, 1024], BF16) as psD:
                for g in range(NGRP):
                    tiles = [g * GT + j for j in range(GT)]
                    gs = slice(g * GTOK, (g + 1) * GTOK)

                    for j, t in enumerate(tiles):
                        pp = j % 2

                        def _(s, t=t, pp=pp, stats_m=stats_m):
                            s.wait_ge(sdv.h, stats_m)
                            s.wait_ge(spe.h, spe.n)
                            sac.inc(s.activation(btA[:, pp * DM:(pp + 1) * DM],
                                                 Rt(t), AF.Identity,
                                                 bias=st_a[:, t:t + 1],
                                                 scale=st_r[:, t:t + 1]))
                        S(_)
                        am = sac.n

                        def _(t_, pp=pp, am=am):
                            t_.wait_ge(sac.h, am)
                            t_.wait_ge(sac.h, sac.n)
                            t_.wait_ge(sdv.h, sdv.n)
                            last = None
                            for k in range(DK):
                                last = t_.transpose(
                                    psD[:, k * 128:(k + 1) * 128],
                                    btA[:, pp * DM + k * 128: pp * DM + (k + 1) * 128],
                                    identb[:])
                            spe.inc(last)
                        T(_)
                        tm = spe.n

                        def _(s, j=j, tm=tm):
                            s.wait_ge(spe.h, tm)
                            s.wait_ge(spe.h, spe.n)
                            dst = xn_fm[:].rearrange("p (k q) -> p k q", q=GTOK)[
                                :, :, j * 128:(j + 1) * 128]
                            sac.inc(s.copy(dst, psD[:].rearrange("p (k q) -> p k q", q=128)))
                        S(_)
                    xn_m = sac.n

                    wmarks = {}
                    for nm, wp in (("q", 0), ("k", 1)):
                        def _(y, nm=nm, wp=wp, g=g, slot=sat[wp]):
                            y.wait_ge(spe.h, spe.n)
                            if g == 0:
                                y.wait_ge(sga.h, wprep_marks["attn"])
                            for k in range(DK):
                                y.dma_start(out=wbufA[:, wp * 8192 + k * 1024: wp * 8192 + (k + 1) * 1024],
                                            in_=s_att[nm][k * 128:(k + 1) * 128, :]).then_inc(slot.h, 16)
                        Y(_)
                        sat[wp].n += 16 * DK
                        wmarks[nm] = (wp, sat[wp].n)

                    projpe = {}
                    for nm, htile in (("q", qh), ("k", kh)):
                        wp, wlm = wmarks[nm]
                        for m in range(DK):
                            pp = m % 2

                            def _(t_, m=m, pp=pp, wp=wp, wlm=wlm, xn_m=xn_m):
                                t_.wait_ge(sat[wp].h, wlm)
                                t_.wait_ge(sac.h, xn_m)
                                t_.wait_ge(sac.h, sac.n)
                                last = None
                                for k in range(DK):
                                    last = t_.matmul(
                                        psA[:, pp * 512: pp * 512 + GTOK],
                                        wbufA[:, wp * 8192 + k * 1024 + m * 128: wp * 8192 + k * 1024 + (m + 1) * 128],
                                        xn_fm[:, k * GTOK:(k + 1) * GTOK],
                                        start=(k == 0), stop=(k == DK - 1))
                                spe.inc(last)
                            T(_)
                            pm = spe.n

                            def _(s, pp=pp, pm=pm):
                                s.wait_ge(spe.h, pm)
                                s.wait_ge(spe.h, spe.n)
                                sac.inc(s.activation(qs_b[:, pp * GTOK:(pp + 1) * GTOK],
                                                     psA[:, pp * 512: pp * 512 + GTOK],
                                                     AF.Silu))
                            S(_)
                            sm = sac.n

                            def _(t_, pp=pp, sm=sm):
                                t_.wait_ge(sac.h, sm)
                                t_.wait_ge(sac.h, sac.n)
                                spe.inc(t_.matmul(psB[:, pp * 512: pp * 512 + GTOK],
                                                  p128t[:],
                                                  qs_b[:, pp * GTOK:(pp + 1) * GTOK],
                                                  start=True, stop=True))
                            T(_)
                            qpm = spe.n

                            def _(s, pp=pp, qpm=qpm):
                                s.wait_ge(spe.h, qpm)
                                s.wait_ge(sdv.h, sdv.n)
                                sac.inc(s.copy(qp_b[:, pp * GTOK:(pp + 1) * GTOK],
                                               psB[:, pp * 512: pp * 512 + GTOK]))
                            S(_)
                            qem = sac.n

                            def _(v, htile=htile, m=m, pp=pp, qem=qem, gs=gs):
                                v.wait_ge(sac.h, qem)
                                v.wait_ge(spe.h, spe.n)
                                v.tensor_tensor(btB[:, 0:GTOK],
                                                qs_b[:, pp * GTOK:(pp + 1) * GTOK],
                                                cosr[:, gs], ALU.mult)
                                v.tensor_tensor(btB[:, GTOK:2 * GTOK],
                                                qp_b[:, pp * GTOK:(pp + 1) * GTOK],
                                                sinr[:, gs], ALU.mult)
                                h0, h1 = 2 * m, 2 * m + 1
                                v.tensor_tensor(htile[:, h0 * GTOK:(h0 + 1) * GTOK],
                                                btB[0:64, 0:GTOK],
                                                btB[0:64, GTOK:2 * GTOK], ALU.add)
                                sdv.inc(v.tensor_tensor(htile[:, h1 * GTOK:(h1 + 1) * GTOK],
                                                        btB[64:128, 0:GTOK],
                                                        btB[64:128, GTOK:2 * GTOK],
                                                        ALU.add))
                            V(_)
                        projpe[nm] = spe.n
                    qk_m = sdv.n

                    for nm, wp, pmark in (("v", 0, projpe["q"]), ("o", 1, projpe["k"])):
                        def _(y, nm=nm, wp=wp, pmark=pmark, slot=sat[wp]):
                            y.wait_ge(spe.h, spe.n)
                            for k in range(DK):
                                y.dma_start(out=wbufA[:, wp * 8192 + k * 1024: wp * 8192 + (k + 1) * 1024],
                                            in_=s_att[nm][k * 128:(k + 1) * 128, :]).then_inc(slot.h, 16)
                        Y(_)
                        sat[wp].n += 16 * DK
                        wmarks[nm] = (wp, sat[wp].n)

                    wp, wlm = wmarks["v"]
                    for c in range(GT):
                        def _(t_, c=c, wp=wp, wlm=wlm, xn_m=xn_m):
                            t_.wait_ge(sat[wp].h, wlm)
                            t_.wait_ge(sac.h, xn_m)
                            t_.wait_ge(sac.h, sac.n)
                            t_.wait_ge(sdv.h, sdv.n)
                            last = None
                            for k in range(DK):
                                lhs = xn_fm[:, k * GTOK + c * 128: k * GTOK + (c + 1) * 128]
                                for hf in range(2):
                                    last = t_.matmul(
                                        psA[:, hf * 512:(hf + 1) * 512], lhs,
                                        wbufA[:, wp * 8192 + k * 1024 + hf * 512: wp * 8192 + k * 1024 + (hf + 1) * 512],
                                        start=(k == 0), stop=(k == DK - 1))
                            spe.inc(last)
                        T(_)
                        vm = spe.n

                        def _(s, c=c, vm=vm):
                            s.wait_ge(spe.h, vm)
                            s.wait_ge(spe.h, spe.n)
                            sac.inc(s.copy(v_g[:, c * DM:(c + 1) * DM], psA[:]))
                        S(_)
                    v_m = sac.n

                    def _(t_, xn_m=xn_m):
                        t_.wait_ge(sac.h, xn_m)
                        t_.wait_ge(sac.h, wgsmall_m)
                        t_.wait_ge(sac.h, sac.n)
                        t_.wait_ge(sdv.h, sdv.n)
                        last = None
                        for k in range(DK):
                            last = t_.matmul(psC[0:64, 0:GTOK],
                                             wg1b[:, k * HD:(k + 1) * HD],
                                             xn_fm[:, k * GTOK:(k + 1) * GTOK],
                                             start=(k == 0), stop=(k == DK - 1))
                        spe.inc(last)
                    T(_)
                    x1m = spe.n

                    def _(s, x1m=x1m):
                        s.wait_ge(spe.h, x1m)
                        s.wait_ge(spe.h, spe.n)
                        sac.inc(s.copy(xg1g[:, 0:GTOK], psC[0:64, 0:GTOK]))
                    S(_)
                    xg1_m = sac.n

                    for c in range(GT):
                        tglob = g * GT + c

                        out_batches = []
                        for b in range(4):
                            pp = b % 2

                            def _(t_, b=b, c=c, qk_m=qk_m, xg1_m=xg1_m):
                                t_.wait_ge(sdv.h, qk_m)
                                t_.wait_ge(sac.h, xg1_m)
                                t_.wait_ge(sdv.h, sdv.n)
                                last = None
                                for hh in range(4):
                                    hc = b * 4 + hh
                                    last = t_.matmul(
                                        psC[:, hh * 128:(hh + 1) * 128],
                                        kh[:, hc * GTOK + c * 128: hc * GTOK + (c + 1) * 128],
                                        qh[:, hc * GTOK + c * 128: hc * GTOK + (c + 1) * 128],
                                        start=True, stop=True)
                                spe.inc(last)
                            T(_)
                            stm = spe.n

                            def _(v, pp=pp, stm=stm):
                                v.wait_ge(spe.h, stm)
                                v.wait_ge(spe.h, spe.n)
                                sdv.inc(v.tensor_tensor(sTm[:, pp * 512:(pp + 1) * 512],
                                                        psC[:], mask4[:], ALU.mult))
                            V(_)
                            evm = sdv.n

                            def _(t_, b=b, c=c, pp=pp, evm=evm, v_m=v_m):
                                t_.wait_ge(sdv.h, evm)
                                t_.wait_ge(sac.h, v_m)
                                last = None
                                for hh in range(4):
                                    hc = b * 4 + hh
                                    t_.matmul(psA[:, hc * 64:(hc + 1) * 64],
                                              sTm[:, pp * 512 + hh * 128: pp * 512 + (hh + 1) * 128],
                                              v_g[:, c * DM + hc * 64: c * DM + (hc + 1) * 64],
                                              start=True, stop=False)
                                    last = t_.matmul(psA[:, hc * 64:(hc + 1) * 64],
                                                     qh[:, hc * GTOK + c * 128: hc * GTOK + (c + 1) * 128],
                                                     Sb[:, hc * 64:(hc + 1) * 64],
                                                     start=False, stop=True)
                                spe.inc(last)
                            T(_)
                        out_m = spe.n

                        def _(t_, c=c, qk_m=qk_m):
                            t_.wait_ge(sdv.h, qk_m)
                            t_.wait_ge(sdv.h, sdv.n)
                            t_.wait_ge(sac.h, sac.n)
                            last = None
                            for hc in range(H):
                                last = t_.transpose(
                                    psD[:, hc * 64:(hc + 1) * 64],
                                    kh[:, hc * GTOK + c * 128: hc * GTOK + (c + 1) * 128],
                                    identb[0:64, 0:64])
                            spe.inc(last)
                        T(_)
                        ktm_m = spe.n

                        def _(v, ktm_m=ktm_m):
                            v.wait_ge(spe.h, ktm_m)
                            v.wait_ge(spe.h, spe.n)
                            sdv.inc(v.tensor_copy(ktm_sb[:], psD[:]))
                        V(_)
                        ktme_m = sdv.n

                        def _(t_, c=c, ktme_m=ktme_m, v_m=v_m):
                            t_.wait_ge(sdv.h, ktme_m)
                            t_.wait_ge(sac.h, v_m)
                            t_.wait_ge(sac.h, sac.n)
                            t_.wait_ge(sdv.h, sdv.n)
                            last = None
                            for hc in range(H):
                                last = t_.matmul(psB[0:64, hc * 64:(hc + 1) * 64],
                                                 ktm_sb[:, hc * 64:(hc + 1) * 64],
                                                 v_g[:, c * DM + hc * 64: c * DM + (hc + 1) * 64],
                                                 start=True, stop=True)
                            spe.inc(last)
                        T(_)
                        sup_m = spe.n

                        def _(v, sup_m=sup_m, out_m=out_m):
                            v.wait_ge(spe.h, sup_m)
                            v.wait_ge(spe.h, out_m)
                            v.tensor_tensor(S32[:], S32[:], psB[0:64, :], ALU.add)
                            sdv.inc(v.tensor_copy(Sb[:], S32[:]))
                        V(_)
                        supd_m = sdv.n

                        def _(t_, c=c, xg1_m=xg1_m, supd_m=supd_m):
                            t_.wait_ge(sac.h, xg1_m)
                            t_.wait_ge(sdv.h, supd_m)
                            t_.wait_ge(sac.h, sac.n)
                            last = None
                            for hf in range(2):
                                last = t_.matmul(psB[:, hf * 512:(hf + 1) * 512],
                                                 xg1g[:, c * 128:(c + 1) * 128],
                                                 wg2b[:, hf * 512:(hf + 1) * 512],
                                                 start=True, stop=True)
                            spe.inc(last)
                        T(_)
                        gz_m = spe.n

                        def _(s, gz_m=gz_m):
                            s.wait_ge(spe.h, gz_m)
                            s.wait_ge(sdv.h, sdv.n)
                            sac.inc(s.activation(wkC[:, 0:DM], psB[:], AF.Tanh,
                                                 scale=0.5))
                        S(_)
                        tg_m = sac.n

                        def _(v, tg_m=tg_m, out_m=out_m):
                            v.wait_ge(sac.h, tg_m)
                            v.wait_ge(spe.h, out_m)
                            v.tensor_scalar(wkC[:, 0:DM], wkC[:, 0:DM], 0.5, 0.5,
                                            ALU.mult, ALU.add)
                            v.tensor_tensor(wkC[:, DM:2 * DM], wkC[:, 0:DM],
                                            psA[:], ALU.mult)
                            sdv.inc(v.tensor_reduce(st_s1[:, 0:1],
                                                    wkC[:, DM:2 * DM], AX.X, ALU.add))
                        V(_)
                        gd_m = sdv.n

                        def _(s, gd_m=gd_m):
                            s.wait_ge(sdv.h, gd_m)
                            sac.inc(s.activation(wkA[:, 0:512], wkC[:, DM:DM + 512],
                                                 AF.Square, accum_out=st_t1[:, 0:1]))
                            sac.inc(s.activation(wkA[:, 512:1024], wkC[:, DM + 512:2 * DM],
                                                 AF.Square, accum_out=st_t2[:, 0:1]))
                        S(_)
                        sq_m = sac.n

                        def _(v, sq_m=sq_m):
                            v.wait_ge(sac.h, sq_m)
                            sl = lambda tt: tt[:, 0:1]
                            v.tensor_tensor(sl(st_ss), sl(st_t1), sl(st_t2), ALU.add)
                            v.tensor_scalar(sl(st_m), sl(st_s1), RD, None, ALU.mult)
                            v.tensor_scalar(sl(st_v), sl(st_ss), RD, None, ALU.mult)
                            v.tensor_tensor(sl(st_t1), sl(st_m), sl(st_m), ALU.mult)
                            v.tensor_tensor(sl(st_v), sl(st_v), sl(st_t1), ALU.subtract)
                            sdv.inc(v.tensor_scalar(sl(st_v), sl(st_v), EPS, None,
                                                    ALU.add))
                        V(_)
                        lnv_m = sdv.n

                        def _(s, lnv_m=lnv_m):
                            s.wait_ge(sdv.h, lnv_m)
                            sac.inc(s.activation(st_t2[:, 0:1], st_v[:, 0:1],
                                                 AF.Sqrt))
                        S(_)
                        sqr_m = sac.n

                        def _(v, sqr_m=sqr_m):
                            v.wait_ge(sac.h, sqr_m)
                            sl = lambda tt: tt[:, 0:1]
                            v.reciprocal(sl(st_r), sl(st_t2))
                            sdv.inc(v.scalar_tensor_tensor(sl(st_a), sl(st_m), -1.0,
                                                           sl(st_r), ALU.mult, ALU.mult))
                        V(_)
                        lns_m = sdv.n

                        def _(s, lns_m=lns_m):
                            s.wait_ge(sdv.h, lns_m)
                            sac.inc(s.activation(wkD[:, 0:DM], wkC[:, DM:2 * DM],
                                                 AF.Identity, bias=st_a[:, 0:1],
                                                 scale=st_r[:, 0:1]))
                        S(_)
                        ya_m = sac.n

                        def _(v, ya_m=ya_m):
                            v.wait_ge(sac.h, ya_m)
                            v.wait_ge(spe.h, spe.n)
                            v.tensor_tensor(wkD[:, DM:2 * DM], wkD[:, 0:DM],
                                            gammaT[:], ALU.mult)
                            sdv.inc(v.tensor_tensor(btA[:, 0:DM], wkD[:, DM:2 * DM],
                                                    betaT[:], ALU.add))
                        V(_)
                        y2_m = sdv.n

                        def _(t_, y2_m=y2_m):
                            t_.wait_ge(sdv.h, y2_m)
                            t_.wait_ge(sdv.h, sdv.n)
                            last = None
                            for k in range(DK):
                                last = t_.transpose(psD[:, k * 128:(k + 1) * 128],
                                                    btA[:, k * 128:(k + 1) * 128],
                                                    identb[:])
                            spe.inc(last)
                        T(_)
                        ytr_m = spe.n

                        def _(v, ytr_m=ytr_m):
                            v.wait_ge(spe.h, ytr_m)
                            v.wait_ge(spe.h, spe.n)
                            sdv.inc(v.tensor_copy(y_fm[:], psD[:]))
                        V(_)
                        yfm_m = sdv.n

                        wp_o, wlm_o = wmarks["o"]

                        def _(t_, yfm_m=yfm_m, wp_o=wp_o, wlm_o=wlm_o):
                            t_.wait_ge(sdv.h, yfm_m)
                            t_.wait_ge(sat[wp_o].h, wlm_o)
                            t_.wait_ge(sdv.h, sdv.n)
                            last = None
                            for k in range(DK):
                                for hf in range(2):
                                    last = t_.matmul(
                                        psA[:, hf * 512:(hf + 1) * 512],
                                        y_fm[:, k * 128:(k + 1) * 128],
                                        wbufA[:, wp_o * 8192 + k * 1024 + hf * 512: wp_o * 8192 + k * 1024 + (hf + 1) * 512],
                                        start=(k == 0), stop=(k == DK - 1))
                            spe.inc(last)
                        T(_)
                        wo_m = spe.n

                        def _(v, tglob=tglob, wo_m=wo_m):
                            v.wait_ge(spe.h, wo_m)
                            sdv.inc(v.scalar_tensor_tensor(Rt(tglob), psA[:], 1.0,
                                                           Rt(tglob), ALU.mult, ALU.add))
                        V(_)
        return sdv.n

    emit_phase_barrier()
    r_mark = [(ssu, setup_m)]
    if "ffn1" in PHASES:
        r_mark = [(sdv, emit_ffn_call(0, r_mark))]
        emit_phase_barrier()
    if "attn" in PHASES:
        r_mark = [(sdv, emit_attn(r_mark))]
        emit_phase_barrier()
    if "ffn2" in PHASES:
        r_mark = [(sdv, emit_ffn_call(1, r_mark))]

    def _(g):
        for ctr, m in r_mark:
            g.wait_ge(ctr.h, m)
        for t in range(NT):
            g.dma_start(out=dout[t * TILE:(t + 1) * TILE, :], in_=Rt(t)).then_inc(sgo.h, 16)
        sgo.n += 16 * NT
        g.wait_ge(sgo.h, sgo.n)
    G(_)

    es.close()
    return nc


def _host_consts():
    BASE = 10000.0
    inv_freq = BASE ** (-(np.arange(32, dtype=np.float64) * 2.0) / 64.0)
    pos = np.arange(NTOK, dtype=np.float64)
    ang = pos[:, None] * inv_freq[None, :]
    cos32 = np.cos(ang).T
    sin32 = np.sin(ang).T
    cosr = np.tile(cos32, (4, 1)).astype(ml_dtypes.bfloat16)
    sinr = np.tile(sin32, (4, 1)).astype(ml_dtypes.bfloat16)
    P64 = np.zeros((64, 64), np.float32)
    for i in range(32):
        P64[i, i + 32] = -1.0
        P64[i + 32, i] = 1.0
    P128 = np.zeros((128, 128), np.float32)
    P128[0:64, 0:64] = P64
    P128[64:128, 64:128] = P64
    p128t = P128.T.astype(ml_dtypes.bfloat16)
    ident = np.eye(128, dtype=np.float32).astype(ml_dtypes.bfloat16)
    mask = np.triu(np.ones((128, 128), np.float32))
    mask4 = np.tile(mask, (1, 4)).astype(ml_dtypes.bfloat16)
    return cosr, sinr, p128t, ident, mask4


def kernel(L0_attn_x, L0_attn_Wq, L0_attn_Wk, L0_attn_Wv, L0_attn_Wo,
           L0_attn_Wg1, L0_attn_Wg2, L0_attn_gamma, L0_attn_beta,
           L0_ffn_w_gate, L0_ffn_w_up, L0_ffn_w_down):
    x = np.ascontiguousarray(np.asarray(L0_attn_x, dtype=np.float32))
    B = x.shape[0]
    cosr, sinr, p128t, ident, mask4 = _host_consts()
    gammaT = np.ascontiguousarray(
        np.broadcast_to(np.asarray(L0_attn_gamma, np.float32), (128, DM)))
    betaT = np.ascontiguousarray(
        np.broadcast_to(np.asarray(L0_attn_beta, np.float32), (128, DM)))
    common = {
        "wq": np.ascontiguousarray(np.asarray(L0_attn_Wq, np.float32)),
        "wk": np.ascontiguousarray(np.asarray(L0_attn_Wk, np.float32)),
        "wv": np.ascontiguousarray(np.asarray(L0_attn_Wv, np.float32)),
        "wo": np.ascontiguousarray(np.asarray(L0_attn_Wo, np.float32)),
        "wg1": np.ascontiguousarray(np.asarray(L0_attn_Wg1, np.float32)),
        "wg2": np.ascontiguousarray(np.asarray(L0_attn_Wg2, np.float32)),
        "d_gammaT": gammaT, "d_betaT": betaT,
        "wgate": np.ascontiguousarray(np.asarray(L0_ffn_w_gate, np.float32)),
        "wup": np.ascontiguousarray(np.asarray(L0_ffn_w_up, np.float32)),
        "wdown": np.ascontiguousarray(np.asarray(L0_ffn_w_down, np.float32)),
        "d_cosr": cosr, "d_sinr": sinr, "d_p128t": p128t, "d_ident": ident,
        "d_mask4": mask4,
    }
    nc = build_program()
    in_maps = [dict(common, x=np.ascontiguousarray(x[b])) for b in range(B)]
    res = run_bass_kernel_spmd(nc, in_maps, core_ids=list(range(B)))
    return np.stack([res.results[b]["out"] for b in range(B)], axis=0)


# version 05
# speedup vs baseline: 1.4605x; 1.2240x over previous
import numpy as np
import ml_dtypes
from contextlib import ExitStack

import concourse.bass as bass
import concourse.mybir as mybir
from concourse.bass_utils import run_bass_kernel_spmd

F32 = mybir.dt.float32
BF16 = mybir.dt.bfloat16
I32 = mybir.dt.int32
AF = mybir.ActivationFunctionType
ALU = mybir.AluOpType
AX = mybir.AxisListType

NTOK, DM, TILE = 2048, 1024, 128
NT = NTOK // TILE
DK = DM // TILE
FFI = 2816
NKI = FFI // TILE
H, HD = 16, 64
MGS = [(0, 512), (512, 512), (1024, 512), (1536, 512), (2048, 512), (2560, 256)]
MGP = [(0, 1024), (1024, 1024), (2048, 768)]
QT = 4
NQ = NT // QT
GT = 2
NGRP = NT // GT
GTOK = GT * TILE

INV6 = float(np.float32(1.0) / np.float32(6.0))
ANDM = float(np.int64(0xFFC00000) - (1 << 32))
RD = float(np.float32(1.0) / np.float32(DM))
EPS = 1e-5

PHASES = ("ffn1", "attn", "ffn2")


def build_program():
    nc = bass.Bass("TRN2", target_bir_lowering=False)

    dx = nc.dram_tensor("x", [NTOK, DM], F32, kind="ExternalInput").ap()
    dwq = nc.dram_tensor("wq", [DM, DM], F32, kind="ExternalInput").ap()
    dwk = nc.dram_tensor("wk", [DM, DM], F32, kind="ExternalInput").ap()
    dwv = nc.dram_tensor("wv", [DM, DM], F32, kind="ExternalInput").ap()
    dwo = nc.dram_tensor("wo", [DM, DM], F32, kind="ExternalInput").ap()
    dwg1 = nc.dram_tensor("wg1", [DM, HD], F32, kind="ExternalInput").ap()
    dwg2 = nc.dram_tensor("wg2", [HD, DM], F32, kind="ExternalInput").ap()
    dgam = nc.dram_tensor("d_gammaT", [128, DM], F32, kind="ExternalInput").ap()
    dbet = nc.dram_tensor("d_betaT", [128, DM], F32, kind="ExternalInput").ap()
    dwgate = nc.dram_tensor("wgate", [FFI, DM], F32, kind="ExternalInput").ap()
    dwup = nc.dram_tensor("wup", [FFI, DM], F32, kind="ExternalInput").ap()
    dwdown = nc.dram_tensor("wdown", [DM, FFI], F32, kind="ExternalInput").ap()
    dcos = nc.dram_tensor("d_cosr", [128, NTOK], BF16, kind="ExternalInput").ap()
    dsin = nc.dram_tensor("d_sinr", [128, NTOK], BF16, kind="ExternalInput").ap()
    dp128 = nc.dram_tensor("d_p128t", [128, 128], BF16, kind="ExternalInput").ap()
    dident = nc.dram_tensor("d_ident", [128, 128], BF16, kind="ExternalInput").ap()
    dmask = nc.dram_tensor("d_mask4", [128, 512], BF16, kind="ExternalInput").ap()
    dout = nc.dram_tensor("out", [NTOK, DM], F32, kind="ExternalOutput").ap()
    s_wgT = nc.dram_tensor("s_wgT", [DM, FFI], BF16).ap()
    s_wuT = nc.dram_tensor("s_wuT", [DM, FFI], BF16).ap()
    s_wdT = nc.dram_tensor("s_wdT", [FFI, DM], BF16).ap()
    s_att = {n: nc.dram_tensor(f"s_w{n}", [DM, DM], BF16).ap()
             for n in ("q", "k", "v", "o")}

    es = ExitStack()
    SB = lambda n, s, d: es.enter_context(nc.sbuf_tensor(n, s, d))

    R = SB("R", [128, NT * DM], F32)
    cosr = SB("cosr", [128, NTOK], BF16)
    sinr = SB("sinr", [128, NTOK], BF16)
    p128t = SB("p128t", [128, 128], BF16)
    identb = SB("identb", [128, 128], BF16)
    mask4 = SB("mask4", [128, 512], BF16)
    gammaT = SB("gammaT", [128, DM], F32)
    betaT = SB("betaT", [128, DM], F32)
    wg1b = SB("wg1b", [128, DK * HD], BF16)
    wg2b = SB("wg2b", [64, DM], BF16)
    st_s1 = SB("st_s1", [128, NT], F32)
    st_ss = SB("st_ss", [128, NT], F32)
    st_m = SB("st_m", [128, NT], F32)
    st_v = SB("st_v", [128, NT], F32)
    st_r = SB("st_r", [128, NT], F32)
    st_a = SB("st_a", [128, NT], F32)
    st_t1 = SB("st_t1", [128, NT], F32)
    st_t2 = SB("st_t2", [128, NT], F32)
    wkA = SB("wkA", [128, 2 * 512], F32)
    wkC = SB("wkC", [128, 2 * DM], F32)
    wkD = SB("wkD", [128, 2 * DM], F32)
    btA = SB("btA", [128, 2 * DM], BF16)
    btB = SB("btB", [128, 2 * DM], BF16)

    class Ctr:
        def __init__(self, name, auto=False):
            self.h = es.enter_context(nc.semaphore(name))
            self.n = 0
            self.auto = auto

        def inc(self, instr, k=1):
            if self.auto:
                return self.n
            instr.then_inc(self.h, k)
            self.n += k
            return self.n

    ssu = Ctr("ssu")
    slw = [Ctr("slw0"), Ctr("slw1")]
    sgw = Ctr("sgw")
    sgwd = Ctr("sgwd")
    sga = Ctr("sga")
    sfg = [Ctr("sfg0"), Ctr("sfg1")]
    sfd = [Ctr("sfd0"), Ctr("sfd1")]
    sat = [Ctr("sat0"), Ctr("sat1")]
    sato = Ctr("sato")
    sgo = Ctr("sgo")
    spe = Ctr("spe")
    sdv = Ctr("sdv", auto=True)
    sac = Ctr("sac", auto=True)
    dma_ctrs = [ssu, slw[0], slw[1], sgw, sgwd, sga, sfg[0], sfg[1],
                sfd[0], sfd[1], sat[0], sat[1], sato, sgo]

    class EngProxy:
        _PASS = ("wait_ge", "dma_start")

        def __init__(self, eng, ctr):
            object.__setattr__(self, "_eng", eng)
            object.__setattr__(self, "_ctr", ctr)

        def __getattr__(self, name):
            f = getattr(self._eng, name)
            if name in EngProxy._PASS or not callable(f):
                return f
            eng, ctr = self._eng, self._ctr

            def wrapped(*a, **kw):
                if ctr.n > 0:
                    eng.wait_ge(ctr.h, ctr.n)
                instr = f(*a, **kw)
                instr.then_inc(ctr.h, 1)
                ctr.n += 1
                return instr
            return wrapped

    blk = es.enter_context(nc.Block())
    T, Y, G = blk.tensor, blk.sync, blk.gpsimd
    V = lambda f: blk.vector(lambda v: f(EngProxy(v, sdv)))
    S = lambda f: blk.scalar(lambda s: f(EngProxy(s, sac)))

    wv3 = lambda ap: ap.rearrange("p (b w) -> p b w", w=16)

    def mkquant(qs):
        q_i, q_r, q_c, q_bm, q_sc, q_iv, q_sh = qs

        def bc3(tile, nb):
            return tile[:, 0:nb].unsqueeze(2).broadcast_to([128, nb, 16])

        def emit_quant(v, src_ap, F, dst_ap):
            nb = F // 16
            v.tensor_reduce(q_bm[:, 0:nb], wv3(src_ap), AX.X, ALU.max,
                            apply_absolute_value=True)
            v.tensor_scalar(q_sc[:, 0:nb], q_bm[:, 0:nb], INV6, 1e-12,
                            ALU.mult, ALU.max)
            v.reciprocal(q_iv[:, 0:nb], q_sc[:, 0:nb])
            v.tensor_scalar(q_iv[:, 0:nb], q_iv[:, 0:nb], 2.0, None, ALU.mult)
            v.tensor_scalar(q_sh[:, 0:nb], q_sc[:, 0:nb], 0.5, None, ALU.mult)
            v.tensor_tensor(wv3(src_ap), wv3(src_ap), bc3(q_iv, nb), ALU.mult)
            v.tensor_scalar(q_i[:, 0:F], src_ap.bitcast(I32),
                            float(0x00200000), None, ALU.add)
            v.tensor_scalar(q_i[:, 0:F], q_i[:, 0:F], int(-4194304), None,
                            ALU.bitwise_and)
            v.tensor_scalar(q_c[:, 0:F], q_i[:, 0:F].bitcast(F32), -4.0, 4.0,
                            ALU.max, ALU.min)
            v.tensor_tensor(q_c[:, 0:F], q_i[:, 0:F].bitcast(F32), q_c[:, 0:F],
                            ALU.subtract)
            v.tensor_scalar(src_ap, src_ap, 4.0, -4.0, ALU.min, ALU.max)
            v.tensor_scalar(q_r[:, 0:F], src_ap, 12582912.0, -12582912.0,
                            ALU.add, ALU.add)
            v.tensor_tensor(q_r[:, 0:F], q_r[:, 0:F], q_c[:, 0:F], ALU.add)
            return v.tensor_tensor(wv3(dst_ap), wv3(q_r[:, 0:F]), bc3(q_sh, nb),
                                   ALU.mult)
        return emit_quant

    def mk_qscratch(F, tag, alloc):
        return (alloc(f"q_i{tag}", [128, F], I32),
                alloc(f"q_r{tag}", [128, F], BF16),
                alloc(f"q_c{tag}", [128, F], BF16), alloc(f"q_bm{tag}", [128, F // 16], F32),
                alloc(f"q_sc{tag}", [128, F // 16], F32), alloc(f"q_iv{tag}", [128, F // 16], F32),
                alloc(f"q_sh{tag}", [128, F // 16], F32))

    def Rt(t):
        return R[:, t * DM:(t + 1) * DM]

    def emit_ln_stats(tiles, rwaits):
        c0, c1 = min(tiles), max(tiles) + 1
        assert tiles == list(range(c0, c1))
        sl = lambda tt: tt[:, c0:c1]

        def _(v):
            for ctr, m in rwaits:
                v.wait_ge(ctr.h, m)
            sdv.inc(v.tensor_reduce(
                sl(st_s1),
                R[:, c0 * DM:c1 * DM].rearrange("p (t c) -> p t c", c=DM),
                AX.X, ALU.add))
        V(_)

        def _(s):
            for ctr, m in rwaits:
                s.wait_ge(ctr.h, m)
            s.wait_ge(sdv.h, sdv.n)
            last = None
            for j, t in enumerate(tiles):
                last = s.activation(wkC[:, (j % 2) * DM:(j % 2) * DM + DM],
                                    Rt(t), AF.Square,
                                    accum_out=st_ss[:, t:t + 1])
            sac.inc(last)
        S(_)
        acm = sac.n

        def _(v):
            v.wait_ge(sac.h, acm)
            v.tensor_scalar(sl(st_m), sl(st_s1), RD, None, ALU.mult)
            v.tensor_scalar(sl(st_v), sl(st_ss), RD, None, ALU.mult)
            v.tensor_tensor(sl(st_t1), sl(st_m), sl(st_m), ALU.mult)
            v.tensor_tensor(sl(st_v), sl(st_v), sl(st_t1), ALU.subtract)
            sdv.inc(v.tensor_scalar(sl(st_v), sl(st_v), EPS, None, ALU.add))
        V(_)
        vm = sdv.n

        def _(s):
            s.wait_ge(sdv.h, vm)
            sac.inc(s.activation(sl(st_t2), sl(st_v), AF.Sqrt))
        S(_)
        sqm = sac.n

        def _(v):
            v.wait_ge(sac.h, sqm)
            v.reciprocal(sl(st_r), sl(st_t2))
            sdv.inc(v.scalar_tensor_tensor(sl(st_a), sl(st_m), -1.0, sl(st_r),
                                           ALU.mult, ALU.mult))
        V(_)
        return sdv.n

    def _(g):
        for t in range(NT):
            g.dma_start(out=Rt(t), in_=dx[t * TILE:(t + 1) * TILE, :]).then_inc(ssu.h, 16)
        for dst, src in [(cosr, dcos), (sinr, dsin), (p128t, dp128),
                         (identb, dident), (mask4, dmask), (gammaT, dgam),
                         (betaT, dbet)]:
            g.dma_start(out=dst[:], in_=src).then_inc(ssu.h, 16)
        g.dma_start(out=wkC[:, 0:DK * HD].rearrange("p (k d) -> p k d", d=HD),
                    in_=dwg1.rearrange("(k p) d -> p k d", p=128)).then_inc(ssu.h, 16)
        g.dma_start(out=wkD[0:64, 0:DM], in_=dwg2).then_inc(ssu.h, 16)
        ssu.n += 16 * (NT + 7 + 2)
    G(_)
    setup_m = ssu.n

    def _(s):
        s.wait_ge(ssu.h, setup_m)
        s.copy(wg1b[:], wkC[:, 0:DK * HD])
        sac.inc(s.copy(wg2b[:], wkD[0:64, 0:DM]))
    S(_)
    wgsmall_m = sac.n

    wprep_marks = {}
    def wait_stores(e):
        for ctr in (sgw, sgwd, sga):
            if ctr.n > 0:
                e.wait_ge(ctr.h, ctr.n)

    with ExitStack() as wes:
        WSB = lambda n, s, d: wes.enter_context(nc.sbuf_tensor(n, s, d))
        wldA = WSB("wldA", [128, FFI], F32)
        wldB = WSB("wldB", [128, FFI], F32)
        wqb = WSB("wqb", [128, FFI], BF16)
        evacW = WSB("evacW", [128, 2048], BF16)
        qsW = mk_qscratch(FFI, "W", WSB)
        quantW = mkquant(qsW)

        with nc.psum_tensor("psW", [128, 2048], BF16) as psW:
            for name, dsrc, ddstT, nchunk, F in [
                ("wg", dwgate, s_wgT, NKI, DM),
                ("wu", dwup, s_wuT, NKI, DM),
                ("wd", dwdown, s_wdT, DK, FFI),
            ]:
                sst = sgw if name in ("wg", "wu") else sgwd
                for j in range(nchunk):
                    buf = wldA if (j % 2 == 0) else wldB
                    slot = slw[j % 2]

                    def _(y, j=j, buf=buf, dsrc=dsrc, F=F):
                        y.wait_ge(sdv.h, sdv.n)
                        y.dma_start(out=buf[:, 0:F],
                                    in_=dsrc[j * 128:(j + 1) * 128, 0:F]).then_inc(slot.h, 16)
                    Y(_)
                    slot.n += 16
                    ldm = slot.n

                    def _(v, buf=buf, F=F, ldm=ldm, slot=slot):
                        v.wait_ge(slot.h, ldm)
                        v.wait_ge(spe.h, spe.n)
                        sdv.inc(quantW(v, buf[:, 0:F], F, wqb[:, 0:F]))
                    V(_)
                    qm = sdv.n

                    nsb = F // 128
                    for g0 in range(0, nsb, 8):
                        gn = min(8, nsb - g0)
                        pp = ((g0 // 8) + j) % 2

                        def _(t, g0=g0, gn=gn, pp=pp, qm=qm):
                            t.wait_ge(sdv.h, qm)
                            t.wait_ge(sac.h, sac.n)
                            last = None
                            for s in range(gn):
                                last = t.transpose(
                                    psW[:, pp * 1024 + s * 128: pp * 1024 + (s + 1) * 128],
                                    wqb[:, (g0 + s) * 128:(g0 + s + 1) * 128],
                                    identb[:])
                            spe.inc(last)
                        T(_)
                        tm = spe.n

                        def _(s, gn=gn, pp=pp, tm=tm):
                            s.wait_ge(spe.h, tm)
                            wait_stores(s)
                            sac.inc(s.copy(evacW[:, pp * 1024: pp * 1024 + gn * 128],
                                           psW[:, pp * 1024: pp * 1024 + gn * 128]))
                        S(_)
                        am = sac.n

                        def _(g, j=j, g0=g0, gn=gn, pp=pp, am=am, ddstT=ddstT,
                              sst=sst):
                            g.wait_ge(sac.h, am)
                            for s in range(gn):
                                g.dma_start(
                                    out=ddstT[(g0 + s) * 128:(g0 + s + 1) * 128,
                                              j * 128:(j + 1) * 128],
                                    in_=evacW[:, pp * 1024 + s * 128: pp * 1024 + (s + 1) * 128],
                                ).then_inc(sst.h, 16)
                        G(_)
                        sst.n += 16 * gn
                wprep_marks[name] = sst.n

            for nm in ("q", "k", "v", "o"):
                dsrc, ddst = {"q": (dwq, s_att["q"]), "k": (dwk, s_att["k"]),
                              "v": (dwv, s_att["v"]), "o": (dwo, s_att["o"])}[nm]
                for k in range(DK):
                    buf = wldA if (k % 2 == 0) else wldB
                    pp = k % 2
                    slot = slw[k % 2]

                    def _(y, k=k, buf=buf, dsrc=dsrc):
                        y.wait_ge(sac.h, sac.n)
                        y.dma_start(out=buf[:, 0:DM],
                                    in_=dsrc[k * 128:(k + 1) * 128, :]).then_inc(slot.h, 16)
                    Y(_)
                    slot.n += 16
                    ldm = slot.n

                    def _(s, buf=buf, ldm=ldm, pp=pp, slot=slot):
                        s.wait_ge(slot.h, ldm)
                        wait_stores(s)
                        sac.inc(s.copy(evacW[:, pp * 1024:(pp + 1) * 1024],
                                       buf[:, 0:DM]))
                    S(_)
                    am = sac.n

                    def _(g, k=k, am=am, pp=pp, ddst=ddst):
                        g.wait_ge(sac.h, am)
                        g.dma_start(out=ddst[k * 128:(k + 1) * 128, :],
                                    in_=evacW[:, pp * 1024:(pp + 1) * 1024]).then_inc(sga.h, 16)
                    G(_)
                    sga.n += 16
            wprep_marks["attn"] = sga.n

    def emit_phase_barrier():
        marks = [(c, c.n) for c in dma_ctrs if c.n > 0]
        marks += [(spe, spe.n), (sdv, sdv.n), (sac, sac.n)]

        def mk(own):
            def _(e):
                for ctr, m in marks:
                    if ctr is not own:
                        e.wait_ge(ctr.h, m)
            return _
        V(mk(None))
        S(mk(None))
        T(mk(None))
        Y(mk(None))
        G(mk(None))

    def emit_ffn_call(ci, rwaits):
        with ExitStack() as fes:
            FSB = lambda n, s, d: fes.enter_context(nc.sbuf_tensor(n, s, d))
            xq_fm = FSB(f"xq{ci}", [128, DK * 512], BF16)
            hq_fm = FSB(f"hq{ci}", [128, NKI * 512], BF16)
            wgbuf = FSB(f"wgb{ci}", [128, 2 * 4096], BF16)
            wubuf = FSB(f"wub{ci}", [128, 2 * 4096], BF16)
            wdbuf = FSB(f"wdb{ci}", [128, 2 * DM], BF16)
            hb = FSB(f"hb{ci}", [128, QT * 1024], F32)
            qsF = mk_qscratch(1024, f"F{ci}", FSB)
            quantF = mkquant(qsF)
            resid_m = [0]
            wkd_m = {}
            btA_m = {}
            wsl_m = {}
            wdl_m = {}

            for iq in range(NQ):
                tiles = [iq * QT + j for j in range(QT)]
                stats_m = emit_ln_stats(tiles, rwaits)

                with nc.psum_tensor(f"pstA{ci}_{iq}", [128, 2048], BF16) as pstA:
                    for j, t in enumerate(tiles):
                        pp = j % 2

                        def _(s, t=t, pp=pp, stats_m=stats_m):
                            if pp in wkd_m:
                                s.wait_ge(sdv.h, wkd_m[pp])
                            s.wait_ge(sdv.h, stats_m)
                            sac.inc(s.activation(wkD[:, pp * DM:(pp + 1) * DM],
                                                 Rt(t), AF.Identity,
                                                 bias=st_a[:, t:t + 1],
                                                 scale=st_r[:, t:t + 1]))
                        S(_)
                        am = sac.n

                        def _(v, pp=pp, am=am):
                            v.wait_ge(sac.h, am)
                            if pp in btA_m:
                                v.wait_ge(spe.h, btA_m[pp])
                            sdv.inc(quantF(v, wkD[:, pp * DM:(pp + 1) * DM], DM,
                                           btA[:, pp * DM:(pp + 1) * DM]))
                        V(_)
                        qm = sdv.n
                        wkd_m[pp] = qm

                        def _(t_, pp=pp, qm=qm):
                            t_.wait_ge(sdv.h, qm)
                            if resid_m[0]:
                                t_.wait_ge(sdv.h, resid_m[0])
                            t_.wait_ge(sac.h, sac.n)
                            last = None
                            for k in range(DK):
                                last = t_.transpose(
                                    pstA[:, pp * 1024 + k * 128: pp * 1024 + (k + 1) * 128],
                                    btA[:, pp * DM + k * 128: pp * DM + (k + 1) * 128],
                                    identb[:])
                            spe.inc(last)
                        T(_)
                        tm = spe.n
                        btA_m[pp] = tm

                        def _(s, j=j, pp=pp, tm=tm):
                            s.wait_ge(spe.h, tm)
                            dst = xq_fm[:].rearrange("p (k q) -> p k q", q=512)[
                                :, :, j * 128:(j + 1) * 128]
                            sac.inc(s.copy(dst,
                                           pstA[:, pp * 1024:(pp + 1) * 1024]
                                           .rearrange("p (k q) -> p k q", q=128)))
                        S(_)
                    xq_m = sac.n

                with nc.psum_tensor(f"psg{ci}_{iq}", [128, 1024], F32) as psg, \
                     nc.psum_tensor(f"psu{ci}_{iq}", [128, 1024], F32) as psu, \
                     nc.psum_tensor(f"pstB{ci}_{iq}", [128, 2048], BF16) as pstB:
                    it = 0
                    qp = 0
                    bmark = {}
                    for pi, (p0, pw) in enumerate(MGP):
                        for sub in range(2):
                            mgi = pi * 2 + sub
                            m0, mw = MGS[mgi]
                            wp = mgi % 2
                            slot = sfg[wp]

                            def _(y, m0=m0, mw=mw, wp=wp, ci=ci, slot=slot):
                                if wp in wsl_m:
                                    y.wait_ge(spe.h, wsl_m[wp])
                                if ci == 0:
                                    y.wait_ge(sgw.h, wprep_marks["wu"])
                                for k in range(DK):
                                    y.dma_start(out=wgbuf[:, wp * 4096 + k * 512: wp * 4096 + k * 512 + mw],
                                                in_=s_wgT[k * 128:(k + 1) * 128, m0:m0 + mw]).then_inc(slot.h, 16)
                                    y.dma_start(out=wubuf[:, wp * 4096 + k * 512: wp * 4096 + k * 512 + mw],
                                                in_=s_wuT[k * 128:(k + 1) * 128, m0:m0 + mw]).then_inc(slot.h, 16)
                            Y(_)
                            slot.n += 16 * 2 * DK
                            wlm = slot.n

                            for j in range(QT):
                                pp = it % 2

                                def _(t_, j=j, pp=pp, mw=mw, wp=wp, wlm=wlm,
                                      xq_m=xq_m, slot=slot, it=it):
                                    t_.wait_ge(slot.h, wlm)
                                    t_.wait_ge(sac.h, xq_m)
                                    if it - 2 in bmark:
                                        t_.wait_ge(sdv.h, bmark[it - 2])
                                    elif resid_m[0]:
                                        t_.wait_ge(sdv.h, resid_m[0])
                                    last = None
                                    for k in range(DK):
                                        lhs = xq_fm[:, k * 512 + j * 128: k * 512 + (j + 1) * 128]
                                        t_.matmul(psg[:, pp * 512: pp * 512 + mw], lhs,
                                                  wgbuf[:, wp * 4096 + k * 512: wp * 4096 + k * 512 + mw],
                                                  start=(k == 0), stop=(k == DK - 1))
                                        last = t_.matmul(psu[:, pp * 512: pp * 512 + mw], lhs,
                                                         wubuf[:, wp * 4096 + k * 512: wp * 4096 + k * 512 + mw],
                                                         start=(k == 0), stop=(k == DK - 1))
                                    spe.inc(last)
                                T(_)
                                mmm = spe.n

                                def _(s, pp=pp, mw=mw, mmm=mmm, it=it):
                                    s.wait_ge(spe.h, mmm)
                                    if it - 2 in bmark:
                                        s.wait_ge(sdv.h, bmark[it - 2])
                                    sac.inc(s.activation(wkA[:, pp * 512: pp * 512 + mw],
                                                         psg[:, pp * 512: pp * 512 + mw],
                                                         AF.Silu))
                                S(_)
                                em = sac.n

                                def _(v, j=j, pp=pp, m0=m0, mw=mw, p0=p0, em=em):
                                    v.wait_ge(sac.h, em)
                                    off = j * 1024 + (m0 - p0)
                                    sdv.inc(v.tensor_tensor(
                                        hb[:, off:off + mw],
                                        wkA[:, pp * 512: pp * 512 + mw],
                                        psu[:, pp * 512: pp * 512 + mw], ALU.mult))
                                V(_)
                                bmark[it] = sdv.n
                                it += 1
                            wsl_m[wp] = spe.n

                        for j in range(QT):
                            qpp = qp % 2
                            qp += 1

                            def _(v, j=j, pw=pw, qpp=qpp):
                                v.wait_ge(spe.h, spe.n)
                                sdv.inc(quantF(v, hb[:, j * 1024: j * 1024 + pw],
                                               pw, btB[:, qpp * DM: qpp * DM + pw]))
                            V(_)
                            hqm = sdv.n

                            def _(t_, pw=pw, qpp=qpp, hqm=hqm):
                                t_.wait_ge(sdv.h, hqm)
                                t_.wait_ge(sac.h, sac.n)
                                last = None
                                for sb in range(pw // 128):
                                    last = t_.transpose(
                                        pstB[:, qpp * 1024 + sb * 128: qpp * 1024 + (sb + 1) * 128],
                                        btB[:, qpp * DM + sb * 128: qpp * DM + (sb + 1) * 128],
                                        identb[:])
                                spe.inc(last)
                            T(_)
                            trm = spe.n

                            def _(s, j=j, p0=p0, pw=pw, qpp=qpp, trm=trm):
                                s.wait_ge(spe.h, trm)
                                kk0 = p0 // 128
                                dst = hq_fm[:].rearrange("p (k q) -> p k q", q=512)[
                                    :, kk0:kk0 + pw // 128, j * 128:(j + 1) * 128]
                                sac.inc(s.copy(dst,
                                               pstB[:, qpp * 1024: qpp * 1024 + pw]
                                               .rearrange("p (k q) -> p k q", q=128)))
                            S(_)
                    hq_m = sac.n

                with nc.psum_tensor(f"psd{ci}_{iq}", [128, 4096], F32) as psd:
                    for kp in range(NKI):
                        wp = kp % 2

                        slot = sfd[wp]

                        def _(y, kp=kp, wp=wp, ci=ci, slot=slot):
                            if wp in wdl_m:
                                y.wait_ge(spe.h, wdl_m[wp])
                            if ci == 0 and kp < 2:
                                y.wait_ge(sgwd.h, wprep_marks["wd"])
                            y.dma_start(out=wdbuf[:, wp * DM:(wp + 1) * DM],
                                        in_=s_wdT[kp * 128:(kp + 1) * 128, :]).then_inc(slot.h, 16)
                        Y(_)
                        slot.n += 16
                        wlm = slot.n

                        def _(t_, kp=kp, wp=wp, wlm=wlm, hq_m=hq_m, slot=slot):
                            t_.wait_ge(slot.h, wlm)
                            t_.wait_ge(sac.h, hq_m)
                            if kp == 0 and resid_m[0]:
                                t_.wait_ge(sdv.h, resid_m[0])
                            last = None
                            for j in range(QT):
                                lhs = hq_fm[:, kp * 512 + j * 128: kp * 512 + (j + 1) * 128]
                                for hf in range(2):
                                    last = t_.matmul(
                                        psd[:, j * 1024 + hf * 512: j * 1024 + (hf + 1) * 512],
                                        lhs,
                                        wdbuf[:, wp * DM + hf * 512: wp * DM + (hf + 1) * 512],
                                        start=(kp == 0), stop=(kp == NKI - 1))
                            spe.inc(last)
                        T(_)
                        wdl_m[wp] = spe.n
                    down_m = spe.n

                    def _(v, tiles=tiles, down_m=down_m):
                        v.wait_ge(spe.h, down_m)
                        t0 = tiles[0]
                        rsl = R[:, t0 * DM:(t0 + QT) * DM]
                        sdv.inc(v.scalar_tensor_tensor(rsl, psd[:], 0.5, rsl,
                                                       ALU.mult, ALU.add))
                    V(_)
                    resid_m[0] = sdv.n
        return sdv.n

    def emit_attn(rwaits):
        with ExitStack() as aes:
            ASB = lambda n, s, d: aes.enter_context(nc.sbuf_tensor(n, s, d))
            xn_fm = ASB("xn_fm", [128, DK * GTOK], BF16)
            qs_b = ASB("qs_b", [128, 2 * GTOK], BF16)
            qp_b = ASB("qp_b", [128, 2 * GTOK], BF16)
            qh = ASB("qh", [64, H * GTOK], BF16)
            kh = ASB("kh", [64, H * GTOK], BF16)
            v_g = ASB("v_g", [128, GT * DM], BF16)
            xg1g = ASB("xg1g", [64, GTOK], BF16)
            sTm = ASB("sTm", [128, 2 * 512], BF16)
            ktm_sb = ASB("ktm_sb", [128, DM], BF16)
            S32 = ASB("S32", [64, H * HD], F32)
            Sb = ASB("Sb", [64, H * HD], BF16)
            wbufA = ASB("wbufA", [128, 3 * 8192], BF16)
            y_fm = ASB("y_fm", [128, DM], BF16)
            gate_sb = ASB("gate_sb", [128, GT * DM], F32)

            stats_m = emit_ln_stats(list(range(NT)), rwaits)

            def _(v):
                v.memset(S32[:], 0.0)
                sdv.inc(v.memset(Sb[:], 0.0))
            V(_)

            with nc.psum_tensor("psA", [128, 2048], F32) as psA, \
                 nc.psum_tensor("psB", [128, 1024], F32) as psB, \
                 nc.psum_tensor("psC", [128, 512], F32) as psC, \
                 nc.psum_tensor("psD", [128, 1024], BF16) as psD:

                rs = {"psC_v": 0, "psD_v": 0, "psD_s": 0, "btA": 0,
                      "psB_s": 0, "psB_v": 0, "wkC_s": 0}
                sTm_rd = {0: 0, 1: 0}
                psA_m = {0: 0, 1: 0}
                gate_rd = [0]
                wsl_rd = {0: 0, 1: 0}
                cs = {}
                gctx = {}

                def sections(g):
                    tiles = [g * GT + j for j in range(GT)]
                    gs = slice(g * GTOK, (g + 1) * GTOK)
                    gp = g % 2
                    ctx = {}

                    for j, t in enumerate(tiles):
                        pp = j % 2

                        def _(s, t=t, pp=pp):
                            s.wait_ge(sdv.h, stats_m)
                            s.wait_ge(spe.h, rs["btA"])
                            sac.inc(s.activation(btA[:, pp * DM:(pp + 1) * DM],
                                                 Rt(t), AF.Identity,
                                                 bias=st_a[:, t:t + 1],
                                                 scale=st_r[:, t:t + 1]))
                        S(_)
                        am = sac.n

                        def _(t_, pp=pp, am=am):
                            t_.wait_ge(sac.h, am)
                            t_.wait_ge(sac.h, rs["psD_s"])
                            t_.wait_ge(sdv.h, rs["psD_v"])
                            last = None
                            for k in range(DK):
                                last = t_.transpose(
                                    psD[:, k * 128:(k + 1) * 128],
                                    btA[:, pp * DM + k * 128: pp * DM + (k + 1) * 128],
                                    identb[:])
                            spe.inc(last)
                        T(_)
                        tm = spe.n
                        rs["btA"] = tm

                        def _(s, j=j, tm=tm):
                            s.wait_ge(spe.h, tm)
                            dst = xn_fm[:].rearrange("p (k q) -> p k q", q=GTOK)[
                                :, :, j * 128:(j + 1) * 128]
                            sac.inc(s.copy(dst, psD[:].rearrange("p (k q) -> p k q", q=128)))
                        S(_)
                        rs["psD_s"] = sac.n
                    xn_m = sac.n
                    ctx["xn_m"] = xn_m

                    wmarks = {}
                    for nm, wp in (("q", 0), ("k", 1)):
                        def _(y, nm=nm, wp=wp, g=g, slot=sat[wp]):
                            y.wait_ge(spe.h, wsl_rd[wp])
                            if g == 0:
                                y.wait_ge(sga.h, wprep_marks["attn"])
                            for k in range(DK):
                                y.dma_start(out=wbufA[:, wp * 8192 + k * 1024: wp * 8192 + (k + 1) * 1024],
                                            in_=s_att[nm][k * 128:(k + 1) * 128, :]).then_inc(slot.h, 16)
                        Y(_)
                        sat[wp].n += 16 * DK
                        wmarks[nm] = (wp, sat[wp].n)

                    projpe = {}
                    for nm, htile in (("q", qh), ("k", kh)):
                        wp, wlm = wmarks[nm]
                        for m in range(DK):
                            pp = m % 2

                            def _(t_, m=m, pp=pp, wp=wp, wlm=wlm, xn_m=xn_m):
                                t_.wait_ge(sat[wp].h, wlm)
                                t_.wait_ge(sac.h, xn_m)
                                t_.wait_ge(sac.h, sac.n)
                                t_.wait_ge(sdv.h, psA_m[0])
                                last = None
                                for k in range(DK):
                                    last = t_.matmul(
                                        psA[:, pp * 512: pp * 512 + GTOK],
                                        wbufA[:, wp * 8192 + k * 1024 + m * 128: wp * 8192 + k * 1024 + (m + 1) * 128],
                                        xn_fm[:, k * GTOK:(k + 1) * GTOK],
                                        start=(k == 0), stop=(k == DK - 1))
                                spe.inc(last)
                            T(_)
                            pm = spe.n

                            def _(s, pp=pp, pm=pm):
                                s.wait_ge(spe.h, pm)
                                s.wait_ge(spe.h, spe.n)
                                sac.inc(s.activation(qs_b[:, pp * GTOK:(pp + 1) * GTOK],
                                                     psA[:, pp * 512: pp * 512 + GTOK],
                                                     AF.Silu))
                            S(_)
                            sm = sac.n

                            def _(t_, pp=pp, sm=sm):
                                t_.wait_ge(sac.h, sm)
                                t_.wait_ge(sac.h, sac.n)
                                t_.wait_ge(sdv.h, rs["psB_v"])
                                spe.inc(t_.matmul(psB[:, pp * 512: pp * 512 + GTOK],
                                                  p128t[:],
                                                  qs_b[:, pp * GTOK:(pp + 1) * GTOK],
                                                  start=True, stop=True))
                            T(_)
                            qpm = spe.n

                            def _(s, pp=pp, qpm=qpm):
                                s.wait_ge(spe.h, qpm)
                                s.wait_ge(sdv.h, sdv.n)
                                sac.inc(s.copy(qp_b[:, pp * GTOK:(pp + 1) * GTOK],
                                               psB[:, pp * 512: pp * 512 + GTOK]))
                            S(_)
                            qem = sac.n
                            rs["psB_s"] = qem

                            def _(v, htile=htile, m=m, pp=pp, qem=qem, gs=gs):
                                v.wait_ge(sac.h, qem)
                                v.wait_ge(spe.h, spe.n)
                                v.tensor_tensor(btB[:, 0:GTOK],
                                                qs_b[:, pp * GTOK:(pp + 1) * GTOK],
                                                cosr[:, gs], ALU.mult)
                                v.tensor_tensor(btB[:, GTOK:2 * GTOK],
                                                qp_b[:, pp * GTOK:(pp + 1) * GTOK],
                                                sinr[:, gs], ALU.mult)
                                h0, h1 = 2 * m, 2 * m + 1
                                v.tensor_tensor(htile[:, h0 * GTOK:(h0 + 1) * GTOK],
                                                btB[0:64, 0:GTOK],
                                                btB[0:64, GTOK:2 * GTOK], ALU.add)
                                sdv.inc(v.tensor_tensor(htile[:, h1 * GTOK:(h1 + 1) * GTOK],
                                                        btB[64:128, 0:GTOK],
                                                        btB[64:128, GTOK:2 * GTOK],
                                                        ALU.add))
                            V(_)
                        projpe[nm] = spe.n
                    ctx["qk_m"] = sdv.n
                    wsl_rd[1] = projpe["k"]

                    def _(y, g=g, slot=sat[0]):
                        y.wait_ge(spe.h, projpe["q"])
                        for k in range(DK):
                            y.dma_start(out=wbufA[:, k * 1024:(k + 1) * 1024],
                                        in_=s_att["v"][k * 128:(k + 1) * 128, :]).then_inc(slot.h, 16)
                    Y(_)
                    sat[0].n += 16 * DK
                    vwlm = sat[0].n

                    for c in range(GT):
                        def _(t_, c=c, vwlm=vwlm, xn_m=xn_m):
                            t_.wait_ge(sat[0].h, vwlm)
                            t_.wait_ge(sac.h, xn_m)
                            t_.wait_ge(sac.h, sac.n)
                            t_.wait_ge(sdv.h, psA_m[0])
                            last = None
                            for k in range(DK):
                                lhs = xn_fm[:, k * GTOK + c * 128: k * GTOK + (c + 1) * 128]
                                for hf in range(2):
                                    last = t_.matmul(
                                        psA[:, hf * 512:(hf + 1) * 512], lhs,
                                        wbufA[:, k * 1024 + hf * 512: k * 1024 + (hf + 1) * 512],
                                        start=(k == 0), stop=(k == DK - 1))
                            spe.inc(last)
                        T(_)
                        vm = spe.n

                        def _(s, c=c, vm=vm):
                            s.wait_ge(spe.h, vm)
                            sac.inc(s.copy(v_g[:, c * DM:(c + 1) * DM], psA[:, 0:1024]))
                        S(_)
                    ctx["v_m"] = sac.n
                    wsl_rd[0] = spe.n

                    def _(t_, xn_m=xn_m):
                        t_.wait_ge(sac.h, xn_m)
                        t_.wait_ge(sac.h, wgsmall_m)
                        t_.wait_ge(sac.h, sac.n)
                        t_.wait_ge(sdv.h, rs["psC_v"])
                        last = None
                        for k in range(DK):
                            last = t_.matmul(psC[0:64, 0:GTOK],
                                             wg1b[:, k * HD:(k + 1) * HD],
                                             xn_fm[:, k * GTOK:(k + 1) * GTOK],
                                             start=(k == 0), stop=(k == DK - 1))
                        spe.inc(last)
                    T(_)
                    x1m = spe.n

                    def _(s, x1m=x1m):
                        s.wait_ge(spe.h, x1m)
                        s.wait_ge(spe.h, spe.n)
                        sac.inc(s.copy(xg1g[:, 0:GTOK], psC[0:64, 0:GTOK]))
                    S(_)
                    ctx["xg1_m"] = sac.n
                    gctx[g] = ctx

                def emit_gate(g):
                    ctx = gctx[g]
                    for cl in range(GT):
                        def _(t_, cl=cl, xg1_m=ctx["xg1_m"]):
                            t_.wait_ge(sac.h, xg1_m)
                            t_.wait_ge(sac.h, wgsmall_m)
                            t_.wait_ge(sdv.h, rs["psB_v"])
                            t_.wait_ge(sac.h, rs["psB_s"])
                            last = None
                            for hf in range(2):
                                last = t_.matmul(psB[:, hf * 512:(hf + 1) * 512],
                                                 xg1g[:, cl * 128:(cl + 1) * 128],
                                                 wg2b[:, hf * 512:(hf + 1) * 512],
                                                 start=True, stop=True)
                            spe.inc(last)
                        T(_)
                        gz_m = spe.n

                        def _(s, cl=cl, gz_m=gz_m):
                            s.wait_ge(spe.h, gz_m)
                            s.wait_ge(sdv.h, gate_rd[0])
                            sac.inc(s.activation(
                                gate_sb[:, cl * DM:(cl + 1) * DM],
                                psB[:], AF.Tanh, scale=0.5))
                        S(_)
                        tg_m = sac.n
                        rs["psB_s"] = tg_m

                        def _(v, cl=cl, tg_m=tg_m):
                            v.wait_ge(sac.h, tg_m)
                            gsl = gate_sb[:, cl * DM:(cl + 1) * DM]
                            sdv.inc(v.tensor_scalar(gsl, gsl, 0.5, 0.5,
                                                    ALU.mult, ALU.add))
                        V(_)

                def o_load(g):
                    def _(y, g=g):
                        if g > 0:
                            y.wait_ge(spe.h, cs[2 * g - 1]["wo_m"])
                        for k in range(DK):
                            y.dma_start(out=wbufA[:, 2 * 8192 + k * 1024: 2 * 8192 + (k + 1) * 1024],
                                        in_=s_att["o"][k * 128:(k + 1) * 128, :]).then_inc(sato.h, 16)
                    Y(_)
                    sato.n += 16 * DK
                    gctx[g]["o_m"] = sato.n

                def mixing(c):
                    g, cl = c // GT, c % GT
                    ctx = gctx[g]
                    pc = c % 2

                    out_m = None
                    for b in range(4):
                        pp = b % 2

                        def _(t_, b=b, cl=cl, ctx=ctx):
                            t_.wait_ge(sdv.h, ctx["qk_m"])
                            t_.wait_ge(sac.h, ctx["xg1_m"])
                            t_.wait_ge(sdv.h, rs["psC_v"])
                            last = None
                            for hh in range(4):
                                hc = b * 4 + hh
                                last = t_.matmul(
                                    psC[:, hh * 128:(hh + 1) * 128],
                                    kh[:, hc * GTOK + cl * 128: hc * GTOK + (cl + 1) * 128],
                                    qh[:, hc * GTOK + cl * 128: hc * GTOK + (cl + 1) * 128],
                                    start=True, stop=True)
                            spe.inc(last)
                        T(_)
                        stm = spe.n

                        def _(v, pp=pp, stm=stm):
                            v.wait_ge(spe.h, stm)
                            v.wait_ge(spe.h, sTm_rd[pp])
                            sdv.inc(v.tensor_tensor(sTm[:, pp * 512:(pp + 1) * 512],
                                                    psC[:], mask4[:], ALU.mult))
                        V(_)
                        evm = sdv.n
                        rs["psC_v"] = evm

                        def _(t_, b=b, cl=cl, pp=pp, pc=pc, evm=evm, ctx=ctx, c=c):
                            t_.wait_ge(sdv.h, evm)
                            t_.wait_ge(sac.h, ctx["v_m"])
                            if c >= 1:
                                t_.wait_ge(sdv.h, cs[c - 1]["supd"])
                            t_.wait_ge(sdv.h, psA_m[pc])
                            last = None
                            for hh in range(4):
                                hc = b * 4 + hh
                                t_.matmul(psA[:, pc * 1024 + hc * 64: pc * 1024 + (hc + 1) * 64],
                                          sTm[:, pp * 512 + hh * 128: pp * 512 + (hh + 1) * 128],
                                          v_g[:, cl * DM + hc * 64: cl * DM + (hc + 1) * 64],
                                          start=True, stop=False)
                                last = t_.matmul(psA[:, pc * 1024 + hc * 64: pc * 1024 + (hc + 1) * 64],
                                                 qh[:, hc * GTOK + cl * 128: hc * GTOK + (cl + 1) * 128],
                                                 Sb[:, hc * 64:(hc + 1) * 64],
                                                 start=False, stop=True)
                            spe.inc(last)
                        T(_)
                        sTm_rd[pp] = spe.n
                    out_m = spe.n

                    def _(t_, cl=cl, ctx=ctx):
                        t_.wait_ge(sdv.h, ctx["qk_m"])
                        t_.wait_ge(sdv.h, rs["psD_v"])
                        t_.wait_ge(sac.h, rs["psD_s"])
                        last = None
                        for hc in range(H):
                            last = t_.transpose(
                                psD[:, hc * 64:(hc + 1) * 64],
                                kh[:, hc * GTOK + cl * 128: hc * GTOK + (cl + 1) * 128],
                                identb[0:64, 0:64])
                        spe.inc(last)
                    T(_)
                    ktm_m = spe.n

                    def _(v, ktm_m=ktm_m):
                        v.wait_ge(spe.h, ktm_m)
                        sdv.inc(v.tensor_copy(ktm_sb[:], psD[:]))
                    V(_)
                    ktme_m = sdv.n
                    rs["psD_v"] = ktme_m

                    def _(t_, cl=cl, ktme_m=ktme_m, ctx=ctx):
                        t_.wait_ge(sdv.h, ktme_m)
                        t_.wait_ge(sac.h, ctx["v_m"])
                        t_.wait_ge(sac.h, rs["psB_s"])
                        t_.wait_ge(sdv.h, rs["psB_v"])
                        last = None
                        for hc in range(H):
                            last = t_.matmul(psB[0:64, hc * 64:(hc + 1) * 64],
                                             ktm_sb[:, hc * 64:(hc + 1) * 64],
                                             v_g[:, cl * DM + hc * 64: cl * DM + (hc + 1) * 64],
                                             start=True, stop=True)
                        spe.inc(last)
                    T(_)
                    sup_m = spe.n

                    def _(v, sup_m=sup_m, out_m=out_m):
                        v.wait_ge(spe.h, sup_m)
                        v.wait_ge(spe.h, out_m)
                        v.tensor_tensor(S32[:], S32[:], psB[0:64, :], ALU.add)
                        sdv.inc(v.tensor_copy(Sb[:], S32[:]))
                    V(_)
                    supd_m = sdv.n
                    rs["psB_v"] = supd_m
                    cs[c] = {"supd": supd_m, "out_m": out_m, "pc": pc,
                             "g": g, "cl": cl}

                def epilogue(c):
                    st = cs[c]
                    g, cl, pc = st["g"], st["cl"], st["pc"]
                    ctx = gctx[g]
                    gsl = gate_sb[:, cl * DM:(cl + 1) * DM]

                    def _(v, st=st, gsl=gsl, pc=pc):
                        v.wait_ge(spe.h, st["out_m"])
                        v.wait_ge(sac.h, rs["wkC_s"])
                        v.tensor_tensor(wkC[:, DM:2 * DM], gsl,
                                        psA[:, pc * 1024:(pc + 1) * 1024], ALU.mult)
                        sdv.inc(v.tensor_reduce(st_s1[:, 0:1],
                                                wkC[:, DM:2 * DM], AX.X, ALU.add))
                    V(_)
                    gd_m = sdv.n
                    gate_rd[0] = gd_m

                    def _(s, gd_m=gd_m):
                        s.wait_ge(sdv.h, gd_m)
                        sac.inc(s.activation(wkA[:, 0:512], wkC[:, DM:DM + 512],
                                             AF.Square, accum_out=st_t1[:, 0:1]))
                        sac.inc(s.activation(wkA[:, 512:1024], wkC[:, DM + 512:2 * DM],
                                             AF.Square, accum_out=st_t2[:, 0:1]))
                    S(_)
                    sq_m = sac.n

                    def _(v, sq_m=sq_m):
                        v.wait_ge(sac.h, sq_m)
                        sl = lambda tt: tt[:, 0:1]
                        v.tensor_tensor(sl(st_ss), sl(st_t1), sl(st_t2), ALU.add)
                        v.tensor_scalar(sl(st_m), sl(st_s1), RD, None, ALU.mult)
                        v.tensor_scalar(sl(st_v), sl(st_ss), RD, None, ALU.mult)
                        v.tensor_tensor(sl(st_t1), sl(st_m), sl(st_m), ALU.mult)
                        v.tensor_tensor(sl(st_v), sl(st_v), sl(st_t1), ALU.subtract)
                        sdv.inc(v.tensor_scalar(sl(st_v), sl(st_v), EPS, None,
                                                ALU.add))
                    V(_)
                    lnv_m = sdv.n

                    def _(s, lnv_m=lnv_m):
                        s.wait_ge(sdv.h, lnv_m)
                        sac.inc(s.activation(st_t2[:, 0:1], st_v[:, 0:1], AF.Sqrt))
                    S(_)
                    sqr_m = sac.n

                    def _(v, sqr_m=sqr_m):
                        v.wait_ge(sac.h, sqr_m)
                        sl = lambda tt: tt[:, 0:1]
                        v.reciprocal(sl(st_r), sl(st_t2))
                        sdv.inc(v.scalar_tensor_tensor(sl(st_a), sl(st_m), -1.0,
                                                       sl(st_r), ALU.mult, ALU.mult))
                    V(_)
                    lns_m = sdv.n

                    def _(s, lns_m=lns_m):
                        s.wait_ge(sdv.h, lns_m)
                        sac.inc(s.activation(wkD[:, 0:DM], wkC[:, DM:2 * DM],
                                             AF.Identity, bias=st_a[:, 0:1],
                                             scale=st_r[:, 0:1]))
                    S(_)
                    ya_m = sac.n
                    rs["wkC_s"] = ya_m

                    def _(v, ya_m=ya_m):
                        v.wait_ge(sac.h, ya_m)
                        v.wait_ge(spe.h, rs["btA"])
                        v.tensor_tensor(wkD[:, DM:2 * DM], wkD[:, 0:DM],
                                        gammaT[:], ALU.mult)
                        sdv.inc(v.tensor_tensor(btA[:, 0:DM], wkD[:, DM:2 * DM],
                                                betaT[:], ALU.add))
                    V(_)
                    y2_m = sdv.n

                    def _(t_, y2_m=y2_m):
                        t_.wait_ge(sdv.h, y2_m)
                        t_.wait_ge(sdv.h, rs["psD_v"])
                        t_.wait_ge(sac.h, rs["psD_s"])
                        last = None
                        for k in range(DK):
                            last = t_.transpose(psD[:, k * 128:(k + 1) * 128],
                                                btA[:, k * 128:(k + 1) * 128],
                                                identb[:])
                        spe.inc(last)
                    T(_)
                    ytr_m = spe.n
                    rs["btA"] = ytr_m

                    def _(v, ytr_m=ytr_m):
                        v.wait_ge(spe.h, ytr_m)
                        sdv.inc(v.tensor_copy(y_fm[:], psD[:]))
                    V(_)
                    yfm_m = sdv.n
                    rs["psD_v"] = yfm_m

                    def _(t_, yfm_m=yfm_m, gd_m=gd_m, pc=pc, o_m=ctx["o_m"]):
                        t_.wait_ge(sdv.h, yfm_m)
                        t_.wait_ge(sato.h, o_m)
                        t_.wait_ge(sdv.h, gd_m)
                        last = None
                        for k in range(DK):
                            for hf in range(2):
                                last = t_.matmul(
                                    psA[:, pc * 1024 + hf * 512: pc * 1024 + (hf + 1) * 512],
                                    y_fm[:, k * 128:(k + 1) * 128],
                                    wbufA[:, 2 * 8192 + k * 1024 + hf * 512: 2 * 8192 + k * 1024 + (hf + 1) * 512],
                                    start=(k == 0), stop=(k == DK - 1))
                        spe.inc(last)
                    T(_)
                    wo_m = spe.n
                    st["wo_m"] = wo_m

                    def _(v, c=c, wo_m=wo_m, pc=pc):
                        v.wait_ge(spe.h, wo_m)
                        sdv.inc(v.scalar_tensor_tensor(Rt(c), psA[:, pc * 1024:(pc + 1) * 1024],
                                                       1.0, Rt(c), ALU.mult, ALU.add))
                    V(_)
                    psA_m[pc] = sdv.n

                pending = None
                for g in range(NGRP):
                    sections(g)
                    mixing(2 * g)
                    if pending is not None:
                        epilogue(pending)
                    o_load(g)
                    emit_gate(g)
                    mixing(2 * g + 1)
                    epilogue(2 * g)
                    pending = 2 * g + 1
                epilogue(pending)
        return sdv.n

    emit_phase_barrier()
    r_mark = [(ssu, setup_m)]
    if "ffn1" in PHASES:
        r_mark = [(sdv, emit_ffn_call(0, r_mark))]
        emit_phase_barrier()
    if "attn" in PHASES:
        r_mark = [(sdv, emit_attn(r_mark))]
        emit_phase_barrier()
    if "ffn2" in PHASES:
        r_mark = [(sdv, emit_ffn_call(1, r_mark))]

    def _(g):
        for ctr, m in r_mark:
            g.wait_ge(ctr.h, m)
        for t in range(NT):
            g.dma_start(out=dout[t * TILE:(t + 1) * TILE, :], in_=Rt(t)).then_inc(sgo.h, 16)
        sgo.n += 16 * NT
        g.wait_ge(sgo.h, sgo.n)
    G(_)

    es.close()
    return nc


def _host_consts():
    BASE = 10000.0
    inv_freq = BASE ** (-(np.arange(32, dtype=np.float64) * 2.0) / 64.0)
    pos = np.arange(NTOK, dtype=np.float64)
    ang = pos[:, None] * inv_freq[None, :]
    cos32 = np.cos(ang).T
    sin32 = np.sin(ang).T
    cosr = np.tile(cos32, (4, 1)).astype(ml_dtypes.bfloat16)
    sinr = np.tile(sin32, (4, 1)).astype(ml_dtypes.bfloat16)
    P64 = np.zeros((64, 64), np.float32)
    for i in range(32):
        P64[i, i + 32] = -1.0
        P64[i + 32, i] = 1.0
    P128 = np.zeros((128, 128), np.float32)
    P128[0:64, 0:64] = P64
    P128[64:128, 64:128] = P64
    p128t = P128.T.astype(ml_dtypes.bfloat16)
    ident = np.eye(128, dtype=np.float32).astype(ml_dtypes.bfloat16)
    mask = np.triu(np.ones((128, 128), np.float32))
    mask4 = np.tile(mask, (1, 4)).astype(ml_dtypes.bfloat16)
    return cosr, sinr, p128t, ident, mask4


def kernel(L0_attn_x, L0_attn_Wq, L0_attn_Wk, L0_attn_Wv, L0_attn_Wo,
           L0_attn_Wg1, L0_attn_Wg2, L0_attn_gamma, L0_attn_beta,
           L0_ffn_w_gate, L0_ffn_w_up, L0_ffn_w_down):
    x = np.ascontiguousarray(np.asarray(L0_attn_x, dtype=np.float32))
    B = x.shape[0]
    cosr, sinr, p128t, ident, mask4 = _host_consts()
    gammaT = np.ascontiguousarray(
        np.broadcast_to(np.asarray(L0_attn_gamma, np.float32), (128, DM)))
    betaT = np.ascontiguousarray(
        np.broadcast_to(np.asarray(L0_attn_beta, np.float32), (128, DM)))
    common = {
        "wq": np.ascontiguousarray(np.asarray(L0_attn_Wq, np.float32)),
        "wk": np.ascontiguousarray(np.asarray(L0_attn_Wk, np.float32)),
        "wv": np.ascontiguousarray(np.asarray(L0_attn_Wv, np.float32)),
        "wo": np.ascontiguousarray(np.asarray(L0_attn_Wo, np.float32)),
        "wg1": np.ascontiguousarray(np.asarray(L0_attn_Wg1, np.float32)),
        "wg2": np.ascontiguousarray(np.asarray(L0_attn_Wg2, np.float32)),
        "d_gammaT": gammaT, "d_betaT": betaT,
        "wgate": np.ascontiguousarray(np.asarray(L0_ffn_w_gate, np.float32)),
        "wup": np.ascontiguousarray(np.asarray(L0_ffn_w_up, np.float32)),
        "wdown": np.ascontiguousarray(np.asarray(L0_ffn_w_down, np.float32)),
        "d_cosr": cosr, "d_sinr": sinr, "d_p128t": p128t, "d_ident": ident,
        "d_mask4": mask4,
    }
    nc = build_program()
    in_maps = [dict(common, x=np.ascontiguousarray(x[b])) for b in range(B)]
    res = run_bass_kernel_spmd(nc, in_maps, core_ids=list(range(B)))
    return np.stack([res.results[b]["out"] for b in range(B)], axis=0)


# version 06
# speedup vs baseline: 1.4702x; 1.0066x over previous
import numpy as np
import ml_dtypes
from contextlib import ExitStack

import concourse.bass as bass
import concourse.mybir as mybir
from concourse.bass_utils import run_bass_kernel_spmd

F32 = mybir.dt.float32
BF16 = mybir.dt.bfloat16
I32 = mybir.dt.int32
AF = mybir.ActivationFunctionType
ALU = mybir.AluOpType
AX = mybir.AxisListType

NTOK, DM, TILE = 2048, 1024, 128
NT = NTOK // TILE
DK = DM // TILE
FFI = 2816
NKI = FFI // TILE
H, HD = 16, 64
MGS = [(0, 512), (512, 512), (1024, 512), (1536, 512), (2048, 512), (2560, 256)]
MGP = [(0, 1024), (1024, 1024), (2048, 768)]
QT = 4
NQ = NT // QT
GT = 2
NGRP = NT // GT
GTOK = GT * TILE

INV6 = float(np.float32(1.0) / np.float32(6.0))
ANDM = float(np.int64(0xFFC00000) - (1 << 32))
RD = float(np.float32(1.0) / np.float32(DM))
EPS = 1e-5

PHASES = ("ffn1", "attn", "ffn2")


def build_program():
    nc = bass.Bass("TRN2", target_bir_lowering=False)

    dx = nc.dram_tensor("x", [NTOK, DM], F32, kind="ExternalInput").ap()
    dwq = nc.dram_tensor("wq", [DM, DM], F32, kind="ExternalInput").ap()
    dwk = nc.dram_tensor("wk", [DM, DM], F32, kind="ExternalInput").ap()
    dwv = nc.dram_tensor("wv", [DM, DM], F32, kind="ExternalInput").ap()
    dwo = nc.dram_tensor("wo", [DM, DM], F32, kind="ExternalInput").ap()
    dwg1 = nc.dram_tensor("wg1", [DM, HD], F32, kind="ExternalInput").ap()
    dwg2 = nc.dram_tensor("wg2", [HD, DM], F32, kind="ExternalInput").ap()
    dgam = nc.dram_tensor("d_gammaT", [128, DM], F32, kind="ExternalInput").ap()
    dbet = nc.dram_tensor("d_betaT", [128, DM], F32, kind="ExternalInput").ap()
    dwgate = nc.dram_tensor("wgate", [FFI, DM], F32, kind="ExternalInput").ap()
    dwup = nc.dram_tensor("wup", [FFI, DM], F32, kind="ExternalInput").ap()
    dwdown = nc.dram_tensor("wdown", [DM, FFI], F32, kind="ExternalInput").ap()
    dcos = nc.dram_tensor("d_cosr", [128, NTOK], BF16, kind="ExternalInput").ap()
    dsin = nc.dram_tensor("d_sinr", [128, NTOK], BF16, kind="ExternalInput").ap()
    dp128 = nc.dram_tensor("d_p128t", [128, 128], BF16, kind="ExternalInput").ap()
    dident = nc.dram_tensor("d_ident", [128, 128], BF16, kind="ExternalInput").ap()
    dmask = nc.dram_tensor("d_mask4", [128, 512], BF16, kind="ExternalInput").ap()
    dout = nc.dram_tensor("out", [NTOK, DM], F32, kind="ExternalOutput").ap()
    s_wgT = nc.dram_tensor("s_wgT", [DM, FFI], BF16).ap()
    s_wuT = nc.dram_tensor("s_wuT", [DM, FFI], BF16).ap()
    s_wdT = nc.dram_tensor("s_wdT", [FFI, DM], BF16).ap()
    s_att = {n: nc.dram_tensor(f"s_w{n}", [DM, DM], BF16).ap()
             for n in ("q", "k", "v", "o")}

    es = ExitStack()
    SB = lambda n, s, d: es.enter_context(nc.sbuf_tensor(n, s, d))

    R = SB("R", [128, NT * DM], F32)
    cosr = SB("cosr", [128, NTOK], BF16)
    sinr = SB("sinr", [128, NTOK], BF16)
    p128t = SB("p128t", [128, 128], BF16)
    identb = SB("identb", [128, 128], BF16)
    mask4 = SB("mask4", [128, 512], BF16)
    gammaT = SB("gammaT", [128, DM], F32)
    betaT = SB("betaT", [128, DM], F32)
    wg1b = SB("wg1b", [128, DK * HD], BF16)
    wg2b = SB("wg2b", [64, DM], BF16)
    st_s1 = SB("st_s1", [128, NT], F32)
    st_ss = SB("st_ss", [128, NT], F32)
    st_m = SB("st_m", [128, NT], F32)
    st_v = SB("st_v", [128, NT], F32)
    st_r = SB("st_r", [128, NT], F32)
    st_a = SB("st_a", [128, NT], F32)
    st_t1 = SB("st_t1", [128, NT], F32)
    st_t2 = SB("st_t2", [128, NT], F32)
    wkA = SB("wkA", [128, 2 * 512], F32)
    wkC = SB("wkC", [128, 2 * DM], F32)
    wkD = SB("wkD", [128, 2 * DM], F32)
    btA = SB("btA", [128, 2 * DM], BF16)
    btB = SB("btB", [128, 2 * DM], BF16)

    class Ctr:
        def __init__(self, name, auto=False):
            self.h = es.enter_context(nc.semaphore(name))
            self.n = 0
            self.auto = auto

        def inc(self, instr, k=1):
            if self.auto:
                return self.n
            instr.then_inc(self.h, k)
            self.n += k
            return self.n

    ssu = Ctr("ssu")
    slw = [Ctr("slw0"), Ctr("slw1")]
    sgw = Ctr("sgw")
    sgwd = Ctr("sgwd")
    sga = Ctr("sga")
    sfg = [Ctr("sfg0"), Ctr("sfg1")]
    sfd = [Ctr("sfd0"), Ctr("sfd1")]
    sat = [Ctr("sat0"), Ctr("sat1")]
    sato = Ctr("sato")
    sgo = Ctr("sgo")
    spe = Ctr("spe")
    sdv = Ctr("sdv", auto=True)
    sac = Ctr("sac", auto=True)
    dma_ctrs = [ssu, slw[0], slw[1], sgw, sgwd, sga, sfg[0], sfg[1],
                sfd[0], sfd[1], sat[0], sat[1], sato, sgo]

    class EngProxy:
        _PASS = ("wait_ge", "dma_start")

        def __init__(self, eng, ctr):
            object.__setattr__(self, "_eng", eng)
            object.__setattr__(self, "_ctr", ctr)

        def __getattr__(self, name):
            f = getattr(self._eng, name)
            if name in EngProxy._PASS or not callable(f):
                return f
            eng, ctr = self._eng, self._ctr

            def wrapped(*a, **kw):
                if ctr.n > 0:
                    eng.wait_ge(ctr.h, ctr.n)
                instr = f(*a, **kw)
                instr.then_inc(ctr.h, 1)
                ctr.n += 1
                return instr
            return wrapped

    blk = es.enter_context(nc.Block())
    T, Y, G = blk.tensor, blk.sync, blk.gpsimd
    V = lambda f: blk.vector(lambda v: f(EngProxy(v, sdv)))
    S = lambda f: blk.scalar(lambda s: f(EngProxy(s, sac)))

    wv3 = lambda ap: ap.rearrange("p (b w) -> p b w", w=16)

    def mkquant(qs):
        q_i, q_r, q_c, q_bm, q_sc, q_iv, q_sh = qs

        def bc3(tile, nb):
            return tile[:, 0:nb].unsqueeze(2).broadcast_to([128, nb, 16])

        def emit_quant(v, src_ap, F, dst_ap):
            nb = F // 16
            v.tensor_reduce(q_bm[:, 0:nb], wv3(src_ap), AX.X, ALU.max,
                            apply_absolute_value=True)
            v.tensor_scalar(q_sc[:, 0:nb], q_bm[:, 0:nb], INV6, 1e-12,
                            ALU.mult, ALU.max)
            v.reciprocal(q_iv[:, 0:nb], q_sc[:, 0:nb])
            v.tensor_scalar(q_iv[:, 0:nb], q_iv[:, 0:nb], 2.0, None, ALU.mult)
            v.tensor_scalar(q_sh[:, 0:nb], q_sc[:, 0:nb], 0.5, None, ALU.mult)
            v.tensor_tensor(wv3(src_ap), wv3(src_ap), bc3(q_iv, nb), ALU.mult)
            v.tensor_scalar(q_i[:, 0:F], src_ap.bitcast(I32),
                            float(0x00200000), None, ALU.add)
            v.tensor_scalar(q_i[:, 0:F], q_i[:, 0:F], int(-4194304), None,
                            ALU.bitwise_and)
            v.tensor_scalar(q_c[:, 0:F], q_i[:, 0:F].bitcast(F32), -4.0, 4.0,
                            ALU.max, ALU.min)
            v.tensor_tensor(q_c[:, 0:F], q_i[:, 0:F].bitcast(F32), q_c[:, 0:F],
                            ALU.subtract)
            v.tensor_scalar(src_ap, src_ap, 4.0, -4.0, ALU.min, ALU.max)
            v.tensor_scalar(q_r[:, 0:F], src_ap, 12582912.0, -12582912.0,
                            ALU.add, ALU.add)
            v.tensor_tensor(q_r[:, 0:F], q_r[:, 0:F], q_c[:, 0:F], ALU.add)
            return v.tensor_tensor(wv3(dst_ap), wv3(q_r[:, 0:F]), bc3(q_sh, nb),
                                   ALU.mult)
        return emit_quant

    def mk_qscratch(F, tag, alloc):
        return (alloc(f"q_i{tag}", [128, F], I32),
                alloc(f"q_r{tag}", [128, F], BF16),
                alloc(f"q_c{tag}", [128, F], BF16), alloc(f"q_bm{tag}", [128, F // 16], F32),
                alloc(f"q_sc{tag}", [128, F // 16], F32), alloc(f"q_iv{tag}", [128, F // 16], F32),
                alloc(f"q_sh{tag}", [128, F // 16], F32))

    def Rt(t):
        return R[:, t * DM:(t + 1) * DM]

    def emit_ln_stats(tiles, rwaits):
        c0, c1 = min(tiles), max(tiles) + 1
        assert tiles == list(range(c0, c1))
        sl = lambda tt: tt[:, c0:c1]

        def _(v):
            for ctr, m in rwaits:
                v.wait_ge(ctr.h, m)
            sdv.inc(v.tensor_reduce(
                sl(st_s1),
                R[:, c0 * DM:c1 * DM].rearrange("p (t c) -> p t c", c=DM),
                AX.X, ALU.add))
        V(_)

        def _(s):
            for ctr, m in rwaits:
                s.wait_ge(ctr.h, m)
            s.wait_ge(sdv.h, sdv.n)
            last = None
            for j, t in enumerate(tiles):
                last = s.activation(wkC[:, (j % 2) * DM:(j % 2) * DM + DM],
                                    Rt(t), AF.Square,
                                    accum_out=st_ss[:, t:t + 1])
            sac.inc(last)
        S(_)
        acm = sac.n

        def _(v):
            v.wait_ge(sac.h, acm)
            v.tensor_scalar(sl(st_m), sl(st_s1), RD, None, ALU.mult)
            v.tensor_scalar(sl(st_v), sl(st_ss), RD, None, ALU.mult)
            v.tensor_tensor(sl(st_t1), sl(st_m), sl(st_m), ALU.mult)
            v.tensor_tensor(sl(st_v), sl(st_v), sl(st_t1), ALU.subtract)
            sdv.inc(v.tensor_scalar(sl(st_v), sl(st_v), EPS, None, ALU.add))
        V(_)
        vm = sdv.n

        def _(s):
            s.wait_ge(sdv.h, vm)
            sac.inc(s.activation(sl(st_t2), sl(st_v), AF.Sqrt))
        S(_)
        sqm = sac.n

        def _(v):
            v.wait_ge(sac.h, sqm)
            v.reciprocal(sl(st_r), sl(st_t2))
            sdv.inc(v.scalar_tensor_tensor(sl(st_a), sl(st_m), -1.0, sl(st_r),
                                           ALU.mult, ALU.mult))
        V(_)
        return sdv.n

    def _(g):
        for t in range(NT):
            g.dma_start(out=Rt(t), in_=dx[t * TILE:(t + 1) * TILE, :]).then_inc(ssu.h, 16)
        for dst, src in [(cosr, dcos), (sinr, dsin), (p128t, dp128),
                         (identb, dident), (mask4, dmask), (gammaT, dgam),
                         (betaT, dbet)]:
            g.dma_start(out=dst[:], in_=src).then_inc(ssu.h, 16)
        g.dma_start(out=wkC[:, 0:DK * HD].rearrange("p (k d) -> p k d", d=HD),
                    in_=dwg1.rearrange("(k p) d -> p k d", p=128)).then_inc(ssu.h, 16)
        g.dma_start(out=wkD[0:64, 0:DM], in_=dwg2).then_inc(ssu.h, 16)
        ssu.n += 16 * (NT + 7 + 2)
    G(_)
    setup_m = ssu.n

    def _(s):
        s.wait_ge(ssu.h, setup_m)
        s.copy(wg1b[:], wkC[:, 0:DK * HD])
        sac.inc(s.copy(wg2b[:], wkD[0:64, 0:DM]))
    S(_)
    wgsmall_m = sac.n

    wprep_marks = {}
    def wait_stores(e):
        for ctr in (sgw, sgwd, sga):
            if ctr.n > 0:
                e.wait_ge(ctr.h, ctr.n)

    with ExitStack() as wes:
        WSB = lambda n, s, d: wes.enter_context(nc.sbuf_tensor(n, s, d))
        wldA = WSB("wldA", [128, FFI], F32)
        wldB = WSB("wldB", [128, FFI], F32)
        wqb = WSB("wqb", [128, FFI], BF16)
        evacW = WSB("evacW", [128, 2048], BF16)
        qsW = mk_qscratch(FFI, "W", WSB)
        quantW = mkquant(qsW)

        with nc.psum_tensor("psW", [128, 2048], BF16) as psW:
            for name, dsrc, ddstT, nchunk, F in [
                ("wg", dwgate, s_wgT, NKI, DM),
                ("wu", dwup, s_wuT, NKI, DM),
                ("wd", dwdown, s_wdT, DK, FFI),
            ]:
                sst = sgw if name in ("wg", "wu") else sgwd
                for j in range(nchunk):
                    buf = wldA if (j % 2 == 0) else wldB
                    slot = slw[j % 2]

                    def _(y, j=j, buf=buf, dsrc=dsrc, F=F):
                        y.wait_ge(sdv.h, sdv.n)
                        y.dma_start(out=buf[:, 0:F],
                                    in_=dsrc[j * 128:(j + 1) * 128, 0:F]).then_inc(slot.h, 16)
                    Y(_)
                    slot.n += 16
                    ldm = slot.n

                    def _(v, buf=buf, F=F, ldm=ldm, slot=slot):
                        v.wait_ge(slot.h, ldm)
                        v.wait_ge(spe.h, spe.n)
                        sdv.inc(quantW(v, buf[:, 0:F], F, wqb[:, 0:F]))
                    V(_)
                    qm = sdv.n

                    nsb = F // 128
                    for g0 in range(0, nsb, 8):
                        gn = min(8, nsb - g0)
                        pp = ((g0 // 8) + j) % 2

                        def _(t, g0=g0, gn=gn, pp=pp, qm=qm):
                            t.wait_ge(sdv.h, qm)
                            t.wait_ge(sac.h, sac.n)
                            last = None
                            for s in range(gn):
                                last = t.transpose(
                                    psW[:, pp * 1024 + s * 128: pp * 1024 + (s + 1) * 128],
                                    wqb[:, (g0 + s) * 128:(g0 + s + 1) * 128],
                                    identb[:])
                            spe.inc(last)
                        T(_)
                        tm = spe.n

                        def _(s, gn=gn, pp=pp, tm=tm):
                            s.wait_ge(spe.h, tm)
                            wait_stores(s)
                            sac.inc(s.copy(evacW[:, pp * 1024: pp * 1024 + gn * 128],
                                           psW[:, pp * 1024: pp * 1024 + gn * 128]))
                        S(_)
                        am = sac.n

                        def _(g, j=j, g0=g0, gn=gn, pp=pp, am=am, ddstT=ddstT,
                              sst=sst):
                            g.wait_ge(sac.h, am)
                            for s in range(gn):
                                g.dma_start(
                                    out=ddstT[(g0 + s) * 128:(g0 + s + 1) * 128,
                                              j * 128:(j + 1) * 128],
                                    in_=evacW[:, pp * 1024 + s * 128: pp * 1024 + (s + 1) * 128],
                                ).then_inc(sst.h, 16)
                        G(_)
                        sst.n += 16 * gn
                wprep_marks[name] = sst.n

            for nm in ("q", "k", "v", "o"):
                dsrc, ddst = {"q": (dwq, s_att["q"]), "k": (dwk, s_att["k"]),
                              "v": (dwv, s_att["v"]), "o": (dwo, s_att["o"])}[nm]
                for k in range(DK):
                    buf = wldA if (k % 2 == 0) else wldB
                    pp = k % 2
                    slot = slw[k % 2]

                    def _(y, k=k, buf=buf, dsrc=dsrc):
                        y.wait_ge(sac.h, sac.n)
                        y.dma_start(out=buf[:, 0:DM],
                                    in_=dsrc[k * 128:(k + 1) * 128, :]).then_inc(slot.h, 16)
                    Y(_)
                    slot.n += 16
                    ldm = slot.n

                    def _(s, buf=buf, ldm=ldm, pp=pp, slot=slot):
                        s.wait_ge(slot.h, ldm)
                        wait_stores(s)
                        sac.inc(s.copy(evacW[:, pp * 1024:(pp + 1) * 1024],
                                       buf[:, 0:DM]))
                    S(_)
                    am = sac.n

                    def _(g, k=k, am=am, pp=pp, ddst=ddst):
                        g.wait_ge(sac.h, am)
                        g.dma_start(out=ddst[k * 128:(k + 1) * 128, :],
                                    in_=evacW[:, pp * 1024:(pp + 1) * 1024]).then_inc(sga.h, 16)
                    G(_)
                    sga.n += 16
            wprep_marks["attn"] = sga.n

    def emit_phase_barrier():
        marks = [(c, c.n) for c in dma_ctrs if c.n > 0]
        marks += [(spe, spe.n), (sdv, sdv.n), (sac, sac.n)]

        def mk(own):
            def _(e):
                for ctr, m in marks:
                    if ctr is not own:
                        e.wait_ge(ctr.h, m)
            return _
        V(mk(None))
        S(mk(None))
        T(mk(None))
        Y(mk(None))
        G(mk(None))

    def emit_ffn_call(ci, rwaits):
        with ExitStack() as fes:
            FSB = lambda n, s, d: fes.enter_context(nc.sbuf_tensor(n, s, d))
            xq_fm = FSB(f"xq{ci}", [128, DK * 512], BF16)
            hq_fm = FSB(f"hq{ci}", [128, NKI * 512], BF16)
            wgbuf = FSB(f"wgb{ci}", [128, 2 * 4096], BF16)
            wubuf = FSB(f"wub{ci}", [128, 2 * 4096], BF16)
            wdbuf = FSB(f"wdb{ci}", [128, 2 * DM], BF16)
            hb = FSB(f"hb{ci}", [128, QT * 1024], F32)
            qsF = mk_qscratch(1024, f"F{ci}", FSB)
            quantF = mkquant(qsF)
            resid_m = [0]
            wkd_m = {}
            btA_m = {}
            wsl_m = {}
            wdl_m = {}

            for iq in range(NQ):
                tiles = [iq * QT + j for j in range(QT)]
                stats_m = emit_ln_stats(tiles, rwaits)

                with nc.psum_tensor(f"pstA{ci}_{iq}", [128, 2048], BF16) as pstA:
                    for j, t in enumerate(tiles):
                        pp = j % 2

                        def _(s, t=t, pp=pp, stats_m=stats_m):
                            if pp in wkd_m:
                                s.wait_ge(sdv.h, wkd_m[pp])
                            s.wait_ge(sdv.h, stats_m)
                            sac.inc(s.activation(wkD[:, pp * DM:(pp + 1) * DM],
                                                 Rt(t), AF.Identity,
                                                 bias=st_a[:, t:t + 1],
                                                 scale=st_r[:, t:t + 1]))
                        S(_)
                        am = sac.n

                        def _(v, pp=pp, am=am):
                            v.wait_ge(sac.h, am)
                            if pp in btA_m:
                                v.wait_ge(spe.h, btA_m[pp])
                            sdv.inc(quantF(v, wkD[:, pp * DM:(pp + 1) * DM], DM,
                                           btA[:, pp * DM:(pp + 1) * DM]))
                        V(_)
                        qm = sdv.n
                        wkd_m[pp] = qm

                        def _(t_, pp=pp, qm=qm):
                            t_.wait_ge(sdv.h, qm)
                            if resid_m[0]:
                                t_.wait_ge(sdv.h, resid_m[0])
                            t_.wait_ge(sac.h, sac.n)
                            last = None
                            for k in range(DK):
                                last = t_.transpose(
                                    pstA[:, pp * 1024 + k * 128: pp * 1024 + (k + 1) * 128],
                                    btA[:, pp * DM + k * 128: pp * DM + (k + 1) * 128],
                                    identb[:])
                            spe.inc(last)
                        T(_)
                        tm = spe.n
                        btA_m[pp] = tm

                        def _(s, j=j, pp=pp, tm=tm):
                            s.wait_ge(spe.h, tm)
                            dst = xq_fm[:].rearrange("p (k q) -> p k q", q=512)[
                                :, :, j * 128:(j + 1) * 128]
                            sac.inc(s.copy(dst,
                                           pstA[:, pp * 1024:(pp + 1) * 1024]
                                           .rearrange("p (k q) -> p k q", q=128)))
                        S(_)
                    xq_m = sac.n

                with nc.psum_tensor(f"psg{ci}_{iq}", [128, 1024], F32) as psg, \
                     nc.psum_tensor(f"psu{ci}_{iq}", [128, 1024], F32) as psu, \
                     nc.psum_tensor(f"pstB{ci}_{iq}", [128, 2048], BF16) as pstB:
                    it = 0
                    qp = 0
                    bmark = {}
                    for pi, (p0, pw) in enumerate(MGP):
                        for sub in range(2):
                            mgi = pi * 2 + sub
                            m0, mw = MGS[mgi]
                            wp = mgi % 2
                            slot = sfg[wp]

                            def _(y, m0=m0, mw=mw, wp=wp, ci=ci, slot=slot):
                                if wp in wsl_m:
                                    y.wait_ge(spe.h, wsl_m[wp])
                                if ci == 0:
                                    y.wait_ge(sgw.h, wprep_marks["wu"])
                                for k in range(DK):
                                    y.dma_start(out=wgbuf[:, wp * 4096 + k * 512: wp * 4096 + k * 512 + mw],
                                                in_=s_wgT[k * 128:(k + 1) * 128, m0:m0 + mw]).then_inc(slot.h, 16)
                                    y.dma_start(out=wubuf[:, wp * 4096 + k * 512: wp * 4096 + k * 512 + mw],
                                                in_=s_wuT[k * 128:(k + 1) * 128, m0:m0 + mw]).then_inc(slot.h, 16)
                            Y(_)
                            slot.n += 16 * 2 * DK
                            wlm = slot.n

                            for j in range(QT):
                                pp = it % 2

                                def _(t_, j=j, pp=pp, mw=mw, wp=wp, wlm=wlm,
                                      xq_m=xq_m, slot=slot, it=it):
                                    t_.wait_ge(slot.h, wlm)
                                    t_.wait_ge(sac.h, xq_m)
                                    if it - 2 in bmark:
                                        t_.wait_ge(sdv.h, bmark[it - 2])
                                    elif resid_m[0]:
                                        t_.wait_ge(sdv.h, resid_m[0])
                                    last = None
                                    for k in range(DK):
                                        lhs = xq_fm[:, k * 512 + j * 128: k * 512 + (j + 1) * 128]
                                        t_.matmul(psg[:, pp * 512: pp * 512 + mw], lhs,
                                                  wgbuf[:, wp * 4096 + k * 512: wp * 4096 + k * 512 + mw],
                                                  start=(k == 0), stop=(k == DK - 1))
                                        last = t_.matmul(psu[:, pp * 512: pp * 512 + mw], lhs,
                                                         wubuf[:, wp * 4096 + k * 512: wp * 4096 + k * 512 + mw],
                                                         start=(k == 0), stop=(k == DK - 1))
                                    spe.inc(last)
                                T(_)
                                mmm = spe.n

                                def _(s, pp=pp, mw=mw, mmm=mmm, it=it):
                                    s.wait_ge(spe.h, mmm)
                                    if it - 2 in bmark:
                                        s.wait_ge(sdv.h, bmark[it - 2])
                                    sac.inc(s.activation(wkA[:, pp * 512: pp * 512 + mw],
                                                         psg[:, pp * 512: pp * 512 + mw],
                                                         AF.Silu))
                                S(_)
                                em = sac.n

                                def _(v, j=j, pp=pp, m0=m0, mw=mw, p0=p0, em=em):
                                    v.wait_ge(sac.h, em)
                                    off = j * 1024 + (m0 - p0)
                                    sdv.inc(v.tensor_tensor(
                                        hb[:, off:off + mw],
                                        wkA[:, pp * 512: pp * 512 + mw],
                                        psu[:, pp * 512: pp * 512 + mw], ALU.mult))
                                V(_)
                                bmark[it] = sdv.n
                                it += 1
                            wsl_m[wp] = spe.n

                        for j in range(QT):
                            qpp = qp % 2
                            qp += 1

                            def _(v, j=j, pw=pw, qpp=qpp):
                                v.wait_ge(spe.h, spe.n)
                                sdv.inc(quantF(v, hb[:, j * 1024: j * 1024 + pw],
                                               pw, btB[:, qpp * DM: qpp * DM + pw]))
                            V(_)
                            hqm = sdv.n

                            def _(t_, pw=pw, qpp=qpp, hqm=hqm):
                                t_.wait_ge(sdv.h, hqm)
                                t_.wait_ge(sac.h, sac.n)
                                last = None
                                for sb in range(pw // 128):
                                    last = t_.transpose(
                                        pstB[:, qpp * 1024 + sb * 128: qpp * 1024 + (sb + 1) * 128],
                                        btB[:, qpp * DM + sb * 128: qpp * DM + (sb + 1) * 128],
                                        identb[:])
                                spe.inc(last)
                            T(_)
                            trm = spe.n

                            def _(s, j=j, p0=p0, pw=pw, qpp=qpp, trm=trm):
                                s.wait_ge(spe.h, trm)
                                kk0 = p0 // 128
                                dst = hq_fm[:].rearrange("p (k q) -> p k q", q=512)[
                                    :, kk0:kk0 + pw // 128, j * 128:(j + 1) * 128]
                                sac.inc(s.copy(dst,
                                               pstB[:, qpp * 1024: qpp * 1024 + pw]
                                               .rearrange("p (k q) -> p k q", q=128)))
                            S(_)
                    hq_m = sac.n

                with nc.psum_tensor(f"psd{ci}_{iq}", [128, 4096], F32) as psd:
                    for kp in range(NKI):
                        wp = kp % 2

                        slot = sfd[wp]

                        def _(y, kp=kp, wp=wp, ci=ci, slot=slot):
                            if wp in wdl_m:
                                y.wait_ge(spe.h, wdl_m[wp])
                            if ci == 0 and kp < 2:
                                y.wait_ge(sgwd.h, wprep_marks["wd"])
                            y.dma_start(out=wdbuf[:, wp * DM:(wp + 1) * DM],
                                        in_=s_wdT[kp * 128:(kp + 1) * 128, :]).then_inc(slot.h, 16)
                        Y(_)
                        slot.n += 16
                        wlm = slot.n

                        def _(t_, kp=kp, wp=wp, wlm=wlm, hq_m=hq_m, slot=slot):
                            t_.wait_ge(slot.h, wlm)
                            t_.wait_ge(sac.h, hq_m)
                            if kp == 0 and resid_m[0]:
                                t_.wait_ge(sdv.h, resid_m[0])
                            last = None
                            for j in range(QT):
                                lhs = hq_fm[:, kp * 512 + j * 128: kp * 512 + (j + 1) * 128]
                                for hf in range(2):
                                    last = t_.matmul(
                                        psd[:, j * 1024 + hf * 512: j * 1024 + (hf + 1) * 512],
                                        lhs,
                                        wdbuf[:, wp * DM + hf * 512: wp * DM + (hf + 1) * 512],
                                        start=(kp == 0), stop=(kp == NKI - 1))
                            spe.inc(last)
                        T(_)
                        wdl_m[wp] = spe.n
                    down_m = spe.n

                    def _(v, tiles=tiles, down_m=down_m):
                        v.wait_ge(spe.h, down_m)
                        t0 = tiles[0]
                        rsl = R[:, t0 * DM:(t0 + QT) * DM]
                        sdv.inc(v.scalar_tensor_tensor(rsl, psd[:], 0.5, rsl,
                                                       ALU.mult, ALU.add))
                    V(_)
                    resid_m[0] = sdv.n
        return sdv.n

    def emit_attn(rwaits):
        with ExitStack() as aes:
            ASB = lambda n, s, d: aes.enter_context(nc.sbuf_tensor(n, s, d))
            xn_fm = ASB("xn_fm", [128, DK * GTOK], BF16)
            qs_b = ASB("qs_b", [128, 2 * GTOK], BF16)
            qp_b = ASB("qp_b", [128, 2 * GTOK], BF16)
            qh = ASB("qh", [64, H * GTOK], BF16)
            kh = ASB("kh", [64, H * GTOK], BF16)
            v_g = ASB("v_g", [128, GT * DM], BF16)
            xg1g = ASB("xg1g", [64, GTOK], BF16)
            sTm = ASB("sTm", [128, 2 * 512], BF16)
            ktm_sb = ASB("ktm_sb", [128, DM], BF16)
            S32 = ASB("S32", [64, H * HD], F32)
            Sb = ASB("Sb", [64, H * HD], BF16)
            wbufA = ASB("wbufA", [128, 3 * 8192], BF16)
            y_fm = ASB("y_fm", [128, DM], BF16)
            gate_sb = ASB("gate_sb", [128, GT * DM], F32)

            stats_m = emit_ln_stats(list(range(NT)), rwaits)

            def _(v):
                v.memset(S32[:], 0.0)
                sdv.inc(v.memset(Sb[:], 0.0))
            V(_)

            with nc.psum_tensor("psA", [128, 2048], F32) as psA, \
                 nc.psum_tensor("psB", [128, 1024], F32) as psB, \
                 nc.psum_tensor("psC", [128, 512], F32) as psC, \
                 nc.psum_tensor("psD", [128, 1024], BF16) as psD:

                rs = {"psC_v": 0, "psD_v": 0, "psD_s": 0, "btA": 0,
                      "psB_s": 0, "psB_v": 0, "wkC_s": 0}
                sTm_rd = {0: 0, 1: 0}
                psA_m = {0: 0, 1: 0}
                gate_rd = [0]
                wsl_rd = {0: 0, 1: 0}
                cs = {}
                gctx = {}

                def sections(g):
                    tiles = [g * GT + j for j in range(GT)]
                    gs = slice(g * GTOK, (g + 1) * GTOK)
                    gp = g % 2
                    ctx = {}

                    for j, t in enumerate(tiles):
                        pp = j % 2

                        def _(s, t=t, pp=pp):
                            s.wait_ge(sdv.h, stats_m)
                            s.wait_ge(spe.h, rs["btA"])
                            sac.inc(s.activation(btA[:, pp * DM:(pp + 1) * DM],
                                                 Rt(t), AF.Identity,
                                                 bias=st_a[:, t:t + 1],
                                                 scale=st_r[:, t:t + 1]))
                        S(_)
                        am = sac.n

                        def _(t_, pp=pp, am=am):
                            t_.wait_ge(sac.h, am)
                            t_.wait_ge(sac.h, rs["psD_s"])
                            t_.wait_ge(sdv.h, rs["psD_v"])
                            last = None
                            for k in range(DK):
                                last = t_.transpose(
                                    psD[:, k * 128:(k + 1) * 128],
                                    btA[:, pp * DM + k * 128: pp * DM + (k + 1) * 128],
                                    identb[:])
                            spe.inc(last)
                        T(_)
                        tm = spe.n
                        rs["btA"] = tm

                        def _(s, j=j, tm=tm):
                            s.wait_ge(spe.h, tm)
                            dst = xn_fm[:].rearrange("p (k q) -> p k q", q=GTOK)[
                                :, :, j * 128:(j + 1) * 128]
                            sac.inc(s.copy(dst, psD[:].rearrange("p (k q) -> p k q", q=128)))
                        S(_)
                        rs["psD_s"] = sac.n
                    xn_m = sac.n
                    ctx["xn_m"] = xn_m

                    wmarks = {}
                    for nm, wp in (("q", 0), ("k", 1)):
                        def _(y, nm=nm, wp=wp, g=g, slot=sat[wp]):
                            y.wait_ge(spe.h, wsl_rd[wp])
                            if g == 0:
                                y.wait_ge(sga.h, wprep_marks["attn"])
                            for k in range(DK):
                                y.dma_start(out=wbufA[:, wp * 8192 + k * 1024: wp * 8192 + (k + 1) * 1024],
                                            in_=s_att[nm][k * 128:(k + 1) * 128, :]).then_inc(slot.h, 16)
                        Y(_)
                        sat[wp].n += 16 * DK
                        wmarks[nm] = (wp, sat[wp].n)

                    projpe = {}
                    for nm, htile in (("q", qh), ("k", kh)):
                        wp, wlm = wmarks[nm]
                        for m in range(DK):
                            pp = m % 2

                            def _(t_, m=m, pp=pp, wp=wp, wlm=wlm, xn_m=xn_m):
                                t_.wait_ge(sat[wp].h, wlm)
                                t_.wait_ge(sac.h, xn_m)
                                t_.wait_ge(sac.h, sac.n)
                                t_.wait_ge(sdv.h, psA_m[0])
                                last = None
                                for k in range(DK):
                                    last = t_.matmul(
                                        psA[:, pp * 512: pp * 512 + GTOK],
                                        wbufA[:, wp * 8192 + k * 1024 + m * 128: wp * 8192 + k * 1024 + (m + 1) * 128],
                                        xn_fm[:, k * GTOK:(k + 1) * GTOK],
                                        start=(k == 0), stop=(k == DK - 1))
                                spe.inc(last)
                            T(_)
                            pm = spe.n

                            def _(s, pp=pp, pm=pm):
                                s.wait_ge(spe.h, pm)
                                s.wait_ge(spe.h, spe.n)
                                sac.inc(s.activation(qs_b[:, pp * GTOK:(pp + 1) * GTOK],
                                                     psA[:, pp * 512: pp * 512 + GTOK],
                                                     AF.Silu))
                            S(_)
                            sm = sac.n

                            def _(t_, pp=pp, sm=sm):
                                t_.wait_ge(sac.h, sm)
                                t_.wait_ge(sac.h, sac.n)
                                t_.wait_ge(sdv.h, rs["psB_v"])
                                spe.inc(t_.matmul(psB[:, pp * 512: pp * 512 + GTOK],
                                                  p128t[:],
                                                  qs_b[:, pp * GTOK:(pp + 1) * GTOK],
                                                  start=True, stop=True))
                            T(_)
                            qpm = spe.n

                            def _(s, pp=pp, qpm=qpm):
                                s.wait_ge(spe.h, qpm)
                                s.wait_ge(sdv.h, sdv.n)
                                sac.inc(s.copy(qp_b[:, pp * GTOK:(pp + 1) * GTOK],
                                               psB[:, pp * 512: pp * 512 + GTOK]))
                            S(_)
                            qem = sac.n
                            rs["psB_s"] = qem

                            def _(v, htile=htile, m=m, pp=pp, qem=qem, gs=gs):
                                v.wait_ge(sac.h, qem)
                                v.wait_ge(spe.h, spe.n)
                                v.tensor_tensor(btB[:, 0:GTOK],
                                                qs_b[:, pp * GTOK:(pp + 1) * GTOK],
                                                cosr[:, gs], ALU.mult)
                                v.tensor_tensor(btB[:, GTOK:2 * GTOK],
                                                qp_b[:, pp * GTOK:(pp + 1) * GTOK],
                                                sinr[:, gs], ALU.mult)
                                h0, h1 = 2 * m, 2 * m + 1
                                v.tensor_tensor(htile[:, h0 * GTOK:(h0 + 1) * GTOK],
                                                btB[0:64, 0:GTOK],
                                                btB[0:64, GTOK:2 * GTOK], ALU.add)
                                sdv.inc(v.tensor_tensor(htile[:, h1 * GTOK:(h1 + 1) * GTOK],
                                                        btB[64:128, 0:GTOK],
                                                        btB[64:128, GTOK:2 * GTOK],
                                                        ALU.add))
                            V(_)
                        projpe[nm] = spe.n
                    ctx["qk_m"] = sdv.n
                    wsl_rd[1] = projpe["k"]

                    def _(y, g=g, slot=sat[0]):
                        y.wait_ge(spe.h, projpe["q"])
                        for k in range(DK):
                            y.dma_start(out=wbufA[:, k * 1024:(k + 1) * 1024],
                                        in_=s_att["v"][k * 128:(k + 1) * 128, :]).then_inc(slot.h, 16)
                    Y(_)
                    sat[0].n += 16 * DK
                    vwlm = sat[0].n

                    for c in range(GT):
                        def _(t_, c=c, vwlm=vwlm, xn_m=xn_m):
                            t_.wait_ge(sat[0].h, vwlm)
                            t_.wait_ge(sac.h, xn_m)
                            t_.wait_ge(sac.h, sac.n)
                            t_.wait_ge(sdv.h, psA_m[0])
                            last = None
                            for k in range(DK):
                                lhs = xn_fm[:, k * GTOK + c * 128: k * GTOK + (c + 1) * 128]
                                for hf in range(2):
                                    last = t_.matmul(
                                        psA[:, hf * 512:(hf + 1) * 512], lhs,
                                        wbufA[:, k * 1024 + hf * 512: k * 1024 + (hf + 1) * 512],
                                        start=(k == 0), stop=(k == DK - 1))
                            spe.inc(last)
                        T(_)
                        vm = spe.n

                        def _(s, c=c, vm=vm):
                            s.wait_ge(spe.h, vm)
                            sac.inc(s.copy(v_g[:, c * DM:(c + 1) * DM], psA[:, 0:1024]))
                        S(_)
                    ctx["v_m"] = sac.n
                    wsl_rd[0] = spe.n

                    def _(t_, xn_m=xn_m):
                        t_.wait_ge(sac.h, xn_m)
                        t_.wait_ge(sac.h, wgsmall_m)
                        t_.wait_ge(sac.h, sac.n)
                        t_.wait_ge(sdv.h, rs["psC_v"])
                        last = None
                        for k in range(DK):
                            last = t_.matmul(psC[0:64, 0:GTOK],
                                             wg1b[:, k * HD:(k + 1) * HD],
                                             xn_fm[:, k * GTOK:(k + 1) * GTOK],
                                             start=(k == 0), stop=(k == DK - 1))
                        spe.inc(last)
                    T(_)
                    x1m = spe.n

                    def _(s, x1m=x1m):
                        s.wait_ge(spe.h, x1m)
                        s.wait_ge(spe.h, spe.n)
                        sac.inc(s.copy(xg1g[:, 0:GTOK], psC[0:64, 0:GTOK]))
                    S(_)
                    ctx["xg1_m"] = sac.n
                    gctx[g] = ctx

                def emit_gate(g):
                    ctx = gctx[g]
                    for cl in range(GT):
                        def _(t_, cl=cl, xg1_m=ctx["xg1_m"]):
                            t_.wait_ge(sac.h, xg1_m)
                            t_.wait_ge(sac.h, wgsmall_m)
                            t_.wait_ge(sdv.h, rs["psB_v"])
                            t_.wait_ge(sac.h, rs["psB_s"])
                            last = None
                            for hf in range(2):
                                last = t_.matmul(psB[:, hf * 512:(hf + 1) * 512],
                                                 xg1g[:, cl * 128:(cl + 1) * 128],
                                                 wg2b[:, hf * 512:(hf + 1) * 512],
                                                 start=True, stop=True)
                            spe.inc(last)
                        T(_)
                        gz_m = spe.n

                        def _(s, cl=cl, gz_m=gz_m):
                            s.wait_ge(spe.h, gz_m)
                            s.wait_ge(sdv.h, gate_rd[0])
                            sac.inc(s.activation(
                                gate_sb[:, cl * DM:(cl + 1) * DM],
                                psB[:], AF.Tanh, scale=0.5))
                        S(_)
                        tg_m = sac.n
                        rs["psB_s"] = tg_m

                        def _(v, cl=cl, tg_m=tg_m):
                            v.wait_ge(sac.h, tg_m)
                            gsl = gate_sb[:, cl * DM:(cl + 1) * DM]
                            sdv.inc(v.tensor_scalar(gsl, gsl, 0.5, 0.5,
                                                    ALU.mult, ALU.add))
                        V(_)

                def o_load(g):
                    def _(y, g=g):
                        if g > 0:
                            y.wait_ge(spe.h, cs[2 * g - 1]["wo_m"])
                        for k in range(DK):
                            y.dma_start(out=wbufA[:, 2 * 8192 + k * 1024: 2 * 8192 + (k + 1) * 1024],
                                        in_=s_att["o"][k * 128:(k + 1) * 128, :]).then_inc(sato.h, 16)
                    Y(_)
                    sato.n += 16 * DK
                    gctx[g]["o_m"] = sato.n

                def mixing(c):
                    g, cl = c // GT, c % GT
                    ctx = gctx[g]
                    pc = c % 2

                    out_m = None
                    for b in range(4):
                        pp = b % 2

                        def _(t_, b=b, cl=cl, ctx=ctx):
                            t_.wait_ge(sdv.h, ctx["qk_m"])
                            t_.wait_ge(sac.h, ctx["xg1_m"])
                            t_.wait_ge(sdv.h, rs["psC_v"])
                            last = None
                            for hh in range(4):
                                hc = b * 4 + hh
                                last = t_.matmul(
                                    psC[:, hh * 128:(hh + 1) * 128],
                                    kh[:, hc * GTOK + cl * 128: hc * GTOK + (cl + 1) * 128],
                                    qh[:, hc * GTOK + cl * 128: hc * GTOK + (cl + 1) * 128],
                                    start=True, stop=True)
                            spe.inc(last)
                        T(_)
                        stm = spe.n

                        def _(v, pp=pp, stm=stm):
                            v.wait_ge(spe.h, stm)
                            v.wait_ge(spe.h, sTm_rd[pp])
                            sdv.inc(v.tensor_tensor(sTm[:, pp * 512:(pp + 1) * 512],
                                                    psC[:], mask4[:], ALU.mult))
                        V(_)
                        evm = sdv.n
                        rs["psC_v"] = evm

                        def _(t_, b=b, cl=cl, pp=pp, pc=pc, evm=evm, ctx=ctx, c=c):
                            t_.wait_ge(sdv.h, evm)
                            t_.wait_ge(sac.h, ctx["v_m"])
                            if c >= 1:
                                t_.wait_ge(sdv.h, cs[c - 1]["supd"])
                            t_.wait_ge(sdv.h, psA_m[pc])
                            last = None
                            for hh in range(4):
                                hc = b * 4 + hh
                                t_.matmul(psA[:, pc * 1024 + hc * 64: pc * 1024 + (hc + 1) * 64],
                                          sTm[:, pp * 512 + hh * 128: pp * 512 + (hh + 1) * 128],
                                          v_g[:, cl * DM + hc * 64: cl * DM + (hc + 1) * 64],
                                          start=True, stop=False)
                                last = t_.matmul(psA[:, pc * 1024 + hc * 64: pc * 1024 + (hc + 1) * 64],
                                                 qh[:, hc * GTOK + cl * 128: hc * GTOK + (cl + 1) * 128],
                                                 Sb[:, hc * 64:(hc + 1) * 64],
                                                 start=False, stop=True)
                            spe.inc(last)
                        T(_)
                        sTm_rd[pp] = spe.n
                    out_m = spe.n

                    def _(t_, cl=cl, ctx=ctx):
                        t_.wait_ge(sdv.h, ctx["qk_m"])
                        t_.wait_ge(sdv.h, rs["psD_v"])
                        t_.wait_ge(sac.h, rs["psD_s"])
                        last = None
                        for hc in range(H):
                            last = t_.transpose(
                                psD[:, hc * 64:(hc + 1) * 64],
                                kh[:, hc * GTOK + cl * 128: hc * GTOK + (cl + 1) * 128],
                                identb[0:64, 0:64])
                        spe.inc(last)
                    T(_)
                    ktm_m = spe.n

                    def _(v, ktm_m=ktm_m):
                        v.wait_ge(spe.h, ktm_m)
                        sdv.inc(v.tensor_copy(ktm_sb[:], psD[:]))
                    V(_)
                    ktme_m = sdv.n
                    rs["psD_v"] = ktme_m

                    def _(t_, cl=cl, ktme_m=ktme_m, ctx=ctx):
                        t_.wait_ge(sdv.h, ktme_m)
                        t_.wait_ge(sac.h, ctx["v_m"])
                        t_.wait_ge(sac.h, rs["psB_s"])
                        t_.wait_ge(sdv.h, rs["psB_v"])
                        last = None
                        for hc in range(H):
                            last = t_.matmul(psB[0:64, hc * 64:(hc + 1) * 64],
                                             ktm_sb[:, hc * 64:(hc + 1) * 64],
                                             v_g[:, cl * DM + hc * 64: cl * DM + (hc + 1) * 64],
                                             start=True, stop=True)
                        spe.inc(last)
                    T(_)
                    sup_m = spe.n

                    def _(v, sup_m=sup_m, out_m=out_m):
                        v.wait_ge(spe.h, sup_m)
                        v.wait_ge(spe.h, out_m)
                        v.tensor_tensor(S32[:], S32[:], psB[0:64, :], ALU.add)
                        sdv.inc(v.tensor_copy(Sb[:], S32[:]))
                    V(_)
                    supd_m = sdv.n
                    rs["psB_v"] = supd_m
                    cs[c] = {"supd": supd_m, "out_m": out_m, "pc": pc,
                             "g": g, "cl": cl}

                def epilogue(c):
                    st = cs[c]
                    g, cl, pc = st["g"], st["cl"], st["pc"]
                    ctx = gctx[g]
                    gsl = gate_sb[:, cl * DM:(cl + 1) * DM]

                    def _(v, st=st, gsl=gsl, pc=pc):
                        v.wait_ge(spe.h, st["out_m"])
                        sl = lambda tt: tt[:, 0:1]
                        v.tensor_tensor(wkC[:, DM:2 * DM], gsl,
                                        psA[:, pc * 1024:(pc + 1) * 1024], ALU.mult)
                        v.tensor_reduce(st_s1[:, 0:1],
                                        wkC[:, DM:2 * DM], AX.X, ALU.add)
                        v.tensor_tensor(wkC[:, 0:DM], wkC[:, DM:2 * DM],
                                        wkC[:, DM:2 * DM], ALU.mult)
                        v.tensor_reduce(st_ss[:, 0:1], wkC[:, 0:DM], AX.X, ALU.add)
                        v.tensor_scalar(sl(st_m), sl(st_s1), RD, None, ALU.mult)
                        v.tensor_scalar(sl(st_v), sl(st_ss), RD, None, ALU.mult)
                        v.tensor_tensor(sl(st_t1), sl(st_m), sl(st_m), ALU.mult)
                        v.tensor_tensor(sl(st_v), sl(st_v), sl(st_t1), ALU.subtract)
                        sdv.inc(v.tensor_scalar(sl(st_v), sl(st_v), EPS, None,
                                                ALU.add))
                    V(_)
                    lnv_m = sdv.n
                    gate_rd[0] = lnv_m

                    def _(s, lnv_m=lnv_m):
                        s.wait_ge(sdv.h, lnv_m)
                        sac.inc(s.activation(st_t2[:, 0:1], st_v[:, 0:1], AF.Sqrt))
                    S(_)
                    sqr_m = sac.n

                    def _(v, sqr_m=sqr_m):
                        v.wait_ge(sac.h, sqr_m)
                        sl = lambda tt: tt[:, 0:1]
                        v.reciprocal(sl(st_r), sl(st_t2))
                        v.scalar_tensor_tensor(sl(st_a), sl(st_m), -1.0,
                                               sl(st_r), ALU.mult, ALU.mult)
                        rbc = st_r[:, 0:1].broadcast_to([128, DM])
                        abc = st_a[:, 0:1].broadcast_to([128, DM])
                        v.tensor_tensor(wkD[:, 0:DM], wkC[:, DM:2 * DM], rbc,
                                        ALU.mult)
                        v.tensor_tensor(wkD[:, 0:DM], wkD[:, 0:DM], abc, ALU.add)
                        v.wait_ge(spe.h, rs["btA"])
                        v.tensor_tensor(wkD[:, DM:2 * DM], wkD[:, 0:DM],
                                        gammaT[:], ALU.mult)
                        sdv.inc(v.tensor_tensor(btA[:, 0:DM], wkD[:, DM:2 * DM],
                                                betaT[:], ALU.add))
                    V(_)
                    y2_m = sdv.n

                    def _(t_, y2_m=y2_m):
                        t_.wait_ge(sdv.h, y2_m)
                        t_.wait_ge(sdv.h, rs["psD_v"])
                        t_.wait_ge(sac.h, rs["psD_s"])
                        last = None
                        for k in range(DK):
                            last = t_.transpose(psD[:, k * 128:(k + 1) * 128],
                                                btA[:, k * 128:(k + 1) * 128],
                                                identb[:])
                        spe.inc(last)
                    T(_)
                    ytr_m = spe.n
                    rs["btA"] = ytr_m

                    def _(v, ytr_m=ytr_m):
                        v.wait_ge(spe.h, ytr_m)
                        sdv.inc(v.tensor_copy(y_fm[:], psD[:]))
                    V(_)
                    yfm_m = sdv.n
                    rs["psD_v"] = yfm_m

                    def _(t_, yfm_m=yfm_m, lnv_m=lnv_m, pc=pc, o_m=ctx["o_m"]):
                        t_.wait_ge(sdv.h, yfm_m)
                        t_.wait_ge(sato.h, o_m)
                        t_.wait_ge(sdv.h, lnv_m)
                        last = None
                        for k in range(DK):
                            for hf in range(2):
                                last = t_.matmul(
                                    psA[:, pc * 1024 + hf * 512: pc * 1024 + (hf + 1) * 512],
                                    y_fm[:, k * 128:(k + 1) * 128],
                                    wbufA[:, 2 * 8192 + k * 1024 + hf * 512: 2 * 8192 + k * 1024 + (hf + 1) * 512],
                                    start=(k == 0), stop=(k == DK - 1))
                        spe.inc(last)
                    T(_)
                    wo_m = spe.n
                    st["wo_m"] = wo_m

                    def _(v, c=c, wo_m=wo_m, pc=pc):
                        v.wait_ge(spe.h, wo_m)
                        sdv.inc(v.scalar_tensor_tensor(Rt(c), psA[:, pc * 1024:(pc + 1) * 1024],
                                                       1.0, Rt(c), ALU.mult, ALU.add))
                    V(_)
                    psA_m[pc] = sdv.n

                pending = None
                for g in range(NGRP):
                    sections(g)
                    mixing(2 * g)
                    if pending is not None:
                        epilogue(pending)
                    o_load(g)
                    emit_gate(g)
                    mixing(2 * g + 1)
                    epilogue(2 * g)
                    pending = 2 * g + 1
                epilogue(pending)
        return sdv.n

    emit_phase_barrier()
    r_mark = [(ssu, setup_m)]
    if "ffn1" in PHASES:
        r_mark = [(sdv, emit_ffn_call(0, r_mark))]
        emit_phase_barrier()
    if "attn" in PHASES:
        r_mark = [(sdv, emit_attn(r_mark))]
        emit_phase_barrier()
    if "ffn2" in PHASES:
        r_mark = [(sdv, emit_ffn_call(1, r_mark))]

    def _(g):
        for ctr, m in r_mark:
            g.wait_ge(ctr.h, m)
        for t in range(NT):
            g.dma_start(out=dout[t * TILE:(t + 1) * TILE, :], in_=Rt(t)).then_inc(sgo.h, 16)
        sgo.n += 16 * NT
        g.wait_ge(sgo.h, sgo.n)
    G(_)

    es.close()
    return nc


def _host_consts():
    BASE = 10000.0
    inv_freq = BASE ** (-(np.arange(32, dtype=np.float64) * 2.0) / 64.0)
    pos = np.arange(NTOK, dtype=np.float64)
    ang = pos[:, None] * inv_freq[None, :]
    cos32 = np.cos(ang).T
    sin32 = np.sin(ang).T
    cosr = np.tile(cos32, (4, 1)).astype(ml_dtypes.bfloat16)
    sinr = np.tile(sin32, (4, 1)).astype(ml_dtypes.bfloat16)
    P64 = np.zeros((64, 64), np.float32)
    for i in range(32):
        P64[i, i + 32] = -1.0
        P64[i + 32, i] = 1.0
    P128 = np.zeros((128, 128), np.float32)
    P128[0:64, 0:64] = P64
    P128[64:128, 64:128] = P64
    p128t = P128.T.astype(ml_dtypes.bfloat16)
    ident = np.eye(128, dtype=np.float32).astype(ml_dtypes.bfloat16)
    mask = np.triu(np.ones((128, 128), np.float32))
    mask4 = np.tile(mask, (1, 4)).astype(ml_dtypes.bfloat16)
    return cosr, sinr, p128t, ident, mask4


def kernel(L0_attn_x, L0_attn_Wq, L0_attn_Wk, L0_attn_Wv, L0_attn_Wo,
           L0_attn_Wg1, L0_attn_Wg2, L0_attn_gamma, L0_attn_beta,
           L0_ffn_w_gate, L0_ffn_w_up, L0_ffn_w_down):
    x = np.ascontiguousarray(np.asarray(L0_attn_x, dtype=np.float32))
    B = x.shape[0]
    cosr, sinr, p128t, ident, mask4 = _host_consts()
    gammaT = np.ascontiguousarray(
        np.broadcast_to(np.asarray(L0_attn_gamma, np.float32), (128, DM)))
    betaT = np.ascontiguousarray(
        np.broadcast_to(np.asarray(L0_attn_beta, np.float32), (128, DM)))
    common = {
        "wq": np.ascontiguousarray(np.asarray(L0_attn_Wq, np.float32)),
        "wk": np.ascontiguousarray(np.asarray(L0_attn_Wk, np.float32)),
        "wv": np.ascontiguousarray(np.asarray(L0_attn_Wv, np.float32)),
        "wo": np.ascontiguousarray(np.asarray(L0_attn_Wo, np.float32)),
        "wg1": np.ascontiguousarray(np.asarray(L0_attn_Wg1, np.float32)),
        "wg2": np.ascontiguousarray(np.asarray(L0_attn_Wg2, np.float32)),
        "d_gammaT": gammaT, "d_betaT": betaT,
        "wgate": np.ascontiguousarray(np.asarray(L0_ffn_w_gate, np.float32)),
        "wup": np.ascontiguousarray(np.asarray(L0_ffn_w_up, np.float32)),
        "wdown": np.ascontiguousarray(np.asarray(L0_ffn_w_down, np.float32)),
        "d_cosr": cosr, "d_sinr": sinr, "d_p128t": p128t, "d_ident": ident,
        "d_mask4": mask4,
    }
    nc = build_program()
    in_maps = [dict(common, x=np.ascontiguousarray(x[b])) for b in range(B)]
    res = run_bass_kernel_spmd(nc, in_maps, core_ids=list(range(B)))
    return np.stack([res.results[b]["out"] for b in range(B)], axis=0)


# version 07
# speedup vs baseline: 1.5346x; 1.0438x over previous
import numpy as np
import ml_dtypes
from contextlib import ExitStack

import concourse.bass as bass
import concourse.mybir as mybir
from concourse.bass_utils import run_bass_kernel_spmd

F32 = mybir.dt.float32
BF16 = mybir.dt.bfloat16
I32 = mybir.dt.int32
AF = mybir.ActivationFunctionType
ALU = mybir.AluOpType
AX = mybir.AxisListType

NTOK, DM, TILE = 2048, 1024, 128
NT = NTOK // TILE
DK = DM // TILE
FFI = 2816
NKI = FFI // TILE
H, HD = 16, 64
MGS = [(0, 512), (512, 512), (1024, 512), (1536, 512), (2048, 512), (2560, 256)]
MGP = [(0, 1024), (1024, 1024), (2048, 768)]
QT = 4
NQ = NT // QT
GT = 2
NGRP = NT // GT
GTOK = GT * TILE

INV6 = float(np.float32(1.0) / np.float32(6.0))
ANDM = float(np.int64(0xFFC00000) - (1 << 32))
RD = float(np.float32(1.0) / np.float32(DM))
EPS = 1e-5

PHASES = ("ffn1", "attn", "ffn2")


def build_program():
    nc = bass.Bass("TRN2", target_bir_lowering=False)

    dx = nc.dram_tensor("x", [NTOK, DM], F32, kind="ExternalInput").ap()
    dwq = nc.dram_tensor("wq", [DM, DM], F32, kind="ExternalInput").ap()
    dwk = nc.dram_tensor("wk", [DM, DM], F32, kind="ExternalInput").ap()
    dwv = nc.dram_tensor("wv", [DM, DM], F32, kind="ExternalInput").ap()
    dwo = nc.dram_tensor("wo", [DM, DM], F32, kind="ExternalInput").ap()
    dwg1 = nc.dram_tensor("wg1", [DM, HD], F32, kind="ExternalInput").ap()
    dwg2 = nc.dram_tensor("wg2", [HD, DM], F32, kind="ExternalInput").ap()
    dgam = nc.dram_tensor("d_gammaT", [128, DM], F32, kind="ExternalInput").ap()
    dbet = nc.dram_tensor("d_betaT", [128, DM], F32, kind="ExternalInput").ap()
    dwgate = nc.dram_tensor("wgate", [FFI, DM], F32, kind="ExternalInput").ap()
    dwup = nc.dram_tensor("wup", [FFI, DM], F32, kind="ExternalInput").ap()
    dwdown = nc.dram_tensor("wdown", [DM, FFI], F32, kind="ExternalInput").ap()
    dcos = nc.dram_tensor("d_cosr", [128, NTOK], BF16, kind="ExternalInput").ap()
    dsin = nc.dram_tensor("d_sinr", [128, NTOK], BF16, kind="ExternalInput").ap()
    dp128 = nc.dram_tensor("d_p128t", [128, 128], BF16, kind="ExternalInput").ap()
    dident = nc.dram_tensor("d_ident", [128, 128], BF16, kind="ExternalInput").ap()
    dmask = nc.dram_tensor("d_mask4", [128, 512], BF16, kind="ExternalInput").ap()
    dout = nc.dram_tensor("out", [NTOK, DM], F32, kind="ExternalOutput").ap()
    s_wgT = nc.dram_tensor("s_wgT", [DM, FFI], BF16).ap()
    s_wuT = nc.dram_tensor("s_wuT", [DM, FFI], BF16).ap()
    s_wdT = nc.dram_tensor("s_wdT", [FFI, DM], BF16).ap()
    s_att = {n: nc.dram_tensor(f"s_w{n}", [DM, DM], BF16).ap()
             for n in ("q", "k", "v", "o")}

    es = ExitStack()
    SB = lambda n, s, d: es.enter_context(nc.sbuf_tensor(n, s, d))

    R = SB("R", [128, NT * DM], F32)
    cosr = SB("cosr", [128, NTOK], BF16)
    sinr = SB("sinr", [128, NTOK], BF16)
    p128t = SB("p128t", [128, 128], BF16)
    identb = SB("identb", [128, 128], BF16)
    mask4 = SB("mask4", [128, 512], BF16)
    gammaT = SB("gammaT", [128, DM], F32)
    betaT = SB("betaT", [128, DM], F32)
    wg1b = SB("wg1b", [128, DK * HD], BF16)
    wg2b = SB("wg2b", [64, DM], BF16)
    st_s1 = SB("st_s1", [128, NT], F32)
    st_ss = SB("st_ss", [128, NT], F32)
    st_m = SB("st_m", [128, NT], F32)
    st_v = SB("st_v", [128, NT], F32)
    st_r = SB("st_r", [128, NT], F32)
    st_a = SB("st_a", [128, NT], F32)
    st_t1 = SB("st_t1", [128, NT], F32)
    st_t2 = SB("st_t2", [128, NT], F32)
    wkA = SB("wkA", [128, 2 * 512], F32)
    wkC = SB("wkC", [128, 2 * DM], F32)
    wkD = SB("wkD", [128, 2 * DM], F32)
    btA = SB("btA", [128, 2 * DM], BF16)
    btB = SB("btB", [128, 2 * DM], BF16)

    class Ctr:
        def __init__(self, name, auto=False):
            self.h = es.enter_context(nc.semaphore(name))
            self.n = 0
            self.auto = auto

        def inc(self, instr, k=1):
            if self.auto:
                return self.n
            instr.then_inc(self.h, k)
            self.n += k
            return self.n

    ssu = Ctr("ssu")
    slw = [Ctr("slw0"), Ctr("slw1")]
    sgw = Ctr("sgw")
    sgwd = Ctr("sgwd")
    sga = Ctr("sga")
    sfg = [Ctr("sfg0"), Ctr("sfg1")]
    sfd = [Ctr("sfd0"), Ctr("sfd1")]
    sat = [Ctr("sat0"), Ctr("sat1")]
    sato = Ctr("sato")
    sgo = Ctr("sgo")
    spe = Ctr("spe")
    sdv = Ctr("sdv", auto=True)
    sac = Ctr("sac", auto=True)
    dma_ctrs = [ssu, slw[0], slw[1], sgw, sgwd, sga, sfg[0], sfg[1],
                sfd[0], sfd[1], sat[0], sat[1], sato, sgo]

    class EngProxy:
        _PASS = ("wait_ge", "dma_start")

        def __init__(self, eng, ctr):
            object.__setattr__(self, "_eng", eng)
            object.__setattr__(self, "_ctr", ctr)

        def __getattr__(self, name):
            f = getattr(self._eng, name)
            if name in EngProxy._PASS or not callable(f):
                return f
            eng, ctr = self._eng, self._ctr

            def wrapped(*a, **kw):
                if ctr.n > 0:
                    eng.wait_ge(ctr.h, ctr.n)
                instr = f(*a, **kw)
                instr.then_inc(ctr.h, 1)
                ctr.n += 1
                return instr
            return wrapped

    blk = es.enter_context(nc.Block())
    T, Y, G = blk.tensor, blk.sync, blk.gpsimd
    V = lambda f: blk.vector(lambda v: f(EngProxy(v, sdv)))
    S = lambda f: blk.scalar(lambda s: f(EngProxy(s, sac)))

    wv3 = lambda ap: ap.rearrange("p (b w) -> p b w", w=16)

    def mkquant(qs):
        q_i, q_r, q_c, q_bm, q_sc, q_iv, q_sh = qs

        def bc3(tile, nb):
            return tile[:, 0:nb].unsqueeze(2).broadcast_to([128, nb, 16])

        def emit_quant(v, src_ap, F, dst_ap):
            nb = F // 16
            v.tensor_reduce(q_bm[:, 0:nb], wv3(src_ap), AX.X, ALU.max,
                            apply_absolute_value=True)
            v.tensor_scalar(q_sc[:, 0:nb], q_bm[:, 0:nb], INV6, 1e-12,
                            ALU.mult, ALU.max)
            v.reciprocal(q_iv[:, 0:nb], q_sc[:, 0:nb])
            v.tensor_scalar(q_iv[:, 0:nb], q_iv[:, 0:nb], 2.0, None, ALU.mult)
            v.tensor_scalar(q_sh[:, 0:nb], q_sc[:, 0:nb], 0.5, None, ALU.mult)
            v.tensor_tensor(wv3(src_ap), wv3(src_ap), bc3(q_iv, nb), ALU.mult)
            v.tensor_scalar(q_i[:, 0:F], src_ap.bitcast(I32),
                            float(0x00200000), None, ALU.add)
            v.tensor_scalar(q_i[:, 0:F], q_i[:, 0:F], int(-4194304), None,
                            ALU.bitwise_and)
            v.tensor_scalar(q_c[:, 0:F], q_i[:, 0:F].bitcast(F32), -4.0, 4.0,
                            ALU.max, ALU.min)
            v.tensor_tensor(q_c[:, 0:F], q_i[:, 0:F].bitcast(F32), q_c[:, 0:F],
                            ALU.subtract)
            v.tensor_scalar(src_ap, src_ap, 4.0, -4.0, ALU.min, ALU.max)
            v.tensor_scalar(q_r[:, 0:F], src_ap, 12582912.0, -12582912.0,
                            ALU.add, ALU.add)
            v.tensor_tensor(q_r[:, 0:F], q_r[:, 0:F], q_c[:, 0:F], ALU.add)
            return v.tensor_tensor(wv3(dst_ap), wv3(q_r[:, 0:F]), bc3(q_sh, nb),
                                   ALU.mult)
        return emit_quant

    def mk_qscratch(F, tag, alloc):
        return (alloc(f"q_i{tag}", [128, F], I32),
                alloc(f"q_r{tag}", [128, F], BF16),
                alloc(f"q_c{tag}", [128, F], BF16), alloc(f"q_bm{tag}", [128, F // 16], F32),
                alloc(f"q_sc{tag}", [128, F // 16], F32), alloc(f"q_iv{tag}", [128, F // 16], F32),
                alloc(f"q_sh{tag}", [128, F // 16], F32))

    def Rt(t):
        return R[:, t * DM:(t + 1) * DM]

    def emit_ln_stats(tiles, rwaits):
        c0, c1 = min(tiles), max(tiles) + 1
        assert tiles == list(range(c0, c1))
        sl = lambda tt: tt[:, c0:c1]

        def _(v):
            for ctr, m in rwaits:
                v.wait_ge(ctr.h, m)
            sdv.inc(v.tensor_reduce(
                sl(st_s1),
                R[:, c0 * DM:c1 * DM].rearrange("p (t c) -> p t c", c=DM),
                AX.X, ALU.add))
        V(_)

        def _(s):
            for ctr, m in rwaits:
                s.wait_ge(ctr.h, m)
            s.wait_ge(sdv.h, sdv.n)
            last = None
            for j, t in enumerate(tiles):
                last = s.activation(wkC[:, (j % 2) * DM:(j % 2) * DM + DM],
                                    Rt(t), AF.Square,
                                    accum_out=st_ss[:, t:t + 1])
            sac.inc(last)
        S(_)
        acm = sac.n

        def _(v):
            v.wait_ge(sac.h, acm)
            v.tensor_scalar(sl(st_m), sl(st_s1), RD, None, ALU.mult)
            v.tensor_scalar(sl(st_v), sl(st_ss), RD, None, ALU.mult)
            v.tensor_tensor(sl(st_t1), sl(st_m), sl(st_m), ALU.mult)
            v.tensor_tensor(sl(st_v), sl(st_v), sl(st_t1), ALU.subtract)
            sdv.inc(v.tensor_scalar(sl(st_v), sl(st_v), EPS, None, ALU.add))
        V(_)
        vm = sdv.n

        def _(s):
            s.wait_ge(sdv.h, vm)
            sac.inc(s.activation(sl(st_t2), sl(st_v), AF.Sqrt))
        S(_)
        sqm = sac.n

        def _(v):
            v.wait_ge(sac.h, sqm)
            v.reciprocal(sl(st_r), sl(st_t2))
            sdv.inc(v.scalar_tensor_tensor(sl(st_a), sl(st_m), -1.0, sl(st_r),
                                           ALU.mult, ALU.mult))
        V(_)
        return sdv.n

    def _(g):
        for t in range(NT):
            g.dma_start(out=Rt(t), in_=dx[t * TILE:(t + 1) * TILE, :]).then_inc(ssu.h, 16)
        for dst, src in [(cosr, dcos), (sinr, dsin), (p128t, dp128),
                         (identb, dident), (mask4, dmask), (gammaT, dgam),
                         (betaT, dbet)]:
            g.dma_start(out=dst[:], in_=src).then_inc(ssu.h, 16)
        g.dma_start(out=wkC[:, 0:DK * HD].rearrange("p (k d) -> p k d", d=HD),
                    in_=dwg1.rearrange("(k p) d -> p k d", p=128)).then_inc(ssu.h, 16)
        g.dma_start(out=wkD[0:64, 0:DM], in_=dwg2).then_inc(ssu.h, 16)
        ssu.n += 16 * (NT + 7 + 2)
    G(_)
    setup_m = ssu.n

    def _(s):
        s.wait_ge(ssu.h, setup_m)
        s.copy(wg1b[:], wkC[:, 0:DK * HD])
        sac.inc(s.copy(wg2b[:], wkD[0:64, 0:DM]))
    S(_)
    wgsmall_m = sac.n

    wprep_marks = {}
    def wait_stores(e):
        for ctr in (sgw, sgwd, sga):
            if ctr.n > 0:
                e.wait_ge(ctr.h, ctr.n)

    with ExitStack() as wes:
        WSB = lambda n, s, d: wes.enter_context(nc.sbuf_tensor(n, s, d))
        wldA = WSB("wldA", [128, FFI], F32)
        wldB = WSB("wldB", [128, FFI], F32)
        wqb = WSB("wqb", [128, 2 * FFI], BF16)
        evacW = WSB("evacW", [128, 2048], BF16)
        qsW = mk_qscratch(FFI, "W", WSB)
        quantW = mkquant(qsW)

        with nc.psum_tensor("psW", [128, 2048], BF16) as psW:
            ld_rd = {}
            wqb_rd = {}
            ppx = [0]
            for name, dsrc, ddstT, nchunk, F, PR in [
                ("wg", dwgate, s_wgT, NKI, DM, 2),
                ("wu", dwup, s_wuT, NKI, DM, 2),
                ("wd", dwdown, s_wdT, DK, FFI, 1),
            ]:
                sst = sgw if name in ("wg", "wu") else sgwd
                items = [(j0, min(PR, nchunk - j0))
                         for j0 in range(0, nchunk, PR)]
                for ii, (j0, pr) in enumerate(items):
                    buf = wldA if (ii % 2 == 0) else wldB
                    bk = (name, ii % 2)
                    W = F * pr
                    slot = slw[ii % 2]

                    def _(y, j0=j0, pr=pr, buf=buf, dsrc=dsrc, F=F, W=W):
                        prev = ld_rd.get(id(buf))
                        if prev:
                            y.wait_ge(sdv.h, prev)
                        y.dma_start(
                            out=buf[:, 0:W].rearrange("p (t c) -> p t c", c=F),
                            in_=dsrc[j0 * 128:(j0 + pr) * 128, 0:F]
                            .rearrange("(t p) c -> p t c", p=128)).then_inc(slot.h, 16)
                    Y(_)
                    slot.n += 16
                    ldm = slot.n
                    qb = ii % 2

                    def _(v, buf=buf, W=W, ldm=ldm, slot=slot, qb=qb):
                        v.wait_ge(slot.h, ldm)
                        prev = wqb_rd.get(qb)
                        if prev:
                            v.wait_ge(spe.h, prev)
                        sdv.inc(quantW(v, buf[:, 0:W], W, wqb[:, qb * FFI: qb * FFI + W]))
                    V(_)
                    qm = sdv.n
                    ld_rd[id(buf)] = qm

                    nsb = W // 128
                    for g0 in range(0, nsb, 8):
                        gn = min(8, nsb - g0)
                        pp = ppx[0] % 2
                        ppx[0] += 1
                        jcol = j0 + (g0 * 128) // F
                        c0 = (g0 * 128) % F

                        def _(t, g0=g0, gn=gn, pp=pp, qm=qm, qb=qb):
                            t.wait_ge(sdv.h, qm)
                            t.wait_ge(sac.h, sac.n)
                            last = None
                            for s in range(gn):
                                last = t.transpose(
                                    psW[:, pp * 1024 + s * 128: pp * 1024 + (s + 1) * 128],
                                    wqb[:, qb * FFI + (g0 + s) * 128: qb * FFI + (g0 + s + 1) * 128],
                                    identb[:])
                            spe.inc(last)
                        T(_)
                        tm = spe.n
                        wqb_rd[qb] = tm

                        def _(s, gn=gn, pp=pp, tm=tm):
                            s.wait_ge(spe.h, tm)
                            wait_stores(s)
                            sac.inc(s.copy(evacW[:, pp * 1024: pp * 1024 + gn * 128],
                                           psW[:, pp * 1024: pp * 1024 + gn * 128]))
                        S(_)
                        am = sac.n

                        def _(g, jcol=jcol, c0=c0, g0=g0, gn=gn, pp=pp, am=am,
                              ddstT=ddstT, sst=sst):
                            g.wait_ge(sac.h, am)
                            for s in range(gn):
                                g.dma_start(
                                    out=ddstT[c0 + s * 128: c0 + (s + 1) * 128,
                                              jcol * 128:(jcol + 1) * 128],
                                    in_=evacW[:, pp * 1024 + s * 128: pp * 1024 + (s + 1) * 128],
                                ).then_inc(sst.h, 16)
                        G(_)
                        sst.n += 16 * gn
                wprep_marks[name] = sst.n

            for nm in ("q", "k", "v", "o"):
                dsrc, ddst = {"q": (dwq, s_att["q"]), "k": (dwk, s_att["k"]),
                              "v": (dwv, s_att["v"]), "o": (dwo, s_att["o"])}[nm]
                for k in range(DK):
                    buf = wldA if (k % 2 == 0) else wldB
                    pp = k % 2
                    slot = slw[k % 2]

                    def _(y, k=k, buf=buf, dsrc=dsrc):
                        y.wait_ge(sac.h, sac.n)
                        y.dma_start(out=buf[:, 0:DM],
                                    in_=dsrc[k * 128:(k + 1) * 128, :]).then_inc(slot.h, 16)
                    Y(_)
                    slot.n += 16
                    ldm = slot.n

                    def _(s, buf=buf, ldm=ldm, pp=pp, slot=slot):
                        s.wait_ge(slot.h, ldm)
                        wait_stores(s)
                        sac.inc(s.copy(evacW[:, pp * 1024:(pp + 1) * 1024],
                                       buf[:, 0:DM]))
                    S(_)
                    am = sac.n

                    def _(g, k=k, am=am, pp=pp, ddst=ddst):
                        g.wait_ge(sac.h, am)
                        g.dma_start(out=ddst[k * 128:(k + 1) * 128, :],
                                    in_=evacW[:, pp * 1024:(pp + 1) * 1024]).then_inc(sga.h, 16)
                    G(_)
                    sga.n += 16
            wprep_marks["attn"] = sga.n

    def emit_phase_barrier():
        marks = [(c, c.n) for c in dma_ctrs if c.n > 0]
        marks += [(spe, spe.n), (sdv, sdv.n), (sac, sac.n)]

        def mk(own):
            def _(e):
                for ctr, m in marks:
                    if ctr is not own:
                        e.wait_ge(ctr.h, m)
            return _
        V(mk(None))
        S(mk(None))
        T(mk(None))
        Y(mk(None))
        G(mk(None))

    def emit_ffn_call(ci, rwaits):
        with ExitStack() as fes:
            FSB = lambda n, s, d: fes.enter_context(nc.sbuf_tensor(n, s, d))
            xq_fm = FSB(f"xq{ci}", [128, DK * 512], BF16)
            hq_fm = FSB(f"hq{ci}", [128, NKI * 512], BF16)
            wgbuf = FSB(f"wgb{ci}", [128, 2 * 4096], BF16)
            wubuf = FSB(f"wub{ci}", [128, 2 * 4096], BF16)
            wdbuf = FSB(f"wdb{ci}", [128, 2 * DM], BF16)
            hb = FSB(f"hb{ci}", [128, QT * 1024], F32)
            qsF = mk_qscratch(1024, f"F{ci}", FSB)
            quantF = mkquant(qsF)
            resid_m = [0]
            wkd_m = {}
            btA_m = {}
            wsl_m = {}
            wdl_m = {}

            for iq in range(NQ):
                tiles = [iq * QT + j for j in range(QT)]
                stats_m = emit_ln_stats(tiles, rwaits)

                with nc.psum_tensor(f"pstA{ci}_{iq}", [128, 2048], BF16) as pstA:
                    for j, t in enumerate(tiles):
                        pp = j % 2

                        def _(s, t=t, pp=pp, stats_m=stats_m):
                            if pp in wkd_m:
                                s.wait_ge(sdv.h, wkd_m[pp])
                            s.wait_ge(sdv.h, stats_m)
                            sac.inc(s.activation(wkD[:, pp * DM:(pp + 1) * DM],
                                                 Rt(t), AF.Identity,
                                                 bias=st_a[:, t:t + 1],
                                                 scale=st_r[:, t:t + 1]))
                        S(_)
                        am = sac.n

                        def _(v, pp=pp, am=am):
                            v.wait_ge(sac.h, am)
                            if pp in btA_m:
                                v.wait_ge(spe.h, btA_m[pp])
                            sdv.inc(quantF(v, wkD[:, pp * DM:(pp + 1) * DM], DM,
                                           btA[:, pp * DM:(pp + 1) * DM]))
                        V(_)
                        qm = sdv.n
                        wkd_m[pp] = qm

                        def _(t_, pp=pp, qm=qm):
                            t_.wait_ge(sdv.h, qm)
                            if resid_m[0]:
                                t_.wait_ge(sdv.h, resid_m[0])
                            t_.wait_ge(sac.h, sac.n)
                            last = None
                            for k in range(DK):
                                last = t_.transpose(
                                    pstA[:, pp * 1024 + k * 128: pp * 1024 + (k + 1) * 128],
                                    btA[:, pp * DM + k * 128: pp * DM + (k + 1) * 128],
                                    identb[:])
                            spe.inc(last)
                        T(_)
                        tm = spe.n
                        btA_m[pp] = tm

                        def _(s, j=j, pp=pp, tm=tm):
                            s.wait_ge(spe.h, tm)
                            dst = xq_fm[:].rearrange("p (k q) -> p k q", q=512)[
                                :, :, j * 128:(j + 1) * 128]
                            sac.inc(s.copy(dst,
                                           pstA[:, pp * 1024:(pp + 1) * 1024]
                                           .rearrange("p (k q) -> p k q", q=128)))
                        S(_)
                    xq_m = sac.n

                with nc.psum_tensor(f"psg{ci}_{iq}", [128, 1024], F32) as psg, \
                     nc.psum_tensor(f"psu{ci}_{iq}", [128, 1024], F32) as psu, \
                     nc.psum_tensor(f"pstB{ci}_{iq}", [128, 2048], BF16) as pstB:
                    it = 0
                    qp = 0
                    bmark = {}
                    for pi, (p0, pw) in enumerate(MGP):
                        for sub in range(2):
                            mgi = pi * 2 + sub
                            m0, mw = MGS[mgi]
                            wp = mgi % 2
                            slot = sfg[wp]

                            def _(y, m0=m0, mw=mw, wp=wp, ci=ci, slot=slot):
                                if wp in wsl_m:
                                    y.wait_ge(spe.h, wsl_m[wp])
                                if ci == 0:
                                    y.wait_ge(sgw.h, wprep_marks["wu"])
                                for k in range(DK):
                                    y.dma_start(out=wgbuf[:, wp * 4096 + k * 512: wp * 4096 + k * 512 + mw],
                                                in_=s_wgT[k * 128:(k + 1) * 128, m0:m0 + mw]).then_inc(slot.h, 16)
                                    y.dma_start(out=wubuf[:, wp * 4096 + k * 512: wp * 4096 + k * 512 + mw],
                                                in_=s_wuT[k * 128:(k + 1) * 128, m0:m0 + mw]).then_inc(slot.h, 16)
                            Y(_)
                            slot.n += 16 * 2 * DK
                            wlm = slot.n

                            for j in range(QT):
                                pp = it % 2

                                def _(t_, j=j, pp=pp, mw=mw, wp=wp, wlm=wlm,
                                      xq_m=xq_m, slot=slot, it=it):
                                    t_.wait_ge(slot.h, wlm)
                                    t_.wait_ge(sac.h, xq_m)
                                    if it - 2 in bmark:
                                        t_.wait_ge(sdv.h, bmark[it - 2])
                                    elif resid_m[0]:
                                        t_.wait_ge(sdv.h, resid_m[0])
                                    last = None
                                    for k in range(DK):
                                        lhs = xq_fm[:, k * 512 + j * 128: k * 512 + (j + 1) * 128]
                                        t_.matmul(psg[:, pp * 512: pp * 512 + mw], lhs,
                                                  wgbuf[:, wp * 4096 + k * 512: wp * 4096 + k * 512 + mw],
                                                  start=(k == 0), stop=(k == DK - 1))
                                        last = t_.matmul(psu[:, pp * 512: pp * 512 + mw], lhs,
                                                         wubuf[:, wp * 4096 + k * 512: wp * 4096 + k * 512 + mw],
                                                         start=(k == 0), stop=(k == DK - 1))
                                    spe.inc(last)
                                T(_)
                                mmm = spe.n

                                def _(s, pp=pp, mw=mw, mmm=mmm, it=it):
                                    s.wait_ge(spe.h, mmm)
                                    if it - 2 in bmark:
                                        s.wait_ge(sdv.h, bmark[it - 2])
                                    sac.inc(s.activation(wkA[:, pp * 512: pp * 512 + mw],
                                                         psg[:, pp * 512: pp * 512 + mw],
                                                         AF.Silu))
                                S(_)
                                em = sac.n

                                def _(v, j=j, pp=pp, m0=m0, mw=mw, p0=p0, em=em):
                                    v.wait_ge(sac.h, em)
                                    off = j * 1024 + (m0 - p0)
                                    sdv.inc(v.tensor_tensor(
                                        hb[:, off:off + mw],
                                        wkA[:, pp * 512: pp * 512 + mw],
                                        psu[:, pp * 512: pp * 512 + mw], ALU.mult))
                                V(_)
                                bmark[it] = sdv.n
                                it += 1
                            wsl_m[wp] = spe.n

                        for j in range(QT):
                            qpp = qp % 2
                            qp += 1

                            def _(v, j=j, pw=pw, qpp=qpp):
                                v.wait_ge(spe.h, spe.n)
                                sdv.inc(quantF(v, hb[:, j * 1024: j * 1024 + pw],
                                               pw, btB[:, qpp * DM: qpp * DM + pw]))
                            V(_)
                            hqm = sdv.n

                            def _(t_, pw=pw, qpp=qpp, hqm=hqm):
                                t_.wait_ge(sdv.h, hqm)
                                t_.wait_ge(sac.h, sac.n)
                                last = None
                                for sb in range(pw // 128):
                                    last = t_.transpose(
                                        pstB[:, qpp * 1024 + sb * 128: qpp * 1024 + (sb + 1) * 128],
                                        btB[:, qpp * DM + sb * 128: qpp * DM + (sb + 1) * 128],
                                        identb[:])
                                spe.inc(last)
                            T(_)
                            trm = spe.n

                            def _(s, j=j, p0=p0, pw=pw, qpp=qpp, trm=trm):
                                s.wait_ge(spe.h, trm)
                                kk0 = p0 // 128
                                dst = hq_fm[:].rearrange("p (k q) -> p k q", q=512)[
                                    :, kk0:kk0 + pw // 128, j * 128:(j + 1) * 128]
                                sac.inc(s.copy(dst,
                                               pstB[:, qpp * 1024: qpp * 1024 + pw]
                                               .rearrange("p (k q) -> p k q", q=128)))
                            S(_)
                    hq_m = sac.n

                with nc.psum_tensor(f"psd{ci}_{iq}", [128, 4096], F32) as psd:
                    for kp in range(NKI):
                        wp = kp % 2

                        slot = sfd[wp]

                        def _(y, kp=kp, wp=wp, ci=ci, slot=slot):
                            if wp in wdl_m:
                                y.wait_ge(spe.h, wdl_m[wp])
                            if ci == 0 and kp < 2:
                                y.wait_ge(sgwd.h, wprep_marks["wd"])
                            y.dma_start(out=wdbuf[:, wp * DM:(wp + 1) * DM],
                                        in_=s_wdT[kp * 128:(kp + 1) * 128, :]).then_inc(slot.h, 16)
                        Y(_)
                        slot.n += 16
                        wlm = slot.n

                        def _(t_, kp=kp, wp=wp, wlm=wlm, hq_m=hq_m, slot=slot):
                            t_.wait_ge(slot.h, wlm)
                            t_.wait_ge(sac.h, hq_m)
                            if kp == 0 and resid_m[0]:
                                t_.wait_ge(sdv.h, resid_m[0])
                            last = None
                            for j in range(QT):
                                lhs = hq_fm[:, kp * 512 + j * 128: kp * 512 + (j + 1) * 128]
                                for hf in range(2):
                                    last = t_.matmul(
                                        psd[:, j * 1024 + hf * 512: j * 1024 + (hf + 1) * 512],
                                        lhs,
                                        wdbuf[:, wp * DM + hf * 512: wp * DM + (hf + 1) * 512],
                                        start=(kp == 0), stop=(kp == NKI - 1))
                            spe.inc(last)
                        T(_)
                        wdl_m[wp] = spe.n
                    down_m = spe.n

                    def _(v, tiles=tiles, down_m=down_m):
                        v.wait_ge(spe.h, down_m)
                        t0 = tiles[0]
                        rsl = R[:, t0 * DM:(t0 + QT) * DM]
                        sdv.inc(v.scalar_tensor_tensor(rsl, psd[:], 0.5, rsl,
                                                       ALU.mult, ALU.add))
                    V(_)
                    resid_m[0] = sdv.n
        return sdv.n

    def emit_attn(rwaits):
        with ExitStack() as aes:
            ASB = lambda n, s, d: aes.enter_context(nc.sbuf_tensor(n, s, d))
            xn_fm = ASB("xn_fm", [128, DK * GTOK], BF16)
            qs_b = ASB("qs_b", [128, 2 * GTOK], BF16)
            qp_b = ASB("qp_b", [128, 2 * GTOK], BF16)
            qh = ASB("qh", [64, H * GTOK], BF16)
            kh = ASB("kh", [64, H * GTOK], BF16)
            v_g = ASB("v_g", [128, GT * DM], BF16)
            xg1g = ASB("xg1g", [64, GTOK], BF16)
            sTm = ASB("sTm", [128, 2 * 512], BF16)
            ktm_sb = ASB("ktm_sb", [128, DM], BF16)
            S32 = ASB("S32", [64, H * HD], F32)
            Sb = ASB("Sb", [64, H * HD], BF16)
            wbufA = ASB("wbufA", [128, 3 * 8192], BF16)
            y_fm = ASB("y_fm", [128, DM], BF16)
            gate_sb = ASB("gate_sb", [128, GT * DM], F32)

            stats_m = emit_ln_stats(list(range(NT)), rwaits)

            def _(v):
                v.memset(S32[:], 0.0)
                sdv.inc(v.memset(Sb[:], 0.0))
            V(_)

            with nc.psum_tensor("psA", [128, 2048], F32) as psA, \
                 nc.psum_tensor("psB", [128, 1024], F32) as psB, \
                 nc.psum_tensor("psC", [128, 512], F32) as psC, \
                 nc.psum_tensor("psD", [128, 1024], BF16) as psD:

                rs = {"psC_v": 0, "psD_v": 0, "psD_s": 0, "btA": 0,
                      "psB_s": 0, "psB_v": 0, "wkC_s": 0}
                sTm_rd = {0: 0, 1: 0}
                psA_m = {0: 0, 1: 0}
                gate_rd = [0]
                wsl_rd = {0: 0, 1: 0}
                cs = {}
                gctx = {}

                def sections(g):
                    tiles = [g * GT + j for j in range(GT)]
                    gs = slice(g * GTOK, (g + 1) * GTOK)
                    gp = g % 2
                    ctx = {}

                    for j, t in enumerate(tiles):
                        pp = j % 2

                        def _(s, t=t, pp=pp):
                            s.wait_ge(sdv.h, stats_m)
                            s.wait_ge(spe.h, rs["btA"])
                            sac.inc(s.activation(btA[:, pp * DM:(pp + 1) * DM],
                                                 Rt(t), AF.Identity,
                                                 bias=st_a[:, t:t + 1],
                                                 scale=st_r[:, t:t + 1]))
                        S(_)
                        am = sac.n

                        def _(t_, pp=pp, am=am):
                            t_.wait_ge(sac.h, am)
                            t_.wait_ge(sac.h, rs["psD_s"])
                            t_.wait_ge(sdv.h, rs["psD_v"])
                            last = None
                            for k in range(DK):
                                last = t_.transpose(
                                    psD[:, k * 128:(k + 1) * 128],
                                    btA[:, pp * DM + k * 128: pp * DM + (k + 1) * 128],
                                    identb[:])
                            spe.inc(last)
                        T(_)
                        tm = spe.n
                        rs["btA"] = tm

                        def _(s, j=j, tm=tm):
                            s.wait_ge(spe.h, tm)
                            dst = xn_fm[:].rearrange("p (k q) -> p k q", q=GTOK)[
                                :, :, j * 128:(j + 1) * 128]
                            sac.inc(s.copy(dst, psD[:].rearrange("p (k q) -> p k q", q=128)))
                        S(_)
                        rs["psD_s"] = sac.n
                    xn_m = sac.n
                    ctx["xn_m"] = xn_m

                    wmarks = {}
                    for nm, wp in (("q", 0), ("k", 1)):
                        def _(y, nm=nm, wp=wp, g=g, slot=sat[wp]):
                            y.wait_ge(spe.h, wsl_rd[wp])
                            if g == 0:
                                y.wait_ge(sga.h, wprep_marks["attn"])
                            for k in range(DK):
                                y.dma_start(out=wbufA[:, wp * 8192 + k * 1024: wp * 8192 + (k + 1) * 1024],
                                            in_=s_att[nm][k * 128:(k + 1) * 128, :]).then_inc(slot.h, 16)
                        Y(_)
                        sat[wp].n += 16 * DK
                        wmarks[nm] = (wp, sat[wp].n)

                    projpe = {}
                    for nm, htile in (("q", qh), ("k", kh)):
                        wp, wlm = wmarks[nm]
                        for m in range(DK):
                            pp = m % 2

                            def _(t_, m=m, pp=pp, wp=wp, wlm=wlm, xn_m=xn_m):
                                t_.wait_ge(sat[wp].h, wlm)
                                t_.wait_ge(sac.h, xn_m)
                                t_.wait_ge(sac.h, sac.n)
                                t_.wait_ge(sdv.h, psA_m[0])
                                last = None
                                for k in range(DK):
                                    last = t_.matmul(
                                        psA[:, pp * 512: pp * 512 + GTOK],
                                        wbufA[:, wp * 8192 + k * 1024 + m * 128: wp * 8192 + k * 1024 + (m + 1) * 128],
                                        xn_fm[:, k * GTOK:(k + 1) * GTOK],
                                        start=(k == 0), stop=(k == DK - 1))
                                spe.inc(last)
                            T(_)
                            pm = spe.n

                            def _(s, pp=pp, pm=pm):
                                s.wait_ge(spe.h, pm)
                                s.wait_ge(spe.h, spe.n)
                                sac.inc(s.activation(qs_b[:, pp * GTOK:(pp + 1) * GTOK],
                                                     psA[:, pp * 512: pp * 512 + GTOK],
                                                     AF.Silu))
                            S(_)
                            sm = sac.n

                            def _(t_, pp=pp, sm=sm):
                                t_.wait_ge(sac.h, sm)
                                t_.wait_ge(sac.h, sac.n)
                                t_.wait_ge(sdv.h, rs["psB_v"])
                                spe.inc(t_.matmul(psB[:, pp * 512: pp * 512 + GTOK],
                                                  p128t[:],
                                                  qs_b[:, pp * GTOK:(pp + 1) * GTOK],
                                                  start=True, stop=True))
                            T(_)
                            qpm = spe.n

                            def _(s, pp=pp, qpm=qpm):
                                s.wait_ge(spe.h, qpm)
                                s.wait_ge(sdv.h, sdv.n)
                                sac.inc(s.copy(qp_b[:, pp * GTOK:(pp + 1) * GTOK],
                                               psB[:, pp * 512: pp * 512 + GTOK]))
                            S(_)
                            qem = sac.n
                            rs["psB_s"] = qem

                            def _(v, htile=htile, m=m, pp=pp, qem=qem, gs=gs):
                                v.wait_ge(sac.h, qem)
                                v.wait_ge(spe.h, spe.n)
                                v.tensor_tensor(btB[:, 0:GTOK],
                                                qs_b[:, pp * GTOK:(pp + 1) * GTOK],
                                                cosr[:, gs], ALU.mult)
                                v.tensor_tensor(btB[:, GTOK:2 * GTOK],
                                                qp_b[:, pp * GTOK:(pp + 1) * GTOK],
                                                sinr[:, gs], ALU.mult)
                                h0, h1 = 2 * m, 2 * m + 1
                                v.tensor_tensor(htile[:, h0 * GTOK:(h0 + 1) * GTOK],
                                                btB[0:64, 0:GTOK],
                                                btB[0:64, GTOK:2 * GTOK], ALU.add)
                                sdv.inc(v.tensor_tensor(htile[:, h1 * GTOK:(h1 + 1) * GTOK],
                                                        btB[64:128, 0:GTOK],
                                                        btB[64:128, GTOK:2 * GTOK],
                                                        ALU.add))
                            V(_)
                        projpe[nm] = spe.n
                    ctx["qk_m"] = sdv.n
                    wsl_rd[1] = projpe["k"]

                    def _(y, g=g, slot=sat[0]):
                        y.wait_ge(spe.h, projpe["q"])
                        for k in range(DK):
                            y.dma_start(out=wbufA[:, k * 1024:(k + 1) * 1024],
                                        in_=s_att["v"][k * 128:(k + 1) * 128, :]).then_inc(slot.h, 16)
                    Y(_)
                    sat[0].n += 16 * DK
                    vwlm = sat[0].n

                    for c in range(GT):
                        def _(t_, c=c, vwlm=vwlm, xn_m=xn_m):
                            t_.wait_ge(sat[0].h, vwlm)
                            t_.wait_ge(sac.h, xn_m)
                            t_.wait_ge(sac.h, sac.n)
                            t_.wait_ge(sdv.h, psA_m[0])
                            last = None
                            for k in range(DK):
                                lhs = xn_fm[:, k * GTOK + c * 128: k * GTOK + (c + 1) * 128]
                                for hf in range(2):
                                    last = t_.matmul(
                                        psA[:, hf * 512:(hf + 1) * 512], lhs,
                                        wbufA[:, k * 1024 + hf * 512: k * 1024 + (hf + 1) * 512],
                                        start=(k == 0), stop=(k == DK - 1))
                            spe.inc(last)
                        T(_)
                        vm = spe.n

                        def _(s, c=c, vm=vm):
                            s.wait_ge(spe.h, vm)
                            sac.inc(s.copy(v_g[:, c * DM:(c + 1) * DM], psA[:, 0:1024]))
                        S(_)
                    ctx["v_m"] = sac.n
                    wsl_rd[0] = spe.n

                    def _(t_, xn_m=xn_m):
                        t_.wait_ge(sac.h, xn_m)
                        t_.wait_ge(sac.h, wgsmall_m)
                        t_.wait_ge(sac.h, sac.n)
                        t_.wait_ge(sdv.h, rs["psC_v"])
                        last = None
                        for k in range(DK):
                            last = t_.matmul(psC[0:64, 0:GTOK],
                                             wg1b[:, k * HD:(k + 1) * HD],
                                             xn_fm[:, k * GTOK:(k + 1) * GTOK],
                                             start=(k == 0), stop=(k == DK - 1))
                        spe.inc(last)
                    T(_)
                    x1m = spe.n

                    def _(s, x1m=x1m):
                        s.wait_ge(spe.h, x1m)
                        s.wait_ge(spe.h, spe.n)
                        sac.inc(s.copy(xg1g[:, 0:GTOK], psC[0:64, 0:GTOK]))
                    S(_)
                    ctx["xg1_m"] = sac.n
                    gctx[g] = ctx

                def emit_gate(g):
                    ctx = gctx[g]
                    for cl in range(GT):
                        def _(t_, cl=cl, xg1_m=ctx["xg1_m"]):
                            t_.wait_ge(sac.h, xg1_m)
                            t_.wait_ge(sac.h, wgsmall_m)
                            t_.wait_ge(sdv.h, rs["psB_v"])
                            t_.wait_ge(sac.h, rs["psB_s"])
                            last = None
                            for hf in range(2):
                                last = t_.matmul(psB[:, hf * 512:(hf + 1) * 512],
                                                 xg1g[:, cl * 128:(cl + 1) * 128],
                                                 wg2b[:, hf * 512:(hf + 1) * 512],
                                                 start=True, stop=True)
                            spe.inc(last)
                        T(_)
                        gz_m = spe.n

                        def _(s, cl=cl, gz_m=gz_m):
                            s.wait_ge(spe.h, gz_m)
                            s.wait_ge(sdv.h, gate_rd[0])
                            sac.inc(s.activation(
                                gate_sb[:, cl * DM:(cl + 1) * DM],
                                psB[:], AF.Tanh, scale=0.5))
                        S(_)
                        tg_m = sac.n
                        rs["psB_s"] = tg_m

                        def _(v, cl=cl, tg_m=tg_m):
                            v.wait_ge(sac.h, tg_m)
                            gsl = gate_sb[:, cl * DM:(cl + 1) * DM]
                            sdv.inc(v.tensor_scalar(gsl, gsl, 0.5, 0.5,
                                                    ALU.mult, ALU.add))
                        V(_)

                def o_load(g):
                    def _(y, g=g):
                        if g > 0:
                            y.wait_ge(spe.h, cs[2 * g - 1]["wo_m"])
                        for k in range(DK):
                            y.dma_start(out=wbufA[:, 2 * 8192 + k * 1024: 2 * 8192 + (k + 1) * 1024],
                                        in_=s_att["o"][k * 128:(k + 1) * 128, :]).then_inc(sato.h, 16)
                    Y(_)
                    sato.n += 16 * DK
                    gctx[g]["o_m"] = sato.n

                def mixing(c):
                    g, cl = c // GT, c % GT
                    ctx = gctx[g]
                    pc = c % 2

                    out_m = None
                    for b in range(4):
                        pp = b % 2

                        def _(t_, b=b, cl=cl, ctx=ctx):
                            t_.wait_ge(sdv.h, ctx["qk_m"])
                            t_.wait_ge(sac.h, ctx["xg1_m"])
                            t_.wait_ge(sdv.h, rs["psC_v"])
                            last = None
                            for hh in range(4):
                                hc = b * 4 + hh
                                last = t_.matmul(
                                    psC[:, hh * 128:(hh + 1) * 128],
                                    kh[:, hc * GTOK + cl * 128: hc * GTOK + (cl + 1) * 128],
                                    qh[:, hc * GTOK + cl * 128: hc * GTOK + (cl + 1) * 128],
                                    start=True, stop=True)
                            spe.inc(last)
                        T(_)
                        stm = spe.n

                        def _(v, pp=pp, stm=stm):
                            v.wait_ge(spe.h, stm)
                            v.wait_ge(spe.h, sTm_rd[pp])
                            sdv.inc(v.tensor_tensor(sTm[:, pp * 512:(pp + 1) * 512],
                                                    psC[:], mask4[:], ALU.mult))
                        V(_)
                        evm = sdv.n
                        rs["psC_v"] = evm

                        def _(t_, b=b, cl=cl, pp=pp, pc=pc, evm=evm, ctx=ctx, c=c):
                            t_.wait_ge(sdv.h, evm)
                            t_.wait_ge(sac.h, ctx["v_m"])
                            if c >= 1:
                                t_.wait_ge(sdv.h, cs[c - 1]["supd"])
                            t_.wait_ge(sdv.h, psA_m[pc])
                            last = None
                            for hh in range(4):
                                hc = b * 4 + hh
                                t_.matmul(psA[:, pc * 1024 + hc * 64: pc * 1024 + (hc + 1) * 64],
                                          sTm[:, pp * 512 + hh * 128: pp * 512 + (hh + 1) * 128],
                                          v_g[:, cl * DM + hc * 64: cl * DM + (hc + 1) * 64],
                                          start=True, stop=False)
                                last = t_.matmul(psA[:, pc * 1024 + hc * 64: pc * 1024 + (hc + 1) * 64],
                                                 qh[:, hc * GTOK + cl * 128: hc * GTOK + (cl + 1) * 128],
                                                 Sb[:, hc * 64:(hc + 1) * 64],
                                                 start=False, stop=True)
                            spe.inc(last)
                        T(_)
                        sTm_rd[pp] = spe.n
                    out_m = spe.n

                    def _(t_, cl=cl, ctx=ctx):
                        t_.wait_ge(sdv.h, ctx["qk_m"])
                        t_.wait_ge(sdv.h, rs["psD_v"])
                        t_.wait_ge(sac.h, rs["psD_s"])
                        last = None
                        for hc in range(H):
                            last = t_.transpose(
                                psD[:, hc * 64:(hc + 1) * 64],
                                kh[:, hc * GTOK + cl * 128: hc * GTOK + (cl + 1) * 128],
                                identb[0:64, 0:64])
                        spe.inc(last)
                    T(_)
                    ktm_m = spe.n

                    def _(v, ktm_m=ktm_m):
                        v.wait_ge(spe.h, ktm_m)
                        sdv.inc(v.tensor_copy(ktm_sb[:], psD[:]))
                    V(_)
                    ktme_m = sdv.n
                    rs["psD_v"] = ktme_m

                    def _(t_, cl=cl, ktme_m=ktme_m, ctx=ctx):
                        t_.wait_ge(sdv.h, ktme_m)
                        t_.wait_ge(sac.h, ctx["v_m"])
                        t_.wait_ge(sac.h, rs["psB_s"])
                        t_.wait_ge(sdv.h, rs["psB_v"])
                        last = None
                        for hc in range(H):
                            last = t_.matmul(psB[0:64, hc * 64:(hc + 1) * 64],
                                             ktm_sb[:, hc * 64:(hc + 1) * 64],
                                             v_g[:, cl * DM + hc * 64: cl * DM + (hc + 1) * 64],
                                             start=True, stop=True)
                        spe.inc(last)
                    T(_)
                    sup_m = spe.n

                    def _(v, sup_m=sup_m, out_m=out_m):
                        v.wait_ge(spe.h, sup_m)
                        v.wait_ge(spe.h, out_m)
                        v.tensor_tensor(S32[:], S32[:], psB[0:64, :], ALU.add)
                        sdv.inc(v.tensor_copy(Sb[:], S32[:]))
                    V(_)
                    supd_m = sdv.n
                    rs["psB_v"] = supd_m
                    cs[c] = {"supd": supd_m, "out_m": out_m, "pc": pc,
                             "g": g, "cl": cl}

                def epilogue(c):
                    st = cs[c]
                    g, cl, pc = st["g"], st["cl"], st["pc"]
                    ctx = gctx[g]
                    gsl = gate_sb[:, cl * DM:(cl + 1) * DM]

                    def _(v, st=st, gsl=gsl, pc=pc):
                        v.wait_ge(spe.h, st["out_m"])
                        sl = lambda tt: tt[:, 0:1]
                        v.tensor_tensor(wkC[:, DM:2 * DM], gsl,
                                        psA[:, pc * 1024:(pc + 1) * 1024], ALU.mult)
                        v.tensor_reduce(st_s1[:, 0:1],
                                        wkC[:, DM:2 * DM], AX.X, ALU.add)
                        v.tensor_tensor(wkC[:, 0:DM], wkC[:, DM:2 * DM],
                                        wkC[:, DM:2 * DM], ALU.mult)
                        v.tensor_reduce(st_ss[:, 0:1], wkC[:, 0:DM], AX.X, ALU.add)
                        v.tensor_scalar(sl(st_m), sl(st_s1), RD, None, ALU.mult)
                        v.tensor_scalar(sl(st_v), sl(st_ss), RD, None, ALU.mult)
                        v.tensor_tensor(sl(st_t1), sl(st_m), sl(st_m), ALU.mult)
                        v.tensor_tensor(sl(st_v), sl(st_v), sl(st_t1), ALU.subtract)
                        sdv.inc(v.tensor_scalar(sl(st_v), sl(st_v), EPS, None,
                                                ALU.add))
                    V(_)
                    lnv_m = sdv.n
                    gate_rd[0] = lnv_m

                    def _(s, lnv_m=lnv_m):
                        s.wait_ge(sdv.h, lnv_m)
                        sac.inc(s.activation(st_t2[:, 0:1], st_v[:, 0:1], AF.Sqrt))
                    S(_)
                    sqr_m = sac.n

                    def _(v, sqr_m=sqr_m):
                        v.wait_ge(sac.h, sqr_m)
                        sl = lambda tt: tt[:, 0:1]
                        v.reciprocal(sl(st_r), sl(st_t2))
                        v.scalar_tensor_tensor(sl(st_a), sl(st_m), -1.0,
                                               sl(st_r), ALU.mult, ALU.mult)
                        rbc = st_r[:, 0:1].broadcast_to([128, DM])
                        abc = st_a[:, 0:1].broadcast_to([128, DM])
                        v.tensor_tensor(wkD[:, 0:DM], wkC[:, DM:2 * DM], rbc,
                                        ALU.mult)
                        v.tensor_tensor(wkD[:, 0:DM], wkD[:, 0:DM], abc, ALU.add)
                        v.wait_ge(spe.h, rs["btA"])
                        v.tensor_tensor(wkD[:, DM:2 * DM], wkD[:, 0:DM],
                                        gammaT[:], ALU.mult)
                        sdv.inc(v.tensor_tensor(btA[:, 0:DM], wkD[:, DM:2 * DM],
                                                betaT[:], ALU.add))
                    V(_)
                    y2_m = sdv.n

                    def _(t_, y2_m=y2_m):
                        t_.wait_ge(sdv.h, y2_m)
                        t_.wait_ge(sdv.h, rs["psD_v"])
                        t_.wait_ge(sac.h, rs["psD_s"])
                        last = None
                        for k in range(DK):
                            last = t_.transpose(psD[:, k * 128:(k + 1) * 128],
                                                btA[:, k * 128:(k + 1) * 128],
                                                identb[:])
                        spe.inc(last)
                    T(_)
                    ytr_m = spe.n
                    rs["btA"] = ytr_m

                    def _(v, ytr_m=ytr_m):
                        v.wait_ge(spe.h, ytr_m)
                        sdv.inc(v.tensor_copy(y_fm[:], psD[:]))
                    V(_)
                    yfm_m = sdv.n
                    rs["psD_v"] = yfm_m

                    def _(t_, yfm_m=yfm_m, lnv_m=lnv_m, pc=pc, o_m=ctx["o_m"]):
                        t_.wait_ge(sdv.h, yfm_m)
                        t_.wait_ge(sato.h, o_m)
                        t_.wait_ge(sdv.h, lnv_m)
                        last = None
                        for k in range(DK):
                            for hf in range(2):
                                last = t_.matmul(
                                    psA[:, pc * 1024 + hf * 512: pc * 1024 + (hf + 1) * 512],
                                    y_fm[:, k * 128:(k + 1) * 128],
                                    wbufA[:, 2 * 8192 + k * 1024 + hf * 512: 2 * 8192 + k * 1024 + (hf + 1) * 512],
                                    start=(k == 0), stop=(k == DK - 1))
                        spe.inc(last)
                    T(_)
                    wo_m = spe.n
                    st["wo_m"] = wo_m

                    def _(v, c=c, wo_m=wo_m, pc=pc):
                        v.wait_ge(spe.h, wo_m)
                        sdv.inc(v.scalar_tensor_tensor(Rt(c), psA[:, pc * 1024:(pc + 1) * 1024],
                                                       1.0, Rt(c), ALU.mult, ALU.add))
                    V(_)
                    psA_m[pc] = sdv.n

                pending = None
                for g in range(NGRP):
                    sections(g)
                    mixing(2 * g)
                    if pending is not None:
                        epilogue(pending)
                    o_load(g)
                    emit_gate(g)
                    mixing(2 * g + 1)
                    epilogue(2 * g)
                    pending = 2 * g + 1
                epilogue(pending)
        return sdv.n

    emit_phase_barrier()
    r_mark = [(ssu, setup_m)]
    if "ffn1" in PHASES:
        r_mark = [(sdv, emit_ffn_call(0, r_mark))]
        emit_phase_barrier()
    if "attn" in PHASES:
        r_mark = [(sdv, emit_attn(r_mark))]
        emit_phase_barrier()
    if "ffn2" in PHASES:
        r_mark = [(sdv, emit_ffn_call(1, r_mark))]

    def _(g):
        for ctr, m in r_mark:
            g.wait_ge(ctr.h, m)
        for t in range(NT):
            g.dma_start(out=dout[t * TILE:(t + 1) * TILE, :], in_=Rt(t)).then_inc(sgo.h, 16)
        sgo.n += 16 * NT
        g.wait_ge(sgo.h, sgo.n)
    G(_)

    es.close()
    return nc


def _host_consts():
    BASE = 10000.0
    inv_freq = BASE ** (-(np.arange(32, dtype=np.float64) * 2.0) / 64.0)
    pos = np.arange(NTOK, dtype=np.float64)
    ang = pos[:, None] * inv_freq[None, :]
    cos32 = np.cos(ang).T
    sin32 = np.sin(ang).T
    cosr = np.tile(cos32, (4, 1)).astype(ml_dtypes.bfloat16)
    sinr = np.tile(sin32, (4, 1)).astype(ml_dtypes.bfloat16)
    P64 = np.zeros((64, 64), np.float32)
    for i in range(32):
        P64[i, i + 32] = -1.0
        P64[i + 32, i] = 1.0
    P128 = np.zeros((128, 128), np.float32)
    P128[0:64, 0:64] = P64
    P128[64:128, 64:128] = P64
    p128t = P128.T.astype(ml_dtypes.bfloat16)
    ident = np.eye(128, dtype=np.float32).astype(ml_dtypes.bfloat16)
    mask = np.triu(np.ones((128, 128), np.float32))
    mask4 = np.tile(mask, (1, 4)).astype(ml_dtypes.bfloat16)
    return cosr, sinr, p128t, ident, mask4


def kernel(L0_attn_x, L0_attn_Wq, L0_attn_Wk, L0_attn_Wv, L0_attn_Wo,
           L0_attn_Wg1, L0_attn_Wg2, L0_attn_gamma, L0_attn_beta,
           L0_ffn_w_gate, L0_ffn_w_up, L0_ffn_w_down):
    x = np.ascontiguousarray(np.asarray(L0_attn_x, dtype=np.float32))
    B = x.shape[0]
    cosr, sinr, p128t, ident, mask4 = _host_consts()
    gammaT = np.ascontiguousarray(
        np.broadcast_to(np.asarray(L0_attn_gamma, np.float32), (128, DM)))
    betaT = np.ascontiguousarray(
        np.broadcast_to(np.asarray(L0_attn_beta, np.float32), (128, DM)))
    common = {
        "wq": np.ascontiguousarray(np.asarray(L0_attn_Wq, np.float32)),
        "wk": np.ascontiguousarray(np.asarray(L0_attn_Wk, np.float32)),
        "wv": np.ascontiguousarray(np.asarray(L0_attn_Wv, np.float32)),
        "wo": np.ascontiguousarray(np.asarray(L0_attn_Wo, np.float32)),
        "wg1": np.ascontiguousarray(np.asarray(L0_attn_Wg1, np.float32)),
        "wg2": np.ascontiguousarray(np.asarray(L0_attn_Wg2, np.float32)),
        "d_gammaT": gammaT, "d_betaT": betaT,
        "wgate": np.ascontiguousarray(np.asarray(L0_ffn_w_gate, np.float32)),
        "wup": np.ascontiguousarray(np.asarray(L0_ffn_w_up, np.float32)),
        "wdown": np.ascontiguousarray(np.asarray(L0_ffn_w_down, np.float32)),
        "d_cosr": cosr, "d_sinr": sinr, "d_p128t": p128t, "d_ident": ident,
        "d_mask4": mask4,
    }
    nc = build_program()
    in_maps = [dict(common, x=np.ascontiguousarray(x[b])) for b in range(B)]
    res = run_bass_kernel_spmd(nc, in_maps, core_ids=list(range(B)))
    return np.stack([res.results[b]["out"] for b in range(B)], axis=0)


# version 08
# speedup vs baseline: 1.5367x; 1.0014x over previous
import numpy as np
import ml_dtypes
from contextlib import ExitStack

import concourse.bass as bass
import concourse.mybir as mybir
from concourse.bass_utils import run_bass_kernel_spmd

F32 = mybir.dt.float32
BF16 = mybir.dt.bfloat16
I32 = mybir.dt.int32
AF = mybir.ActivationFunctionType
ALU = mybir.AluOpType
AX = mybir.AxisListType

NTOK, DM, TILE = 2048, 1024, 128
NT = NTOK // TILE
DK = DM // TILE
FFI = 2816
NKI = FFI // TILE
H, HD = 16, 64
MGS = [(0, 512), (512, 512), (1024, 512), (1536, 512), (2048, 512), (2560, 256)]
MGP = [(0, 1024), (1024, 1024), (2048, 768)]
QT = 4
NQ = NT // QT
GT = 2
NGRP = NT // GT
GTOK = GT * TILE

INV6 = float(np.float32(1.0) / np.float32(6.0))
ANDM = float(np.int64(0xFFC00000) - (1 << 32))
RD = float(np.float32(1.0) / np.float32(DM))
EPS = 1e-5

PHASES = ("ffn1", "attn", "ffn2")


def build_program():
    nc = bass.Bass("TRN2", target_bir_lowering=False)

    dx = nc.dram_tensor("x", [NTOK, DM], F32, kind="ExternalInput").ap()
    dwq = nc.dram_tensor("wq", [DM, DM], F32, kind="ExternalInput").ap()
    dwk = nc.dram_tensor("wk", [DM, DM], F32, kind="ExternalInput").ap()
    dwv = nc.dram_tensor("wv", [DM, DM], F32, kind="ExternalInput").ap()
    dwo = nc.dram_tensor("wo", [DM, DM], F32, kind="ExternalInput").ap()
    dwg1 = nc.dram_tensor("wg1", [DM, HD], F32, kind="ExternalInput").ap()
    dwg2 = nc.dram_tensor("wg2", [HD, DM], F32, kind="ExternalInput").ap()
    dgam = nc.dram_tensor("d_gammaT", [128, DM], F32, kind="ExternalInput").ap()
    dbet = nc.dram_tensor("d_betaT", [128, DM], F32, kind="ExternalInput").ap()
    dwgate = nc.dram_tensor("wgate", [FFI, DM], F32, kind="ExternalInput").ap()
    dwup = nc.dram_tensor("wup", [FFI, DM], F32, kind="ExternalInput").ap()
    dwdown = nc.dram_tensor("wdown", [DM, FFI], F32, kind="ExternalInput").ap()
    dcos = nc.dram_tensor("d_cosr", [128, NTOK], BF16, kind="ExternalInput").ap()
    dsin = nc.dram_tensor("d_sinr", [128, NTOK], BF16, kind="ExternalInput").ap()
    dp128 = nc.dram_tensor("d_p128t", [128, 128], BF16, kind="ExternalInput").ap()
    dident = nc.dram_tensor("d_ident", [128, 128], BF16, kind="ExternalInput").ap()
    dmask = nc.dram_tensor("d_mask4", [128, 512], BF16, kind="ExternalInput").ap()
    dout = nc.dram_tensor("out", [NTOK, DM], F32, kind="ExternalOutput").ap()
    s_wgT = nc.dram_tensor("s_wgT", [DM, FFI], BF16).ap()
    s_wuT = nc.dram_tensor("s_wuT", [DM, FFI], BF16).ap()
    s_wdT = nc.dram_tensor("s_wdT", [FFI, DM], BF16).ap()
    s_att = {n: nc.dram_tensor(f"s_w{n}", [DM, DM], BF16).ap()
             for n in ("q", "k", "v", "o")}

    es = ExitStack()
    SB = lambda n, s, d: es.enter_context(nc.sbuf_tensor(n, s, d))

    R = SB("R", [128, NT * DM], F32)
    cosr = SB("cosr", [128, NTOK], BF16)
    sinr = SB("sinr", [128, NTOK], BF16)
    p128t = SB("p128t", [128, 128], BF16)
    identb = SB("identb", [128, 128], BF16)
    mask4 = SB("mask4", [128, 512], BF16)
    gammaT = SB("gammaT", [128, DM], F32)
    betaT = SB("betaT", [128, DM], F32)
    wg1b = SB("wg1b", [128, DK * HD], BF16)
    wg2b = SB("wg2b", [64, DM], BF16)
    st_s1 = SB("st_s1", [128, NT], F32)
    st_ss = SB("st_ss", [128, NT], F32)
    st_m = SB("st_m", [128, NT], F32)
    st_v = SB("st_v", [128, NT], F32)
    st_r = SB("st_r", [128, NT], F32)
    st_a = SB("st_a", [128, NT], F32)
    st_t1 = SB("st_t1", [128, NT], F32)
    st_t2 = SB("st_t2", [128, NT], F32)
    wkA = SB("wkA", [128, 2 * 512], F32)
    wkC = SB("wkC", [128, 2 * DM], F32)
    wkD = SB("wkD", [128, 2 * DM], F32)
    btA = SB("btA", [128, 2 * DM], BF16)
    btB = SB("btB", [128, 2 * DM], BF16)

    class Ctr:
        def __init__(self, name, auto=False):
            self.h = es.enter_context(nc.semaphore(name))
            self.n = 0
            self.auto = auto

        def inc(self, instr, k=1):
            if self.auto:
                return self.n
            instr.then_inc(self.h, k)
            self.n += k
            return self.n

    ssu = Ctr("ssu")
    slw = [Ctr("slw0"), Ctr("slw1")]
    sgw = Ctr("sgw")
    sgwd = Ctr("sgwd")
    sga = Ctr("sga")
    sfg = [Ctr("sfg0"), Ctr("sfg1")]
    sfd = [Ctr("sfd0"), Ctr("sfd1")]
    sat = [Ctr("sat0"), Ctr("sat1")]
    sato = Ctr("sato")
    sgo = Ctr("sgo")
    spe = Ctr("spe")
    sdv = Ctr("sdv", auto=True)
    sac = Ctr("sac", auto=True)
    dma_ctrs = [ssu, slw[0], slw[1], sgw, sgwd, sga, sfg[0], sfg[1],
                sfd[0], sfd[1], sat[0], sat[1], sato, sgo]

    class EngProxy:
        _PASS = ("wait_ge", "dma_start")

        def __init__(self, eng, ctr):
            object.__setattr__(self, "_eng", eng)
            object.__setattr__(self, "_ctr", ctr)

        def __getattr__(self, name):
            f = getattr(self._eng, name)
            if name in EngProxy._PASS or not callable(f):
                return f
            eng, ctr = self._eng, self._ctr

            def wrapped(*a, **kw):
                if ctr.n > 0:
                    eng.wait_ge(ctr.h, ctr.n)
                instr = f(*a, **kw)
                instr.then_inc(ctr.h, 1)
                ctr.n += 1
                return instr
            return wrapped

    blk = es.enter_context(nc.Block())
    T, Y, G = blk.tensor, blk.sync, blk.gpsimd
    V = lambda f: blk.vector(lambda v: f(EngProxy(v, sdv)))
    S = lambda f: blk.scalar(lambda s: f(EngProxy(s, sac)))

    wv3 = lambda ap: ap.rearrange("p (b w) -> p b w", w=16)

    def mkquant(qs):
        q_i, q_r, q_c, q_bm, q_sc, q_iv, q_sh = qs

        def bc3(tile, nb):
            return tile[:, 0:nb].unsqueeze(2).broadcast_to([128, nb, 16])

        def emit_quant(v, src_ap, F, dst_ap):
            nb = F // 16
            v.tensor_reduce(q_bm[:, 0:nb], wv3(src_ap), AX.X, ALU.max,
                            apply_absolute_value=True)
            v.tensor_scalar(q_sc[:, 0:nb], q_bm[:, 0:nb], INV6, 1e-12,
                            ALU.mult, ALU.max)
            v.reciprocal(q_iv[:, 0:nb], q_sc[:, 0:nb])
            v.tensor_scalar(q_iv[:, 0:nb], q_iv[:, 0:nb], 2.0, None, ALU.mult)
            v.tensor_scalar(q_sh[:, 0:nb], q_sc[:, 0:nb], 0.5, None, ALU.mult)
            v.tensor_tensor(wv3(src_ap), wv3(src_ap), bc3(q_iv, nb), ALU.mult)
            v.tensor_scalar(q_i[:, 0:F], src_ap.bitcast(I32),
                            float(0x00200000), None, ALU.add)
            v.tensor_scalar(q_i[:, 0:F], q_i[:, 0:F], int(-4194304), None,
                            ALU.bitwise_and)
            v.tensor_scalar(q_c[:, 0:F], q_i[:, 0:F].bitcast(F32), -4.0, 4.0,
                            ALU.max, ALU.min)
            v.tensor_tensor(q_c[:, 0:F], q_i[:, 0:F].bitcast(F32), q_c[:, 0:F],
                            ALU.subtract)
            v.tensor_scalar(src_ap, src_ap, 4.0, -4.0, ALU.min, ALU.max)
            v.tensor_scalar(q_r[:, 0:F], src_ap, 12582912.0, -12582912.0,
                            ALU.add, ALU.add)
            v.tensor_tensor(q_r[:, 0:F], q_r[:, 0:F], q_c[:, 0:F], ALU.add)
            return v.tensor_tensor(wv3(dst_ap), wv3(q_r[:, 0:F]), bc3(q_sh, nb),
                                   ALU.mult)
        return emit_quant

    def mk_qscratch(F, tag, alloc):
        return (alloc(f"q_i{tag}", [128, F], I32),
                alloc(f"q_r{tag}", [128, F], BF16),
                alloc(f"q_c{tag}", [128, F], BF16), alloc(f"q_bm{tag}", [128, F // 16], F32),
                alloc(f"q_sc{tag}", [128, F // 16], F32), alloc(f"q_iv{tag}", [128, F // 16], F32),
                alloc(f"q_sh{tag}", [128, F // 16], F32))

    def Rt(t):
        return R[:, t * DM:(t + 1) * DM]

    def emit_ln_stats(tiles, rwaits):
        c0, c1 = min(tiles), max(tiles) + 1
        assert tiles == list(range(c0, c1))
        sl = lambda tt: tt[:, c0:c1]

        def _(v):
            for ctr, m in rwaits:
                v.wait_ge(ctr.h, m)
            sdv.inc(v.tensor_reduce(
                sl(st_s1),
                R[:, c0 * DM:c1 * DM].rearrange("p (t c) -> p t c", c=DM),
                AX.X, ALU.add))
        V(_)

        def _(s):
            for ctr, m in rwaits:
                s.wait_ge(ctr.h, m)
            s.wait_ge(sdv.h, sdv.n)
            last = None
            for j, t in enumerate(tiles):
                last = s.activation(wkC[:, (j % 2) * DM:(j % 2) * DM + DM],
                                    Rt(t), AF.Square,
                                    accum_out=st_ss[:, t:t + 1])
            sac.inc(last)
        S(_)
        acm = sac.n

        def _(v):
            v.wait_ge(sac.h, acm)
            v.tensor_scalar(sl(st_m), sl(st_s1), RD, None, ALU.mult)
            v.tensor_scalar(sl(st_v), sl(st_ss), RD, None, ALU.mult)
            v.tensor_tensor(sl(st_t1), sl(st_m), sl(st_m), ALU.mult)
            v.tensor_tensor(sl(st_v), sl(st_v), sl(st_t1), ALU.subtract)
            sdv.inc(v.tensor_scalar(sl(st_v), sl(st_v), EPS, None, ALU.add))
        V(_)
        vm = sdv.n

        def _(s):
            s.wait_ge(sdv.h, vm)
            sac.inc(s.activation(sl(st_t2), sl(st_v), AF.Sqrt))
        S(_)
        sqm = sac.n

        def _(v):
            v.wait_ge(sac.h, sqm)
            v.reciprocal(sl(st_r), sl(st_t2))
            sdv.inc(v.scalar_tensor_tensor(sl(st_a), sl(st_m), -1.0, sl(st_r),
                                           ALU.mult, ALU.mult))
        V(_)
        return sdv.n

    def _(g):
        for t in range(NT):
            g.dma_start(out=Rt(t), in_=dx[t * TILE:(t + 1) * TILE, :]).then_inc(ssu.h, 16)
        for dst, src in [(cosr, dcos), (sinr, dsin), (p128t, dp128),
                         (identb, dident), (mask4, dmask), (gammaT, dgam),
                         (betaT, dbet)]:
            g.dma_start(out=dst[:], in_=src).then_inc(ssu.h, 16)
        g.dma_start(out=wkC[:, 0:DK * HD].rearrange("p (k d) -> p k d", d=HD),
                    in_=dwg1.rearrange("(k p) d -> p k d", p=128)).then_inc(ssu.h, 16)
        g.dma_start(out=wkD[0:64, 0:DM], in_=dwg2).then_inc(ssu.h, 16)
        ssu.n += 16 * (NT + 7 + 2)
    G(_)
    setup_m = ssu.n

    def _(s):
        s.wait_ge(ssu.h, setup_m)
        s.copy(wg1b[:], wkC[:, 0:DK * HD])
        sac.inc(s.copy(wg2b[:], wkD[0:64, 0:DM]))
    S(_)
    wgsmall_m = sac.n

    wprep_marks = {}
    def wait_stores(e):
        for ctr in (sgw, sgwd, sga):
            if ctr.n > 0:
                e.wait_ge(ctr.h, ctr.n)

    with ExitStack() as wes:
        WSB = lambda n, s, d: wes.enter_context(nc.sbuf_tensor(n, s, d))
        wldA = WSB("wldA", [128, FFI], F32)
        wldB = WSB("wldB", [128, FFI], F32)
        wqb = WSB("wqb", [128, 2 * FFI], BF16)
        evacW = WSB("evacW", [128, 2048], BF16)
        qsW = mk_qscratch(FFI, "W", WSB)
        quantW = mkquant(qsW)

        with nc.psum_tensor("psW", [128, 2048], BF16) as psW:
            ld_rd = {}
            wqb_rd = {}
            ppx = [0]
            for name, dsrc, ddstT, nchunk, F, PR in [
                ("wg", dwgate, s_wgT, NKI, DM, 2),
                ("wu", dwup, s_wuT, NKI, DM, 2),
                ("wd", dwdown, s_wdT, DK, FFI, 1),
            ]:
                sst = sgw if name in ("wg", "wu") else sgwd
                items = [(j0, min(PR, nchunk - j0))
                         for j0 in range(0, nchunk, PR)]
                for ii, (j0, pr) in enumerate(items):
                    buf = wldA if (ii % 2 == 0) else wldB
                    bk = (name, ii % 2)
                    W = F * pr
                    slot = slw[ii % 2]

                    def _(y, j0=j0, pr=pr, buf=buf, dsrc=dsrc, F=F, W=W):
                        prev = ld_rd.get(id(buf))
                        if prev:
                            y.wait_ge(sdv.h, prev)
                        y.dma_start(
                            out=buf[:, 0:W].rearrange("p (t c) -> p t c", c=F),
                            in_=dsrc[j0 * 128:(j0 + pr) * 128, 0:F]
                            .rearrange("(t p) c -> p t c", p=128)).then_inc(slot.h, 16)
                    Y(_)
                    slot.n += 16
                    ldm = slot.n
                    qb = ii % 2

                    def _(v, buf=buf, W=W, ldm=ldm, slot=slot, qb=qb):
                        v.wait_ge(slot.h, ldm)
                        prev = wqb_rd.get(qb)
                        if prev:
                            v.wait_ge(spe.h, prev)
                        sdv.inc(quantW(v, buf[:, 0:W], W, wqb[:, qb * FFI: qb * FFI + W]))
                    V(_)
                    qm = sdv.n
                    ld_rd[id(buf)] = qm

                    nsb = W // 128
                    for g0 in range(0, nsb, 8):
                        gn = min(8, nsb - g0)
                        pp = ppx[0] % 2
                        ppx[0] += 1
                        jcol = j0 + (g0 * 128) // F
                        c0 = (g0 * 128) % F

                        def _(t, g0=g0, gn=gn, pp=pp, qm=qm, qb=qb):
                            t.wait_ge(sdv.h, qm)
                            t.wait_ge(sac.h, sac.n)
                            last = None
                            for s in range(gn):
                                last = t.transpose(
                                    psW[:, pp * 1024 + s * 128: pp * 1024 + (s + 1) * 128],
                                    wqb[:, qb * FFI + (g0 + s) * 128: qb * FFI + (g0 + s + 1) * 128],
                                    identb[:])
                            spe.inc(last)
                        T(_)
                        tm = spe.n
                        wqb_rd[qb] = tm

                        def _(s, gn=gn, pp=pp, tm=tm):
                            s.wait_ge(spe.h, tm)
                            wait_stores(s)
                            sac.inc(s.copy(evacW[:, pp * 1024: pp * 1024 + gn * 128],
                                           psW[:, pp * 1024: pp * 1024 + gn * 128]))
                        S(_)
                        am = sac.n

                        def _(g, jcol=jcol, c0=c0, g0=g0, gn=gn, pp=pp, am=am,
                              ddstT=ddstT, sst=sst):
                            g.wait_ge(sac.h, am)
                            for s in range(gn):
                                g.dma_start(
                                    out=ddstT[c0 + s * 128: c0 + (s + 1) * 128,
                                              jcol * 128:(jcol + 1) * 128],
                                    in_=evacW[:, pp * 1024 + s * 128: pp * 1024 + (s + 1) * 128],
                                ).then_inc(sst.h, 16)
                        G(_)
                        sst.n += 16 * gn
                wprep_marks[name] = sst.n

            for nm in ("q", "k", "v", "o"):
                dsrc, ddst = {"q": (dwq, s_att["q"]), "k": (dwk, s_att["k"]),
                              "v": (dwv, s_att["v"]), "o": (dwo, s_att["o"])}[nm]
                for k in range(DK):
                    buf = wldA if (k % 2 == 0) else wldB
                    pp = k % 2
                    slot = slw[k % 2]

                    def _(y, k=k, buf=buf, dsrc=dsrc):
                        y.wait_ge(sac.h, sac.n)
                        y.dma_start(out=buf[:, 0:DM],
                                    in_=dsrc[k * 128:(k + 1) * 128, :]).then_inc(slot.h, 16)
                    Y(_)
                    slot.n += 16
                    ldm = slot.n

                    def _(s, buf=buf, ldm=ldm, pp=pp, slot=slot):
                        s.wait_ge(slot.h, ldm)
                        wait_stores(s)
                        sac.inc(s.copy(evacW[:, pp * 1024:(pp + 1) * 1024],
                                       buf[:, 0:DM]))
                    S(_)
                    am = sac.n

                    def _(g, k=k, am=am, pp=pp, ddst=ddst):
                        g.wait_ge(sac.h, am)
                        g.dma_start(out=ddst[k * 128:(k + 1) * 128, :],
                                    in_=evacW[:, pp * 1024:(pp + 1) * 1024]).then_inc(sga.h, 16)
                    G(_)
                    sga.n += 16
            wprep_marks["attn"] = sga.n

    def emit_phase_barrier():
        marks = [(c, c.n) for c in dma_ctrs if c.n > 0]
        marks += [(spe, spe.n), (sdv, sdv.n), (sac, sac.n)]

        def mk(own):
            def _(e):
                for ctr, m in marks:
                    if ctr is not own:
                        e.wait_ge(ctr.h, m)
            return _
        V(mk(None))
        S(mk(None))
        T(mk(None))
        Y(mk(None))
        G(mk(None))

    def emit_ffn_call(ci, rwaits):
        with ExitStack() as fes:
            FSB = lambda n, s, d: fes.enter_context(nc.sbuf_tensor(n, s, d))
            xq_fm = FSB(f"xq{ci}", [128, DK * 512], BF16)
            hq_fm = FSB(f"hq{ci}", [128, NKI * 512], BF16)
            wgbuf = FSB(f"wgb{ci}", [128, 2 * 4096], BF16)
            wubuf = FSB(f"wub{ci}", [128, 2 * 4096], BF16)
            wdbuf = FSB(f"wdb{ci}", [128, 2 * DM], BF16)
            hb = FSB(f"hb{ci}", [128, QT * 1024], F32)
            qsF = mk_qscratch(1024, f"F{ci}", FSB)
            quantF = mkquant(qsF)
            resid_m = [0]
            wkd_m = {}
            btA_m = {}
            wsl_m = {}
            wdl_m = {}

            for iq in range(NQ):
                tiles = [iq * QT + j for j in range(QT)]
                stats_m = emit_ln_stats(tiles, rwaits)

                with nc.psum_tensor(f"pstA{ci}_{iq}", [128, 2048], BF16) as pstA:
                    for j, t in enumerate(tiles):
                        pp = j % 2

                        def _(s, t=t, pp=pp, stats_m=stats_m):
                            if pp in wkd_m:
                                s.wait_ge(sdv.h, wkd_m[pp])
                            s.wait_ge(sdv.h, stats_m)
                            sac.inc(s.activation(wkD[:, pp * DM:(pp + 1) * DM],
                                                 Rt(t), AF.Identity,
                                                 bias=st_a[:, t:t + 1],
                                                 scale=st_r[:, t:t + 1]))
                        S(_)
                        am = sac.n

                        def _(v, pp=pp, am=am):
                            v.wait_ge(sac.h, am)
                            if pp in btA_m:
                                v.wait_ge(spe.h, btA_m[pp])
                            sdv.inc(quantF(v, wkD[:, pp * DM:(pp + 1) * DM], DM,
                                           btA[:, pp * DM:(pp + 1) * DM]))
                        V(_)
                        qm = sdv.n
                        wkd_m[pp] = qm

                        def _(t_, pp=pp, qm=qm):
                            t_.wait_ge(sdv.h, qm)
                            if resid_m[0]:
                                t_.wait_ge(sdv.h, resid_m[0])
                            t_.wait_ge(sac.h, sac.n)
                            last = None
                            for k in range(DK):
                                last = t_.transpose(
                                    pstA[:, pp * 1024 + k * 128: pp * 1024 + (k + 1) * 128],
                                    btA[:, pp * DM + k * 128: pp * DM + (k + 1) * 128],
                                    identb[:])
                            spe.inc(last)
                        T(_)
                        tm = spe.n
                        btA_m[pp] = tm

                        def _(s, j=j, pp=pp, tm=tm):
                            s.wait_ge(spe.h, tm)
                            dst = xq_fm[:].rearrange("p (k q) -> p k q", q=512)[
                                :, :, j * 128:(j + 1) * 128]
                            sac.inc(s.copy(dst,
                                           pstA[:, pp * 1024:(pp + 1) * 1024]
                                           .rearrange("p (k q) -> p k q", q=128)))
                        S(_)
                    xq_m = sac.n

                with nc.psum_tensor(f"psg{ci}_{iq}", [128, 1024], F32) as psg, \
                     nc.psum_tensor(f"psu{ci}_{iq}", [128, 1024], F32) as psu, \
                     nc.psum_tensor(f"pstB{ci}_{iq}", [128, 2048], BF16) as pstB:
                    it = 0
                    qp = 0
                    bmark = {}
                    for pi, (p0, pw) in enumerate(MGP):
                        for sub in range(2):
                            mgi = pi * 2 + sub
                            m0, mw = MGS[mgi]
                            wp = mgi % 2
                            slot = sfg[wp]

                            def _(y, m0=m0, mw=mw, wp=wp, ci=ci, slot=slot):
                                if wp in wsl_m:
                                    y.wait_ge(spe.h, wsl_m[wp])
                                if ci == 0:
                                    y.wait_ge(sgw.h, wprep_marks["wu"])
                                for k in range(DK):
                                    y.dma_start(out=wgbuf[:, wp * 4096 + k * 512: wp * 4096 + k * 512 + mw],
                                                in_=s_wgT[k * 128:(k + 1) * 128, m0:m0 + mw]).then_inc(slot.h, 16)
                                    y.dma_start(out=wubuf[:, wp * 4096 + k * 512: wp * 4096 + k * 512 + mw],
                                                in_=s_wuT[k * 128:(k + 1) * 128, m0:m0 + mw]).then_inc(slot.h, 16)
                            Y(_)
                            slot.n += 16 * 2 * DK
                            wlm = slot.n

                            for j in range(QT):
                                pp = it % 2

                                def _(t_, j=j, pp=pp, mw=mw, wp=wp, wlm=wlm,
                                      xq_m=xq_m, slot=slot, it=it):
                                    t_.wait_ge(slot.h, wlm)
                                    t_.wait_ge(sac.h, xq_m)
                                    if it - 2 in bmark:
                                        t_.wait_ge(sdv.h, bmark[it - 2])
                                    elif resid_m[0]:
                                        t_.wait_ge(sdv.h, resid_m[0])
                                    last = None
                                    for k in range(DK):
                                        lhs = xq_fm[:, k * 512 + j * 128: k * 512 + (j + 1) * 128]
                                        t_.matmul(psg[:, pp * 512: pp * 512 + mw], lhs,
                                                  wgbuf[:, wp * 4096 + k * 512: wp * 4096 + k * 512 + mw],
                                                  start=(k == 0), stop=(k == DK - 1))
                                        last = t_.matmul(psu[:, pp * 512: pp * 512 + mw], lhs,
                                                         wubuf[:, wp * 4096 + k * 512: wp * 4096 + k * 512 + mw],
                                                         start=(k == 0), stop=(k == DK - 1))
                                    spe.inc(last)
                                T(_)
                                mmm = spe.n

                                def _(s, pp=pp, mw=mw, mmm=mmm, it=it):
                                    s.wait_ge(spe.h, mmm)
                                    if it - 2 in bmark:
                                        s.wait_ge(sdv.h, bmark[it - 2])
                                    sac.inc(s.activation(wkA[:, pp * 512: pp * 512 + mw],
                                                         psg[:, pp * 512: pp * 512 + mw],
                                                         AF.Silu))
                                S(_)
                                em = sac.n

                                def _(v, j=j, pp=pp, m0=m0, mw=mw, p0=p0, em=em):
                                    v.wait_ge(sac.h, em)
                                    off = j * 1024 + (m0 - p0)
                                    sdv.inc(v.tensor_tensor(
                                        hb[:, off:off + mw],
                                        wkA[:, pp * 512: pp * 512 + mw],
                                        psu[:, pp * 512: pp * 512 + mw], ALU.mult))
                                V(_)
                                bmark[it] = sdv.n
                                it += 1
                            wsl_m[wp] = spe.n

                        for j in range(QT):
                            qpp = qp % 2
                            qp += 1

                            def _(v, j=j, pw=pw, qpp=qpp):
                                v.wait_ge(spe.h, spe.n)
                                sdv.inc(quantF(v, hb[:, j * 1024: j * 1024 + pw],
                                               pw, btB[:, qpp * DM: qpp * DM + pw]))
                            V(_)
                            hqm = sdv.n

                            def _(t_, pw=pw, qpp=qpp, hqm=hqm):
                                t_.wait_ge(sdv.h, hqm)
                                t_.wait_ge(sac.h, sac.n)
                                last = None
                                for sb in range(pw // 128):
                                    last = t_.transpose(
                                        pstB[:, qpp * 1024 + sb * 128: qpp * 1024 + (sb + 1) * 128],
                                        btB[:, qpp * DM + sb * 128: qpp * DM + (sb + 1) * 128],
                                        identb[:])
                                spe.inc(last)
                            T(_)
                            trm = spe.n

                            def _(s, j=j, p0=p0, pw=pw, qpp=qpp, trm=trm):
                                s.wait_ge(spe.h, trm)
                                kk0 = p0 // 128
                                dst = hq_fm[:].rearrange("p (k q) -> p k q", q=512)[
                                    :, kk0:kk0 + pw // 128, j * 128:(j + 1) * 128]
                                sac.inc(s.copy(dst,
                                               pstB[:, qpp * 1024: qpp * 1024 + pw]
                                               .rearrange("p (k q) -> p k q", q=128)))
                            S(_)
                    hq_m = sac.n

                with nc.psum_tensor(f"psd{ci}_{iq}", [128, 4096], F32) as psd:
                    for kp in range(NKI):
                        wp = kp % 2

                        slot = sfd[wp]

                        def _(y, kp=kp, wp=wp, ci=ci, slot=slot):
                            if wp in wdl_m:
                                y.wait_ge(spe.h, wdl_m[wp])
                            if ci == 0 and kp < 2:
                                y.wait_ge(sgwd.h, wprep_marks["wd"])
                            y.dma_start(out=wdbuf[:, wp * DM:(wp + 1) * DM],
                                        in_=s_wdT[kp * 128:(kp + 1) * 128, :]).then_inc(slot.h, 16)
                        Y(_)
                        slot.n += 16
                        wlm = slot.n

                        def _(t_, kp=kp, wp=wp, wlm=wlm, hq_m=hq_m, slot=slot):
                            t_.wait_ge(slot.h, wlm)
                            t_.wait_ge(sac.h, hq_m)
                            if kp == 0 and resid_m[0]:
                                t_.wait_ge(sdv.h, resid_m[0])
                            last = None
                            for j in range(QT):
                                lhs = hq_fm[:, kp * 512 + j * 128: kp * 512 + (j + 1) * 128]
                                for hf in range(2):
                                    last = t_.matmul(
                                        psd[:, j * 1024 + hf * 512: j * 1024 + (hf + 1) * 512],
                                        lhs,
                                        wdbuf[:, wp * DM + hf * 512: wp * DM + (hf + 1) * 512],
                                        start=(kp == 0), stop=(kp == NKI - 1))
                            spe.inc(last)
                        T(_)
                        wdl_m[wp] = spe.n
                    down_m = spe.n

                    def _(v, tiles=tiles, down_m=down_m):
                        v.wait_ge(spe.h, down_m)
                        t0 = tiles[0]
                        rsl = R[:, t0 * DM:(t0 + QT) * DM]
                        sdv.inc(v.scalar_tensor_tensor(rsl, psd[:], 0.5, rsl,
                                                       ALU.mult, ALU.add))
                    V(_)
                    resid_m[0] = sdv.n
        return sdv.n

    def emit_attn(rwaits):
        with ExitStack() as aes:
            ASB = lambda n, s, d: aes.enter_context(nc.sbuf_tensor(n, s, d))
            xn_fm = ASB("xn_fm", [128, DK * GTOK], BF16)
            qs_b = ASB("qs_b", [128, 2 * GTOK], BF16)
            qp_b = ASB("qp_b", [128, 2 * GTOK], BF16)
            qh = ASB("qh", [64, H * GTOK], BF16)
            kh = ASB("kh", [64, H * GTOK], BF16)
            v_g = ASB("v_g", [128, GT * DM], BF16)
            xg1g = ASB("xg1g", [64, GTOK], BF16)
            sTm = ASB("sTm", [128, 2 * 512], BF16)
            ktm_sb = ASB("ktm_sb", [128, DM], BF16)
            S32 = ASB("S32", [64, H * HD], F32)
            Sb = ASB("Sb", [64, H * HD], BF16)
            wbufA = ASB("wbufA", [128, 3 * 8192], BF16)
            y_fm = ASB("y_fm", [128, DM], BF16)
            gate_sb = ASB("gate_sb", [128, GT * DM], F32)

            stats_m = emit_ln_stats(list(range(NT)), rwaits)

            def _(v):
                v.memset(S32[:], 0.0)
                sdv.inc(v.memset(Sb[:], 0.0))
            V(_)

            with nc.psum_tensor("psA", [128, 2048], F32) as psA, \
                 nc.psum_tensor("psB", [128, 1024], F32) as psB, \
                 nc.psum_tensor("psC", [128, 512], F32) as psC, \
                 nc.psum_tensor("psD", [128, 1024], BF16) as psD:

                rs = {"psC_v": 0, "psD_v": 0, "psD_s": 0, "btA": 0,
                      "psB_s": 0, "psB_v": 0, "wkC_s": 0}
                sTm_rd = {0: 0, 1: 0}
                psA_m = {0: 0, 1: 0}
                gate_rd = [0]
                wsl_rd = {0: 0, 1: 0}
                ktm_rd = {0: 0, 1: 0}
                ktm_b = {0: ktm_sb, 1: wkA[:, 0:512].bitcast(BF16)}
                cs = {}
                gctx = {}

                def sections(g):
                    tiles = [g * GT + j for j in range(GT)]
                    gs = slice(g * GTOK, (g + 1) * GTOK)
                    gp = g % 2
                    ctx = {}

                    for j, t in enumerate(tiles):
                        pp = j % 2

                        def _(s, t=t, pp=pp):
                            s.wait_ge(sdv.h, stats_m)
                            s.wait_ge(spe.h, rs["btA"])
                            sac.inc(s.activation(btA[:, pp * DM:(pp + 1) * DM],
                                                 Rt(t), AF.Identity,
                                                 bias=st_a[:, t:t + 1],
                                                 scale=st_r[:, t:t + 1]))
                        S(_)
                        am = sac.n

                        def _(t_, pp=pp, am=am):
                            t_.wait_ge(sac.h, am)
                            t_.wait_ge(sac.h, rs["psD_s"])
                            t_.wait_ge(sdv.h, rs["psD_v"])
                            last = None
                            for k in range(DK):
                                last = t_.transpose(
                                    psD[:, k * 128:(k + 1) * 128],
                                    btA[:, pp * DM + k * 128: pp * DM + (k + 1) * 128],
                                    identb[:])
                            spe.inc(last)
                        T(_)
                        tm = spe.n
                        rs["btA"] = tm

                        def _(s, j=j, tm=tm):
                            s.wait_ge(spe.h, tm)
                            dst = xn_fm[:].rearrange("p (k q) -> p k q", q=GTOK)[
                                :, :, j * 128:(j + 1) * 128]
                            sac.inc(s.copy(dst, psD[:].rearrange("p (k q) -> p k q", q=128)))
                        S(_)
                        rs["psD_s"] = sac.n
                    xn_m = sac.n
                    ctx["xn_m"] = xn_m

                    wmarks = {}
                    for nm, wp in (("q", 0), ("k", 1)):
                        def _(y, nm=nm, wp=wp, g=g, slot=sat[wp]):
                            y.wait_ge(spe.h, wsl_rd[wp])
                            if g == 0:
                                y.wait_ge(sga.h, wprep_marks["attn"])
                            for k in range(DK):
                                y.dma_start(out=wbufA[:, wp * 8192 + k * 1024: wp * 8192 + (k + 1) * 1024],
                                            in_=s_att[nm][k * 128:(k + 1) * 128, :]).then_inc(slot.h, 16)
                        Y(_)
                        sat[wp].n += 16 * DK
                        wmarks[nm] = (wp, sat[wp].n)

                    projpe = {}
                    for nm, htile in (("q", qh), ("k", kh)):
                        wp, wlm = wmarks[nm]
                        for m in range(DK):
                            pp = m % 2

                            def _(t_, m=m, pp=pp, wp=wp, wlm=wlm, xn_m=xn_m):
                                t_.wait_ge(sat[wp].h, wlm)
                                t_.wait_ge(sac.h, xn_m)
                                t_.wait_ge(sac.h, sac.n)
                                t_.wait_ge(sdv.h, psA_m[0])
                                last = None
                                for k in range(DK):
                                    last = t_.matmul(
                                        psA[:, pp * 512: pp * 512 + GTOK],
                                        wbufA[:, wp * 8192 + k * 1024 + m * 128: wp * 8192 + k * 1024 + (m + 1) * 128],
                                        xn_fm[:, k * GTOK:(k + 1) * GTOK],
                                        start=(k == 0), stop=(k == DK - 1))
                                spe.inc(last)
                            T(_)
                            pm = spe.n

                            def _(s, pp=pp, pm=pm):
                                s.wait_ge(spe.h, pm)
                                s.wait_ge(spe.h, spe.n)
                                sac.inc(s.activation(qs_b[:, pp * GTOK:(pp + 1) * GTOK],
                                                     psA[:, pp * 512: pp * 512 + GTOK],
                                                     AF.Silu))
                            S(_)
                            sm = sac.n

                            def _(t_, pp=pp, sm=sm):
                                t_.wait_ge(sac.h, sm)
                                t_.wait_ge(sac.h, sac.n)
                                t_.wait_ge(sdv.h, rs["psB_v"])
                                spe.inc(t_.matmul(psB[:, pp * 512: pp * 512 + GTOK],
                                                  p128t[:],
                                                  qs_b[:, pp * GTOK:(pp + 1) * GTOK],
                                                  start=True, stop=True))
                            T(_)
                            qpm = spe.n

                            def _(s, pp=pp, qpm=qpm):
                                s.wait_ge(spe.h, qpm)
                                s.wait_ge(sdv.h, sdv.n)
                                sac.inc(s.copy(qp_b[:, pp * GTOK:(pp + 1) * GTOK],
                                               psB[:, pp * 512: pp * 512 + GTOK]))
                            S(_)
                            qem = sac.n
                            rs["psB_s"] = qem

                            def _(v, htile=htile, m=m, pp=pp, qem=qem, gs=gs):
                                v.wait_ge(sac.h, qem)
                                v.wait_ge(spe.h, spe.n)
                                v.tensor_tensor(btB[:, 0:GTOK],
                                                qs_b[:, pp * GTOK:(pp + 1) * GTOK],
                                                cosr[:, gs], ALU.mult)
                                v.tensor_tensor(btB[:, GTOK:2 * GTOK],
                                                qp_b[:, pp * GTOK:(pp + 1) * GTOK],
                                                sinr[:, gs], ALU.mult)
                                h0, h1 = 2 * m, 2 * m + 1
                                v.tensor_tensor(htile[:, h0 * GTOK:(h0 + 1) * GTOK],
                                                btB[0:64, 0:GTOK],
                                                btB[0:64, GTOK:2 * GTOK], ALU.add)
                                sdv.inc(v.tensor_tensor(htile[:, h1 * GTOK:(h1 + 1) * GTOK],
                                                        btB[64:128, 0:GTOK],
                                                        btB[64:128, GTOK:2 * GTOK],
                                                        ALU.add))
                            V(_)
                        projpe[nm] = spe.n
                    ctx["qk_m"] = sdv.n
                    wsl_rd[1] = projpe["k"]

                    qk_now = sdv.n
                    ctx["ktm_e"] = {}
                    for cl in range(GT):
                        def _(t_, cl=cl, qk_now=qk_now):
                            t_.wait_ge(sdv.h, qk_now)
                            t_.wait_ge(sdv.h, rs["psD_v"])
                            t_.wait_ge(sac.h, rs["psD_s"])
                            last = None
                            for hc in range(H):
                                last = t_.transpose(
                                    psD[:, hc * 64:(hc + 1) * 64],
                                    kh[:, hc * GTOK + cl * 128: hc * GTOK + (cl + 1) * 128],
                                    identb[0:64, 0:64])
                            spe.inc(last)
                        T(_)
                        ktt_m = spe.n

                        def _(v, cl=cl, ktt_m=ktt_m):
                            v.wait_ge(spe.h, ktt_m)
                            v.wait_ge(spe.h, ktm_rd[cl])
                            sdv.inc(v.tensor_copy(ktm_b[cl][:], psD[:]))
                        V(_)
                        ctx["ktm_e"][cl] = sdv.n
                        rs["psD_v"] = sdv.n

                    def _(y, g=g, slot=sat[0]):
                        y.wait_ge(spe.h, projpe["q"])
                        for k in range(DK):
                            y.dma_start(out=wbufA[:, k * 1024:(k + 1) * 1024],
                                        in_=s_att["v"][k * 128:(k + 1) * 128, :]).then_inc(slot.h, 16)
                    Y(_)
                    sat[0].n += 16 * DK
                    vwlm = sat[0].n

                    for c in range(GT):
                        def _(t_, c=c, vwlm=vwlm, xn_m=xn_m):
                            t_.wait_ge(sat[0].h, vwlm)
                            t_.wait_ge(sac.h, xn_m)
                            t_.wait_ge(sac.h, sac.n)
                            t_.wait_ge(sdv.h, psA_m[0])
                            last = None
                            for k in range(DK):
                                lhs = xn_fm[:, k * GTOK + c * 128: k * GTOK + (c + 1) * 128]
                                for hf in range(2):
                                    last = t_.matmul(
                                        psA[:, hf * 512:(hf + 1) * 512], lhs,
                                        wbufA[:, k * 1024 + hf * 512: k * 1024 + (hf + 1) * 512],
                                        start=(k == 0), stop=(k == DK - 1))
                            spe.inc(last)
                        T(_)
                        vm = spe.n

                        def _(s, c=c, vm=vm):
                            s.wait_ge(spe.h, vm)
                            sac.inc(s.copy(v_g[:, c * DM:(c + 1) * DM], psA[:, 0:1024]))
                        S(_)
                    ctx["v_m"] = sac.n
                    wsl_rd[0] = spe.n

                    def _(t_, xn_m=xn_m):
                        t_.wait_ge(sac.h, xn_m)
                        t_.wait_ge(sac.h, wgsmall_m)
                        t_.wait_ge(sac.h, sac.n)
                        t_.wait_ge(sdv.h, rs["psC_v"])
                        last = None
                        for k in range(DK):
                            last = t_.matmul(psC[0:64, 0:GTOK],
                                             wg1b[:, k * HD:(k + 1) * HD],
                                             xn_fm[:, k * GTOK:(k + 1) * GTOK],
                                             start=(k == 0), stop=(k == DK - 1))
                        spe.inc(last)
                    T(_)
                    x1m = spe.n

                    def _(s, x1m=x1m):
                        s.wait_ge(spe.h, x1m)
                        s.wait_ge(spe.h, spe.n)
                        sac.inc(s.copy(xg1g[:, 0:GTOK], psC[0:64, 0:GTOK]))
                    S(_)
                    ctx["xg1_m"] = sac.n
                    gctx[g] = ctx

                def emit_gate(g):
                    ctx = gctx[g]
                    for cl in range(GT):
                        def _(t_, cl=cl, xg1_m=ctx["xg1_m"]):
                            t_.wait_ge(sac.h, xg1_m)
                            t_.wait_ge(sac.h, wgsmall_m)
                            t_.wait_ge(sdv.h, rs["psB_v"])
                            t_.wait_ge(sac.h, rs["psB_s"])
                            last = None
                            for hf in range(2):
                                last = t_.matmul(psB[:, hf * 512:(hf + 1) * 512],
                                                 xg1g[:, cl * 128:(cl + 1) * 128],
                                                 wg2b[:, hf * 512:(hf + 1) * 512],
                                                 start=True, stop=True)
                            spe.inc(last)
                        T(_)
                        gz_m = spe.n

                        def _(s, cl=cl, gz_m=gz_m):
                            s.wait_ge(spe.h, gz_m)
                            s.wait_ge(sdv.h, gate_rd[0])
                            sac.inc(s.activation(
                                gate_sb[:, cl * DM:(cl + 1) * DM],
                                psB[:], AF.Tanh, scale=0.5))
                        S(_)
                        tg_m = sac.n
                        rs["psB_s"] = tg_m

                        def _(v, cl=cl, tg_m=tg_m):
                            v.wait_ge(sac.h, tg_m)
                            gsl = gate_sb[:, cl * DM:(cl + 1) * DM]
                            sdv.inc(v.tensor_scalar(gsl, gsl, 0.5, 0.5,
                                                    ALU.mult, ALU.add))
                        V(_)

                def o_load(g):
                    def _(y, g=g):
                        if g > 0:
                            y.wait_ge(spe.h, cs[2 * g - 1]["wo_m"])
                        for k in range(DK):
                            y.dma_start(out=wbufA[:, 2 * 8192 + k * 1024: 2 * 8192 + (k + 1) * 1024],
                                        in_=s_att["o"][k * 128:(k + 1) * 128, :]).then_inc(sato.h, 16)
                    Y(_)
                    sato.n += 16 * DK
                    gctx[g]["o_m"] = sato.n

                def mixing(c):
                    g, cl = c // GT, c % GT
                    ctx = gctx[g]
                    pc = c % 2

                    out_m = None
                    for b in range(4):
                        pp = b % 2

                        def _(t_, b=b, cl=cl, ctx=ctx):
                            t_.wait_ge(sdv.h, ctx["qk_m"])
                            t_.wait_ge(sac.h, ctx["xg1_m"])
                            t_.wait_ge(sdv.h, rs["psC_v"])
                            last = None
                            for hh in range(4):
                                hc = b * 4 + hh
                                last = t_.matmul(
                                    psC[:, hh * 128:(hh + 1) * 128],
                                    kh[:, hc * GTOK + cl * 128: hc * GTOK + (cl + 1) * 128],
                                    qh[:, hc * GTOK + cl * 128: hc * GTOK + (cl + 1) * 128],
                                    start=True, stop=True)
                            spe.inc(last)
                        T(_)
                        stm = spe.n

                        def _(v, pp=pp, stm=stm):
                            v.wait_ge(spe.h, stm)
                            v.wait_ge(spe.h, sTm_rd[pp])
                            sdv.inc(v.tensor_tensor(sTm[:, pp * 512:(pp + 1) * 512],
                                                    psC[:], mask4[:], ALU.mult))
                        V(_)
                        evm = sdv.n
                        rs["psC_v"] = evm

                        def _(t_, b=b, cl=cl, pp=pp, pc=pc, evm=evm, ctx=ctx, c=c):
                            t_.wait_ge(sdv.h, evm)
                            t_.wait_ge(sac.h, ctx["v_m"])
                            if c >= 1:
                                t_.wait_ge(sdv.h, cs[c - 1]["supd"])
                            t_.wait_ge(sdv.h, psA_m[pc])
                            last = None
                            for hh in range(4):
                                hc = b * 4 + hh
                                t_.matmul(psA[:, pc * 1024 + hc * 64: pc * 1024 + (hc + 1) * 64],
                                          sTm[:, pp * 512 + hh * 128: pp * 512 + (hh + 1) * 128],
                                          v_g[:, cl * DM + hc * 64: cl * DM + (hc + 1) * 64],
                                          start=True, stop=False)
                                last = t_.matmul(psA[:, pc * 1024 + hc * 64: pc * 1024 + (hc + 1) * 64],
                                                 qh[:, hc * GTOK + cl * 128: hc * GTOK + (cl + 1) * 128],
                                                 Sb[:, hc * 64:(hc + 1) * 64],
                                                 start=False, stop=True)
                            spe.inc(last)
                        T(_)
                        sTm_rd[pp] = spe.n
                    out_m = spe.n

                    def _(t_, cl=cl, ctx=ctx):
                        t_.wait_ge(sdv.h, ctx["ktm_e"][cl])
                        t_.wait_ge(sac.h, ctx["v_m"])
                        t_.wait_ge(sac.h, rs["psB_s"])
                        t_.wait_ge(sdv.h, rs["psB_v"])
                        kts = ktm_b[cl]
                        last = None
                        for hc in range(H):
                            last = t_.matmul(psB[0:64, hc * 64:(hc + 1) * 64],
                                             kts[:, hc * 64:(hc + 1) * 64],
                                             v_g[:, cl * DM + hc * 64: cl * DM + (hc + 1) * 64],
                                             start=True, stop=True)
                        spe.inc(last)
                    T(_)
                    sup_m = spe.n
                    ktm_rd[cl] = sup_m

                    def _(v, sup_m=sup_m, out_m=out_m):
                        v.wait_ge(spe.h, sup_m)
                        v.wait_ge(spe.h, out_m)
                        v.tensor_tensor(S32[:], S32[:], psB[0:64, :], ALU.add)
                        sdv.inc(v.tensor_copy(Sb[:], S32[:]))
                    V(_)
                    supd_m = sdv.n
                    rs["psB_v"] = supd_m
                    cs[c] = {"supd": supd_m, "out_m": out_m, "pc": pc,
                             "g": g, "cl": cl}

                def epilogue(c):
                    st = cs[c]
                    g, cl, pc = st["g"], st["cl"], st["pc"]
                    ctx = gctx[g]
                    gsl = gate_sb[:, cl * DM:(cl + 1) * DM]

                    def _(v, st=st, gsl=gsl, pc=pc):
                        v.wait_ge(spe.h, st["out_m"])
                        sl = lambda tt: tt[:, 0:1]
                        v.tensor_tensor(wkC[:, DM:2 * DM], gsl,
                                        psA[:, pc * 1024:(pc + 1) * 1024], ALU.mult)
                        v.tensor_reduce(st_s1[:, 0:1],
                                        wkC[:, DM:2 * DM], AX.X, ALU.add)
                        v.tensor_tensor(wkC[:, 0:DM], wkC[:, DM:2 * DM],
                                        wkC[:, DM:2 * DM], ALU.mult)
                        v.tensor_reduce(st_ss[:, 0:1], wkC[:, 0:DM], AX.X, ALU.add)
                        v.tensor_scalar(sl(st_m), sl(st_s1), RD, None, ALU.mult)
                        v.tensor_scalar(sl(st_v), sl(st_ss), RD, None, ALU.mult)
                        v.tensor_tensor(sl(st_t1), sl(st_m), sl(st_m), ALU.mult)
                        v.tensor_tensor(sl(st_v), sl(st_v), sl(st_t1), ALU.subtract)
                        sdv.inc(v.tensor_scalar(sl(st_v), sl(st_v), EPS, None,
                                                ALU.add))
                    V(_)
                    lnv_m = sdv.n
                    gate_rd[0] = lnv_m

                    def _(s, lnv_m=lnv_m):
                        s.wait_ge(sdv.h, lnv_m)
                        sac.inc(s.activation(st_t2[:, 0:1], st_v[:, 0:1], AF.Sqrt))
                    S(_)
                    sqr_m = sac.n

                    def _(v, sqr_m=sqr_m):
                        v.wait_ge(sac.h, sqr_m)
                        sl = lambda tt: tt[:, 0:1]
                        v.reciprocal(sl(st_r), sl(st_t2))
                        v.scalar_tensor_tensor(sl(st_a), sl(st_m), -1.0,
                                               sl(st_r), ALU.mult, ALU.mult)
                        rbc = st_r[:, 0:1].broadcast_to([128, DM])
                        abc = st_a[:, 0:1].broadcast_to([128, DM])
                        v.tensor_tensor(wkD[:, 0:DM], wkC[:, DM:2 * DM], rbc,
                                        ALU.mult)
                        v.tensor_tensor(wkD[:, 0:DM], wkD[:, 0:DM], abc, ALU.add)
                        v.wait_ge(spe.h, rs["btA"])
                        v.tensor_tensor(wkD[:, DM:2 * DM], wkD[:, 0:DM],
                                        gammaT[:], ALU.mult)
                        sdv.inc(v.tensor_tensor(btA[:, 0:DM], wkD[:, DM:2 * DM],
                                                betaT[:], ALU.add))
                    V(_)
                    y2_m = sdv.n

                    def _(t_, y2_m=y2_m):
                        t_.wait_ge(sdv.h, y2_m)
                        t_.wait_ge(sdv.h, rs["psD_v"])
                        t_.wait_ge(sac.h, rs["psD_s"])
                        last = None
                        for k in range(DK):
                            last = t_.transpose(psD[:, k * 128:(k + 1) * 128],
                                                btA[:, k * 128:(k + 1) * 128],
                                                identb[:])
                        spe.inc(last)
                    T(_)
                    ytr_m = spe.n
                    rs["btA"] = ytr_m

                    def _(v, ytr_m=ytr_m):
                        v.wait_ge(spe.h, ytr_m)
                        sdv.inc(v.tensor_copy(y_fm[:], psD[:]))
                    V(_)
                    yfm_m = sdv.n
                    rs["psD_v"] = yfm_m

                    def _(t_, yfm_m=yfm_m, lnv_m=lnv_m, pc=pc, o_m=ctx["o_m"]):
                        t_.wait_ge(sdv.h, yfm_m)
                        t_.wait_ge(sato.h, o_m)
                        t_.wait_ge(sdv.h, lnv_m)
                        last = None
                        for k in range(DK):
                            for hf in range(2):
                                last = t_.matmul(
                                    psA[:, pc * 1024 + hf * 512: pc * 1024 + (hf + 1) * 512],
                                    y_fm[:, k * 128:(k + 1) * 128],
                                    wbufA[:, 2 * 8192 + k * 1024 + hf * 512: 2 * 8192 + k * 1024 + (hf + 1) * 512],
                                    start=(k == 0), stop=(k == DK - 1))
                        spe.inc(last)
                    T(_)
                    wo_m = spe.n
                    st["wo_m"] = wo_m

                    def _(v, c=c, wo_m=wo_m, pc=pc):
                        v.wait_ge(spe.h, wo_m)
                        sdv.inc(v.scalar_tensor_tensor(Rt(c), psA[:, pc * 1024:(pc + 1) * 1024],
                                                       1.0, Rt(c), ALU.mult, ALU.add))
                    V(_)
                    psA_m[pc] = sdv.n

                pending = None
                for g in range(NGRP):
                    sections(g)
                    mixing(2 * g)
                    if pending is not None:
                        epilogue(pending)
                    o_load(g)
                    emit_gate(g)
                    mixing(2 * g + 1)
                    epilogue(2 * g)
                    pending = 2 * g + 1
                epilogue(pending)
        return sdv.n

    emit_phase_barrier()
    r_mark = [(ssu, setup_m)]
    if "ffn1" in PHASES:
        r_mark = [(sdv, emit_ffn_call(0, r_mark))]
        emit_phase_barrier()
    if "attn" in PHASES:
        r_mark = [(sdv, emit_attn(r_mark))]
        emit_phase_barrier()
    if "ffn2" in PHASES:
        r_mark = [(sdv, emit_ffn_call(1, r_mark))]

    def _(g):
        for ctr, m in r_mark:
            g.wait_ge(ctr.h, m)
        for t in range(NT):
            g.dma_start(out=dout[t * TILE:(t + 1) * TILE, :], in_=Rt(t)).then_inc(sgo.h, 16)
        sgo.n += 16 * NT
        g.wait_ge(sgo.h, sgo.n)
    G(_)

    es.close()
    return nc


def _host_consts():
    BASE = 10000.0
    inv_freq = BASE ** (-(np.arange(32, dtype=np.float64) * 2.0) / 64.0)
    pos = np.arange(NTOK, dtype=np.float64)
    ang = pos[:, None] * inv_freq[None, :]
    cos32 = np.cos(ang).T
    sin32 = np.sin(ang).T
    cosr = np.tile(cos32, (4, 1)).astype(ml_dtypes.bfloat16)
    sinr = np.tile(sin32, (4, 1)).astype(ml_dtypes.bfloat16)
    P64 = np.zeros((64, 64), np.float32)
    for i in range(32):
        P64[i, i + 32] = -1.0
        P64[i + 32, i] = 1.0
    P128 = np.zeros((128, 128), np.float32)
    P128[0:64, 0:64] = P64
    P128[64:128, 64:128] = P64
    p128t = P128.T.astype(ml_dtypes.bfloat16)
    ident = np.eye(128, dtype=np.float32).astype(ml_dtypes.bfloat16)
    mask = np.triu(np.ones((128, 128), np.float32))
    mask4 = np.tile(mask, (1, 4)).astype(ml_dtypes.bfloat16)
    return cosr, sinr, p128t, ident, mask4


def kernel(L0_attn_x, L0_attn_Wq, L0_attn_Wk, L0_attn_Wv, L0_attn_Wo,
           L0_attn_Wg1, L0_attn_Wg2, L0_attn_gamma, L0_attn_beta,
           L0_ffn_w_gate, L0_ffn_w_up, L0_ffn_w_down):
    x = np.ascontiguousarray(np.asarray(L0_attn_x, dtype=np.float32))
    B = x.shape[0]
    cosr, sinr, p128t, ident, mask4 = _host_consts()
    gammaT = np.ascontiguousarray(
        np.broadcast_to(np.asarray(L0_attn_gamma, np.float32), (128, DM)))
    betaT = np.ascontiguousarray(
        np.broadcast_to(np.asarray(L0_attn_beta, np.float32), (128, DM)))
    common = {
        "wq": np.ascontiguousarray(np.asarray(L0_attn_Wq, np.float32)),
        "wk": np.ascontiguousarray(np.asarray(L0_attn_Wk, np.float32)),
        "wv": np.ascontiguousarray(np.asarray(L0_attn_Wv, np.float32)),
        "wo": np.ascontiguousarray(np.asarray(L0_attn_Wo, np.float32)),
        "wg1": np.ascontiguousarray(np.asarray(L0_attn_Wg1, np.float32)),
        "wg2": np.ascontiguousarray(np.asarray(L0_attn_Wg2, np.float32)),
        "d_gammaT": gammaT, "d_betaT": betaT,
        "wgate": np.ascontiguousarray(np.asarray(L0_ffn_w_gate, np.float32)),
        "wup": np.ascontiguousarray(np.asarray(L0_ffn_w_up, np.float32)),
        "wdown": np.ascontiguousarray(np.asarray(L0_ffn_w_down, np.float32)),
        "d_cosr": cosr, "d_sinr": sinr, "d_p128t": p128t, "d_ident": ident,
        "d_mask4": mask4,
    }
    nc = build_program()
    in_maps = [dict(common, x=np.ascontiguousarray(x[b])) for b in range(B)]
    res = run_bass_kernel_spmd(nc, in_maps, core_ids=list(range(B)))
    return np.stack([res.results[b]["out"] for b in range(B)], axis=0)


# version 10 (final)
# speedup vs baseline: 1.5727x; 1.0234x over previous
import numpy as np
import ml_dtypes
from contextlib import ExitStack

import concourse.bass as bass
import concourse.mybir as mybir
from concourse.bass_utils import run_bass_kernel_spmd

F32 = mybir.dt.float32
BF16 = mybir.dt.bfloat16
I32 = mybir.dt.int32
AF = mybir.ActivationFunctionType
ALU = mybir.AluOpType
AX = mybir.AxisListType

NTOK, DM, TILE = 2048, 1024, 128
NT = NTOK // TILE
DK = DM // TILE
FFI = 2816
NKI = FFI // TILE
H, HD = 16, 64
MGS = [(0, 512), (512, 512), (1024, 512), (1536, 512), (2048, 512), (2560, 256)]
MGP = [(0, 1024), (1024, 1024), (2048, 768)]
QT = 4
NQ = NT // QT
GT = 2
NGRP = NT // GT
GTOK = GT * TILE

INV6 = float(np.float32(1.0) / np.float32(6.0))
ANDM = float(np.int64(0xFFC00000) - (1 << 32))
RD = float(np.float32(1.0) / np.float32(DM))
EPS = 1e-5

PHASES = ("ffn1", "attn", "ffn2")


def build_program():
    nc = bass.Bass("TRN2", target_bir_lowering=False)

    dx = nc.dram_tensor("x", [NTOK, DM], F32, kind="ExternalInput").ap()
    dwq = nc.dram_tensor("wq", [DM, DM], F32, kind="ExternalInput").ap()
    dwk = nc.dram_tensor("wk", [DM, DM], F32, kind="ExternalInput").ap()
    dwv = nc.dram_tensor("wv", [DM, DM], F32, kind="ExternalInput").ap()
    dwo = nc.dram_tensor("wo", [DM, DM], F32, kind="ExternalInput").ap()
    dwg1 = nc.dram_tensor("wg1", [DM, HD], F32, kind="ExternalInput").ap()
    dwg2 = nc.dram_tensor("wg2", [HD, DM], F32, kind="ExternalInput").ap()
    dgam = nc.dram_tensor("d_gammaT", [128, DM], F32, kind="ExternalInput").ap()
    dbet = nc.dram_tensor("d_betaT", [128, DM], F32, kind="ExternalInput").ap()
    dwgate = nc.dram_tensor("wgate", [FFI, DM], F32, kind="ExternalInput").ap()
    dwup = nc.dram_tensor("wup", [FFI, DM], F32, kind="ExternalInput").ap()
    dwdown = nc.dram_tensor("wdown", [DM, FFI], F32, kind="ExternalInput").ap()
    dcos = nc.dram_tensor("d_cosr", [128, NTOK], BF16, kind="ExternalInput").ap()
    dsin = nc.dram_tensor("d_sinr", [128, NTOK], BF16, kind="ExternalInput").ap()
    dp128 = nc.dram_tensor("d_p128t", [128, 128], BF16, kind="ExternalInput").ap()
    dident = nc.dram_tensor("d_ident", [128, 128], BF16, kind="ExternalInput").ap()
    dmask = nc.dram_tensor("d_mask4", [128, 512], BF16, kind="ExternalInput").ap()
    dout = nc.dram_tensor("out", [NTOK, DM], F32, kind="ExternalOutput").ap()
    s_wgT = nc.dram_tensor("s_wgT", [DM, FFI], BF16).ap()
    s_wuT = nc.dram_tensor("s_wuT", [DM, FFI], BF16).ap()
    s_wdT = nc.dram_tensor("s_wdT", [FFI, DM], BF16).ap()
    s_att = {n: nc.dram_tensor(f"s_w{n}", [DM, DM], BF16).ap()
             for n in ("q", "k", "v", "o")}

    es = ExitStack()
    SB = lambda n, s, d: es.enter_context(nc.sbuf_tensor(n, s, d))

    R = SB("R", [128, NT * DM], F32)
    cosr = SB("cosr", [128, NTOK], BF16)
    sinr = SB("sinr", [128, NTOK], BF16)
    p128t = SB("p128t", [128, 128], BF16)
    identb = SB("identb", [128, 128], BF16)
    mask4 = SB("mask4", [128, 512], BF16)
    gammaT = SB("gammaT", [128, DM], F32)
    betaT = SB("betaT", [128, DM], F32)
    wg1b = SB("wg1b", [128, DK * HD], BF16)
    wg2b = SB("wg2b", [64, DM], BF16)
    st_s1 = SB("st_s1", [128, NT], F32)
    st_ss = SB("st_ss", [128, NT], F32)
    st_m = SB("st_m", [128, NT], F32)
    st_v = SB("st_v", [128, NT], F32)
    st_r = SB("st_r", [128, NT], F32)
    st_a = SB("st_a", [128, NT], F32)
    st_t1 = SB("st_t1", [128, NT], F32)
    st_t2 = SB("st_t2", [128, NT], F32)
    wkA = SB("wkA", [128, 2 * 512], F32)
    wkC = SB("wkC", [128, 2 * DM], F32)
    wkD = SB("wkD", [128, 2 * DM], F32)
    btA = SB("btA", [128, 2 * DM], BF16)
    btB = SB("btB", [128, 2 * DM], BF16)

    class Ctr:
        def __init__(self, name, auto=False):
            self.h = es.enter_context(nc.semaphore(name))
            self.n = 0
            self.auto = auto

        def inc(self, instr, k=1):
            if self.auto:
                return self.n
            instr.then_inc(self.h, k)
            self.n += k
            return self.n

    ssu = Ctr("ssu")
    slw = [Ctr("slw0"), Ctr("slw1")]
    sgw = Ctr("sgw")
    sgwd = Ctr("sgwd")
    sga = Ctr("sga")
    sga2 = Ctr("sga2")
    sgat = [sga, sga2]
    sfg = [Ctr("sfg0"), Ctr("sfg1")]
    sfd = [Ctr("sfd0"), Ctr("sfd1")]
    sat = [Ctr("sat0"), Ctr("sat1")]
    sato = Ctr("sato")
    sgo = Ctr("sgo")
    spe = Ctr("spe")
    sdv = Ctr("sdv", auto=True)
    sac = Ctr("sac", auto=True)
    dma_ctrs = [ssu, slw[0], slw[1], sgw, sgwd, sga, sga2, sfg[0], sfg[1],
                sfd[0], sfd[1], sat[0], sat[1], sato, sgo]

    class EngProxy:
        _PASS = ("wait_ge", "dma_start")

        def __init__(self, eng, ctr):
            object.__setattr__(self, "_eng", eng)
            object.__setattr__(self, "_ctr", ctr)

        def __getattr__(self, name):
            f = getattr(self._eng, name)
            if name in EngProxy._PASS or not callable(f):
                return f
            eng, ctr = self._eng, self._ctr

            def wrapped(*a, **kw):
                if ctr.n > 0:
                    eng.wait_ge(ctr.h, ctr.n)
                instr = f(*a, **kw)
                instr.then_inc(ctr.h, 1)
                ctr.n += 1
                return instr
            return wrapped

    blk = es.enter_context(nc.Block())
    T, Y, G = blk.tensor, blk.sync, blk.gpsimd
    V = lambda f: blk.vector(lambda v: f(EngProxy(v, sdv)))
    S = lambda f: blk.scalar(lambda s: f(EngProxy(s, sac)))

    wv3 = lambda ap: ap.rearrange("p (b w) -> p b w", w=16)

    def mkquant(qs):
        q_i, q_r, q_c, q_bm, q_sc, q_iv, q_sh = qs

        def bc3(tile, nb):
            return tile[:, 0:nb].unsqueeze(2).broadcast_to([128, nb, 16])

        def emit_quant(v, src_ap, F, dst_ap):
            nb = F // 16
            v.tensor_reduce(q_bm[:, 0:nb], wv3(src_ap), AX.X, ALU.max,
                            apply_absolute_value=True)
            v.tensor_scalar(q_sc[:, 0:nb], q_bm[:, 0:nb], INV6, 1e-12,
                            ALU.mult, ALU.max)
            v.reciprocal(q_iv[:, 0:nb], q_sc[:, 0:nb])
            v.tensor_scalar(q_iv[:, 0:nb], q_iv[:, 0:nb], 2.0, None, ALU.mult)
            v.tensor_scalar(q_sh[:, 0:nb], q_sc[:, 0:nb], 0.5, None, ALU.mult)
            v.tensor_tensor(wv3(src_ap), wv3(src_ap), bc3(q_iv, nb), ALU.mult)
            v.tensor_scalar(q_i[:, 0:F], src_ap.bitcast(I32),
                            float(0x00200000), None, ALU.add)
            v.tensor_scalar(q_i[:, 0:F], q_i[:, 0:F], int(-4194304), None,
                            ALU.bitwise_and)
            v.tensor_scalar(q_c[:, 0:F], q_i[:, 0:F].bitcast(F32), -4.0, 4.0,
                            ALU.max, ALU.min)
            v.tensor_tensor(q_c[:, 0:F], q_i[:, 0:F].bitcast(F32), q_c[:, 0:F],
                            ALU.subtract)
            v.tensor_scalar(src_ap, src_ap, 4.0, -4.0, ALU.min, ALU.max)
            v.tensor_scalar(q_r[:, 0:F], src_ap, 12582912.0, -12582912.0,
                            ALU.add, ALU.add)
            v.tensor_tensor(q_r[:, 0:F], q_r[:, 0:F], q_c[:, 0:F], ALU.add)
            return v.tensor_tensor(wv3(dst_ap), wv3(q_r[:, 0:F]), bc3(q_sh, nb),
                                   ALU.mult)
        return emit_quant

    def mk_qscratch(F, tag, alloc):
        return (alloc(f"q_i{tag}", [128, F], I32),
                alloc(f"q_r{tag}", [128, F], BF16),
                alloc(f"q_c{tag}", [128, F], BF16), alloc(f"q_bm{tag}", [128, F // 16], F32),
                alloc(f"q_sc{tag}", [128, F // 16], F32), alloc(f"q_iv{tag}", [128, F // 16], F32),
                alloc(f"q_sh{tag}", [128, F // 16], F32))

    def Rt(t):
        return R[:, t * DM:(t + 1) * DM]

    def emit_ln_stats(tiles, rwaits):
        c0, c1 = min(tiles), max(tiles) + 1
        assert tiles == list(range(c0, c1))
        sl = lambda tt: tt[:, c0:c1]

        def _(v):
            for ctr, m in rwaits:
                v.wait_ge(ctr.h, m)
            sdv.inc(v.tensor_reduce(
                sl(st_s1),
                R[:, c0 * DM:c1 * DM].rearrange("p (t c) -> p t c", c=DM),
                AX.X, ALU.add))
        V(_)

        def _(s):
            for ctr, m in rwaits:
                s.wait_ge(ctr.h, m)
            s.wait_ge(sdv.h, sdv.n)
            last = None
            for j, t in enumerate(tiles):
                last = s.activation(wkC[:, (j % 2) * DM:(j % 2) * DM + DM],
                                    Rt(t), AF.Square,
                                    accum_out=st_ss[:, t:t + 1])
            sac.inc(last)
        S(_)
        acm = sac.n

        def _(v):
            v.wait_ge(sac.h, acm)
            v.tensor_scalar(sl(st_m), sl(st_s1), RD, None, ALU.mult)
            v.tensor_scalar(sl(st_v), sl(st_ss), RD, None, ALU.mult)
            v.tensor_tensor(sl(st_t1), sl(st_m), sl(st_m), ALU.mult)
            v.tensor_tensor(sl(st_v), sl(st_v), sl(st_t1), ALU.subtract)
            sdv.inc(v.tensor_scalar(sl(st_v), sl(st_v), EPS, None, ALU.add))
        V(_)
        vm = sdv.n

        def _(s):
            s.wait_ge(sdv.h, vm)
            sac.inc(s.activation(sl(st_t2), sl(st_v), AF.Sqrt))
        S(_)
        sqm = sac.n

        def _(v):
            v.wait_ge(sac.h, sqm)
            v.reciprocal(sl(st_r), sl(st_t2))
            sdv.inc(v.scalar_tensor_tensor(sl(st_a), sl(st_m), -1.0, sl(st_r),
                                           ALU.mult, ALU.mult))
        V(_)
        return sdv.n

    def _(g):
        for t in range(NT):
            g.dma_start(out=Rt(t), in_=dx[t * TILE:(t + 1) * TILE, :]).then_inc(ssu.h, 16)
        for dst, src in [(cosr, dcos), (sinr, dsin), (p128t, dp128),
                         (identb, dident), (mask4, dmask), (gammaT, dgam),
                         (betaT, dbet)]:
            g.dma_start(out=dst[:], in_=src).then_inc(ssu.h, 16)
        g.dma_start(out=wkC[:, 0:DK * HD].rearrange("p (k d) -> p k d", d=HD),
                    in_=dwg1.rearrange("(k p) d -> p k d", p=128)).then_inc(ssu.h, 16)
        g.dma_start(out=wkD[0:64, 0:DM], in_=dwg2).then_inc(ssu.h, 16)
        ssu.n += 16 * (NT + 7 + 2)
    G(_)
    setup_m = ssu.n

    def _(s):
        s.wait_ge(ssu.h, setup_m)
        s.copy(wg1b[:], wkC[:, 0:DK * HD])
        sac.inc(s.copy(wg2b[:], wkD[0:64, 0:DM]))
    S(_)
    wgsmall_m = sac.n

    wprep_marks = {}
    def wait_stores(e):
        for ctr in (sgw, sgwd, sga, sga2):
            if ctr.n > 0:
                e.wait_ge(ctr.h, ctr.n)

    with ExitStack() as wes:
        WSB = lambda n, s, d: wes.enter_context(nc.sbuf_tensor(n, s, d))
        wldA = WSB("wldA", [128, FFI], F32)
        wldB = WSB("wldB", [128, FFI], F32)
        wqb = WSB("wqb", [128, 2 * FFI], BF16)
        evacW = WSB("evacW", [128, 4096], BF16)
        qsW = mk_qscratch(FFI, "W", WSB)
        quantW = mkquant(qsW)

        with nc.psum_tensor("psW", [128, 2048], BF16) as psW:
            ld_rd = {}
            wqb_rd = {}
            ppx = [0]
            for name, dsrc, ddstT, nchunk, F, PR in [
                ("wg", dwgate, s_wgT, NKI, DM, 2),
                ("wu", dwup, s_wuT, NKI, DM, 2),
                ("wd", dwdown, s_wdT, DK, FFI, 1),
            ]:
                sst = sgw if name in ("wg", "wu") else sgwd
                items = [(j0, min(PR, nchunk - j0))
                         for j0 in range(0, nchunk, PR)]
                for ii, (j0, pr) in enumerate(items):
                    buf = wldA if (ii % 2 == 0) else wldB
                    bk = (name, ii % 2)
                    W = F * pr
                    slot = slw[ii % 2]

                    def _(y, j0=j0, pr=pr, buf=buf, dsrc=dsrc, F=F, W=W):
                        prev = ld_rd.get(id(buf))
                        if prev:
                            y.wait_ge(sdv.h, prev)
                        y.dma_start(
                            out=buf[:, 0:W].rearrange("p (t c) -> p t c", c=F),
                            in_=dsrc[j0 * 128:(j0 + pr) * 128, 0:F]
                            .rearrange("(t p) c -> p t c", p=128)).then_inc(slot.h, 16)
                    Y(_)
                    slot.n += 16
                    ldm = slot.n
                    qb = ii % 2

                    def _(v, buf=buf, W=W, ldm=ldm, slot=slot, qb=qb):
                        v.wait_ge(slot.h, ldm)
                        prev = wqb_rd.get(qb)
                        if prev:
                            v.wait_ge(spe.h, prev)
                        sdv.inc(quantW(v, buf[:, 0:W], W, wqb[:, qb * FFI: qb * FFI + W]))
                    V(_)
                    qm = sdv.n
                    ld_rd[id(buf)] = qm

                    nsb = W // 128
                    for g0 in range(0, nsb, 8):
                        gn = min(8, nsb - g0)
                        pp = ppx[0] % 2
                        ppx[0] += 1
                        jcol = j0 + (g0 * 128) // F
                        c0 = (g0 * 128) % F

                        def _(t, g0=g0, gn=gn, pp=pp, qm=qm, qb=qb):
                            t.wait_ge(sdv.h, qm)
                            t.wait_ge(sac.h, sac.n)
                            last = None
                            for s in range(gn):
                                last = t.transpose(
                                    psW[:, pp * 1024 + s * 128: pp * 1024 + (s + 1) * 128],
                                    wqb[:, qb * FFI + (g0 + s) * 128: qb * FFI + (g0 + s + 1) * 128],
                                    identb[:])
                            spe.inc(last)
                        T(_)
                        tm = spe.n
                        wqb_rd[qb] = tm

                        def _(s, gn=gn, pp=pp, tm=tm):
                            s.wait_ge(spe.h, tm)
                            wait_stores(s)
                            sac.inc(s.copy(evacW[:, pp * 1024: pp * 1024 + gn * 128],
                                           psW[:, pp * 1024: pp * 1024 + gn * 128]))
                        S(_)
                        am = sac.n

                        def _(g, jcol=jcol, c0=c0, g0=g0, gn=gn, pp=pp, am=am,
                              ddstT=ddstT, sst=sst):
                            g.wait_ge(sac.h, am)
                            for s in range(gn):
                                g.dma_start(
                                    out=ddstT[c0 + s * 128: c0 + (s + 1) * 128,
                                              jcol * 128:(jcol + 1) * 128],
                                    in_=evacW[:, pp * 1024 + s * 128: pp * 1024 + (s + 1) * 128],
                                ).then_inc(sst.h, 16)
                        G(_)
                        sst.n += 16 * gn
                wprep_marks[name] = sst.n

            it2 = [0]
            cp_marks = {}
            for nm in ("q", "k", "v", "o"):
                dsrc, ddst = {"q": (dwq, s_att["q"]), "k": (dwk, s_att["k"]),
                              "v": (dwv, s_att["v"]), "o": (dwo, s_att["o"])}[nm]
                for k0 in range(0, DK, 2):
                    ii = it2[0]
                    it2[0] += 1
                    buf = wldA if (ii % 2 == 0) else wldB
                    pp = ii % 2
                    slot = slw[pp]
                    sst = sgat[pp]
                    use_v = (pp == 1)

                    def _(y, k0=k0, buf=buf, dsrc=dsrc, ii=ii, slot=slot):
                        prev = cp_marks.get(ii - 2)
                        if prev:
                            y.wait_ge(prev[0].h, prev[1])
                        else:
                            y.wait_ge(sdv.h, sdv.n)
                        y.dma_start(
                            out=buf[:, 0:2 * DM].rearrange("p (t c) -> p t c", c=DM),
                            in_=dsrc[k0 * 128:(k0 + 2) * 128, :]
                            .rearrange("(t p) c -> p t c", p=128)).then_inc(slot.h, 16)
                    Y(_)
                    slot.n += 16
                    ldm = slot.n

                    def cpwaits(e, ldm=ldm, slot=slot, sst=sst):
                        e.wait_ge(slot.h, ldm)
                        for ctr in (sgw, sgwd, sst):
                            if ctr.n > 0:
                                e.wait_ge(ctr.h, ctr.n)
                    if use_v:
                        def _(v, pp=pp, buf=buf):
                            cpwaits(v)
                            sdv.inc(v.tensor_copy(
                                evacW[:, pp * 2048:(pp + 1) * 2048],
                                buf[:, 0:2 * DM]))
                        V(_)
                        cp_marks[ii] = (sdv, sdv.n)
                    else:
                        def _(s, pp=pp, buf=buf):
                            cpwaits(s)
                            sac.inc(s.copy(
                                evacW[:, pp * 2048:(pp + 1) * 2048],
                                buf[:, 0:2 * DM]))
                        S(_)
                        cp_marks[ii] = (sac, sac.n)
                    cm = cp_marks[ii]

                    def _(g, k0=k0, pp=pp, ddst=ddst, cm=cm, sst=sst):
                        g.wait_ge(cm[0].h, cm[1])
                        g.dma_start(
                            out=ddst[k0 * 128:(k0 + 2) * 128, :]
                            .rearrange("(t p) c -> p t c", p=128),
                            in_=evacW[:, pp * 2048:(pp + 1) * 2048]
                            .rearrange("p (t c) -> p t c", c=DM)).then_inc(sst.h, 16)
                    G(_)
                    sst.n += 16
            wprep_marks["attn"] = (sgat[0].n, sgat[1].n)

    def emit_phase_barrier():
        marks = [(c, c.n) for c in dma_ctrs if c.n > 0]
        marks += [(spe, spe.n), (sdv, sdv.n), (sac, sac.n)]

        def mk(own):
            def _(e):
                for ctr, m in marks:
                    if ctr is not own:
                        e.wait_ge(ctr.h, m)
            return _
        V(mk(None))
        S(mk(None))
        T(mk(None))
        Y(mk(None))
        G(mk(None))

    def emit_ffn_call(ci, rwaits):
        with ExitStack() as fes:
            FSB = lambda n, s, d: fes.enter_context(nc.sbuf_tensor(n, s, d))
            xq_fm = FSB(f"xq{ci}", [128, DK * 512], BF16)
            hq_fm = FSB(f"hq{ci}", [128, NKI * 512], BF16)
            wgbuf = FSB(f"wgb{ci}", [128, 2 * 4096], BF16)
            wubuf = FSB(f"wub{ci}", [128, 2 * 4096], BF16)
            wdbuf = FSB(f"wdb{ci}", [128, 2 * DM], BF16)
            hb = FSB(f"hb{ci}", [128, QT * 1024], F32)
            qsF = mk_qscratch(1024, f"F{ci}", FSB)
            quantF = mkquant(qsF)
            resid_m = [0]
            wkd_m = {}
            btA_m = {}
            wsl_m = {}
            wdl_m = {}

            for iq in range(NQ):
                tiles = [iq * QT + j for j in range(QT)]
                stats_m = emit_ln_stats(tiles, rwaits)

                with nc.psum_tensor(f"pstA{ci}_{iq}", [128, 2048], BF16) as pstA:
                    for j, t in enumerate(tiles):
                        pp = j % 2

                        def _(s, t=t, pp=pp, stats_m=stats_m):
                            if pp in wkd_m:
                                s.wait_ge(sdv.h, wkd_m[pp])
                            s.wait_ge(sdv.h, stats_m)
                            sac.inc(s.activation(wkD[:, pp * DM:(pp + 1) * DM],
                                                 Rt(t), AF.Identity,
                                                 bias=st_a[:, t:t + 1],
                                                 scale=st_r[:, t:t + 1]))
                        S(_)
                        am = sac.n

                        def _(v, pp=pp, am=am):
                            v.wait_ge(sac.h, am)
                            if pp in btA_m:
                                v.wait_ge(spe.h, btA_m[pp])
                            sdv.inc(quantF(v, wkD[:, pp * DM:(pp + 1) * DM], DM,
                                           btA[:, pp * DM:(pp + 1) * DM]))
                        V(_)
                        qm = sdv.n
                        wkd_m[pp] = qm

                        def _(t_, pp=pp, qm=qm):
                            t_.wait_ge(sdv.h, qm)
                            if resid_m[0]:
                                t_.wait_ge(sdv.h, resid_m[0])
                            t_.wait_ge(sac.h, sac.n)
                            last = None
                            for k in range(DK):
                                last = t_.transpose(
                                    pstA[:, pp * 1024 + k * 128: pp * 1024 + (k + 1) * 128],
                                    btA[:, pp * DM + k * 128: pp * DM + (k + 1) * 128],
                                    identb[:])
                            spe.inc(last)
                        T(_)
                        tm = spe.n
                        btA_m[pp] = tm

                        def _(s, j=j, pp=pp, tm=tm):
                            s.wait_ge(spe.h, tm)
                            dst = xq_fm[:].rearrange("p (k q) -> p k q", q=512)[
                                :, :, j * 128:(j + 1) * 128]
                            sac.inc(s.copy(dst,
                                           pstA[:, pp * 1024:(pp + 1) * 1024]
                                           .rearrange("p (k q) -> p k q", q=128)))
                        S(_)
                    xq_m = sac.n

                with nc.psum_tensor(f"psg{ci}_{iq}", [128, 1024], F32) as psg, \
                     nc.psum_tensor(f"psu{ci}_{iq}", [128, 1024], F32) as psu, \
                     nc.psum_tensor(f"pstB{ci}_{iq}", [128, 2048], BF16) as pstB:
                    it = 0
                    qp = 0
                    bmark = {}
                    for pi, (p0, pw) in enumerate(MGP):
                        for sub in range(2):
                            mgi = pi * 2 + sub
                            m0, mw = MGS[mgi]
                            wp = mgi % 2
                            slot = sfg[wp]

                            def _(y, m0=m0, mw=mw, wp=wp, ci=ci, slot=slot):
                                if wp in wsl_m:
                                    y.wait_ge(spe.h, wsl_m[wp])
                                if ci == 0:
                                    y.wait_ge(sgw.h, wprep_marks["wu"])
                                for k in range(DK):
                                    y.dma_start(out=wgbuf[:, wp * 4096 + k * 512: wp * 4096 + k * 512 + mw],
                                                in_=s_wgT[k * 128:(k + 1) * 128, m0:m0 + mw]).then_inc(slot.h, 16)
                                    y.dma_start(out=wubuf[:, wp * 4096 + k * 512: wp * 4096 + k * 512 + mw],
                                                in_=s_wuT[k * 128:(k + 1) * 128, m0:m0 + mw]).then_inc(slot.h, 16)
                            Y(_)
                            slot.n += 16 * 2 * DK
                            wlm = slot.n

                            for j in range(QT):
                                pp = it % 2

                                def _(t_, j=j, pp=pp, mw=mw, wp=wp, wlm=wlm,
                                      xq_m=xq_m, slot=slot, it=it):
                                    t_.wait_ge(slot.h, wlm)
                                    t_.wait_ge(sac.h, xq_m)
                                    if it - 2 in bmark:
                                        t_.wait_ge(sdv.h, bmark[it - 2])
                                    elif resid_m[0]:
                                        t_.wait_ge(sdv.h, resid_m[0])
                                    last = None
                                    for k in range(DK):
                                        lhs = xq_fm[:, k * 512 + j * 128: k * 512 + (j + 1) * 128]
                                        t_.matmul(psg[:, pp * 512: pp * 512 + mw], lhs,
                                                  wgbuf[:, wp * 4096 + k * 512: wp * 4096 + k * 512 + mw],
                                                  start=(k == 0), stop=(k == DK - 1))
                                        last = t_.matmul(psu[:, pp * 512: pp * 512 + mw], lhs,
                                                         wubuf[:, wp * 4096 + k * 512: wp * 4096 + k * 512 + mw],
                                                         start=(k == 0), stop=(k == DK - 1))
                                    spe.inc(last)
                                T(_)
                                mmm = spe.n

                                def _(s, pp=pp, mw=mw, mmm=mmm, it=it):
                                    s.wait_ge(spe.h, mmm)
                                    if it - 2 in bmark:
                                        s.wait_ge(sdv.h, bmark[it - 2])
                                    sac.inc(s.activation(wkA[:, pp * 512: pp * 512 + mw],
                                                         psg[:, pp * 512: pp * 512 + mw],
                                                         AF.Silu))
                                S(_)
                                em = sac.n

                                def _(v, j=j, pp=pp, m0=m0, mw=mw, p0=p0, em=em):
                                    v.wait_ge(sac.h, em)
                                    off = j * 1024 + (m0 - p0)
                                    sdv.inc(v.tensor_tensor(
                                        hb[:, off:off + mw],
                                        wkA[:, pp * 512: pp * 512 + mw],
                                        psu[:, pp * 512: pp * 512 + mw], ALU.mult))
                                V(_)
                                bmark[it] = sdv.n
                                it += 1
                            wsl_m[wp] = spe.n

                        for j in range(QT):
                            qpp = qp % 2
                            qp += 1

                            def _(v, j=j, pw=pw, qpp=qpp):
                                v.wait_ge(spe.h, spe.n)
                                sdv.inc(quantF(v, hb[:, j * 1024: j * 1024 + pw],
                                               pw, btB[:, qpp * DM: qpp * DM + pw]))
                            V(_)
                            hqm = sdv.n

                            def _(t_, pw=pw, qpp=qpp, hqm=hqm):
                                t_.wait_ge(sdv.h, hqm)
                                t_.wait_ge(sac.h, sac.n)
                                last = None
                                for sb in range(pw // 128):
                                    last = t_.transpose(
                                        pstB[:, qpp * 1024 + sb * 128: qpp * 1024 + (sb + 1) * 128],
                                        btB[:, qpp * DM + sb * 128: qpp * DM + (sb + 1) * 128],
                                        identb[:])
                                spe.inc(last)
                            T(_)
                            trm = spe.n

                            def _(s, j=j, p0=p0, pw=pw, qpp=qpp, trm=trm):
                                s.wait_ge(spe.h, trm)
                                kk0 = p0 // 128
                                dst = hq_fm[:].rearrange("p (k q) -> p k q", q=512)[
                                    :, kk0:kk0 + pw // 128, j * 128:(j + 1) * 128]
                                sac.inc(s.copy(dst,
                                               pstB[:, qpp * 1024: qpp * 1024 + pw]
                                               .rearrange("p (k q) -> p k q", q=128)))
                            S(_)
                    hq_m = sac.n

                with nc.psum_tensor(f"psd{ci}_{iq}", [128, 4096], F32) as psd:
                    for kp in range(NKI):
                        wp = kp % 2

                        slot = sfd[wp]

                        def _(y, kp=kp, wp=wp, ci=ci, slot=slot):
                            if wp in wdl_m:
                                y.wait_ge(spe.h, wdl_m[wp])
                            if ci == 0 and kp < 2:
                                y.wait_ge(sgwd.h, wprep_marks["wd"])
                            y.dma_start(out=wdbuf[:, wp * DM:(wp + 1) * DM],
                                        in_=s_wdT[kp * 128:(kp + 1) * 128, :]).then_inc(slot.h, 16)
                        Y(_)
                        slot.n += 16
                        wlm = slot.n

                        def _(t_, kp=kp, wp=wp, wlm=wlm, hq_m=hq_m, slot=slot):
                            t_.wait_ge(slot.h, wlm)
                            t_.wait_ge(sac.h, hq_m)
                            if kp == 0 and resid_m[0]:
                                t_.wait_ge(sdv.h, resid_m[0])
                            last = None
                            for j in range(QT):
                                lhs = hq_fm[:, kp * 512 + j * 128: kp * 512 + (j + 1) * 128]
                                for hf in range(2):
                                    last = t_.matmul(
                                        psd[:, j * 1024 + hf * 512: j * 1024 + (hf + 1) * 512],
                                        lhs,
                                        wdbuf[:, wp * DM + hf * 512: wp * DM + (hf + 1) * 512],
                                        start=(kp == 0), stop=(kp == NKI - 1))
                            spe.inc(last)
                        T(_)
                        wdl_m[wp] = spe.n
                    down_m = spe.n

                    def _(v, tiles=tiles, down_m=down_m):
                        v.wait_ge(spe.h, down_m)
                        t0 = tiles[0]
                        rsl = R[:, t0 * DM:(t0 + QT) * DM]
                        sdv.inc(v.scalar_tensor_tensor(rsl, psd[:], 0.5, rsl,
                                                       ALU.mult, ALU.add))
                    V(_)
                    resid_m[0] = sdv.n
        return sdv.n

    def emit_attn(rwaits):
        with ExitStack() as aes:
            ASB = lambda n, s, d: aes.enter_context(nc.sbuf_tensor(n, s, d))
            xn_fm = ASB("xn_fm", [128, DK * GTOK], BF16)
            qs_b = ASB("qs_b", [128, 2 * GTOK], BF16)
            qp_b = ASB("qp_b", [128, 2 * GTOK], BF16)
            qh = ASB("qh", [64, H * GTOK], BF16)
            kh = ASB("kh", [64, H * GTOK], BF16)
            v_g = ASB("v_g", [128, GT * DM], BF16)
            xg1g = ASB("xg1g", [64, GTOK], BF16)
            sTm = ASB("sTm", [128, 2 * 512], BF16)
            ktm_sb = ASB("ktm_sb", [128, DM], BF16)
            S32 = ASB("S32", [64, H * HD], F32)
            Sb = ASB("Sb", [64, H * HD], BF16)
            wbufA = ASB("wbufA", [128, 3 * 8192], BF16)
            y_fm = ASB("y_fm", [128, DM], BF16)
            gate_sb = ASB("gate_sb", [128, GT * DM], F32)

            stats_m = emit_ln_stats(list(range(NT)), rwaits)

            def _(v):
                v.memset(S32[:], 0.0)
                sdv.inc(v.memset(Sb[:], 0.0))
            V(_)

            with nc.psum_tensor("psA", [128, 2048], F32) as psA, \
                 nc.psum_tensor("psB", [128, 1024], F32) as psB, \
                 nc.psum_tensor("psC", [128, 512], F32) as psC, \
                 nc.psum_tensor("psD", [128, 1024], BF16) as psD:

                rs = {"psC_v": 0, "psD_v": 0, "psD_s": 0, "btA": 0,
                      "psB_s": 0, "psB_v": 0, "wkC_s": 0}
                sTm_rd = {0: 0, 1: 0}
                psA_m = {0: 0, 1: 0}
                gate_rd = [0]
                wsl_rd = {0: 0, 1: 0}
                cs = {}
                gctx = {}

                def sections(g):
                    tiles = [g * GT + j for j in range(GT)]
                    gs = slice(g * GTOK, (g + 1) * GTOK)
                    gp = g % 2
                    ctx = {}

                    for j, t in enumerate(tiles):
                        pp = j % 2

                        def _(s, t=t, pp=pp):
                            s.wait_ge(sdv.h, stats_m)
                            s.wait_ge(spe.h, rs["btA"])
                            sac.inc(s.activation(btA[:, pp * DM:(pp + 1) * DM],
                                                 Rt(t), AF.Identity,
                                                 bias=st_a[:, t:t + 1],
                                                 scale=st_r[:, t:t + 1]))
                        S(_)
                        am = sac.n

                        def _(t_, pp=pp, am=am):
                            t_.wait_ge(sac.h, am)
                            t_.wait_ge(sac.h, rs["psD_s"])
                            t_.wait_ge(sdv.h, rs["psD_v"])
                            last = None
                            for k in range(DK):
                                last = t_.transpose(
                                    psD[:, k * 128:(k + 1) * 128],
                                    btA[:, pp * DM + k * 128: pp * DM + (k + 1) * 128],
                                    identb[:])
                            spe.inc(last)
                        T(_)
                        tm = spe.n
                        rs["btA"] = tm

                        def _(s, j=j, tm=tm):
                            s.wait_ge(spe.h, tm)
                            dst = xn_fm[:].rearrange("p (k q) -> p k q", q=GTOK)[
                                :, :, j * 128:(j + 1) * 128]
                            sac.inc(s.copy(dst, psD[:].rearrange("p (k q) -> p k q", q=128)))
                        S(_)
                        rs["psD_s"] = sac.n
                    xn_m = sac.n
                    ctx["xn_m"] = xn_m

                    wmarks = {}
                    for nm, wp in (("q", 0), ("k", 1)):
                        def _(y, nm=nm, wp=wp, g=g, slot=sat[wp]):
                            y.wait_ge(spe.h, wsl_rd[wp])
                            if g == 0:
                                y.wait_ge(sga.h, wprep_marks["attn"][0])
                                y.wait_ge(sga2.h, wprep_marks["attn"][1])
                            for k in range(DK):
                                y.dma_start(out=wbufA[:, wp * 8192 + k * 1024: wp * 8192 + (k + 1) * 1024],
                                            in_=s_att[nm][k * 128:(k + 1) * 128, :]).then_inc(slot.h, 16)
                        Y(_)
                        sat[wp].n += 16 * DK
                        wmarks[nm] = (wp, sat[wp].n)

                    projpe = {}
                    for nm, htile in (("q", qh), ("k", kh)):
                        wp, wlm = wmarks[nm]
                        for m in range(DK):
                            pp = m % 2

                            def _(t_, m=m, pp=pp, wp=wp, wlm=wlm, xn_m=xn_m):
                                t_.wait_ge(sat[wp].h, wlm)
                                t_.wait_ge(sac.h, xn_m)
                                t_.wait_ge(sac.h, sac.n)
                                t_.wait_ge(sdv.h, psA_m[0])
                                last = None
                                for k in range(DK):
                                    last = t_.matmul(
                                        psA[:, pp * 512: pp * 512 + GTOK],
                                        wbufA[:, wp * 8192 + k * 1024 + m * 128: wp * 8192 + k * 1024 + (m + 1) * 128],
                                        xn_fm[:, k * GTOK:(k + 1) * GTOK],
                                        start=(k == 0), stop=(k == DK - 1))
                                spe.inc(last)
                            T(_)
                            pm = spe.n

                            def _(s, pp=pp, pm=pm):
                                s.wait_ge(spe.h, pm)
                                s.wait_ge(spe.h, spe.n)
                                sac.inc(s.activation(qs_b[:, pp * GTOK:(pp + 1) * GTOK],
                                                     psA[:, pp * 512: pp * 512 + GTOK],
                                                     AF.Silu))
                            S(_)
                            sm = sac.n

                            def _(t_, pp=pp, sm=sm):
                                t_.wait_ge(sac.h, sm)
                                t_.wait_ge(sac.h, sac.n)
                                t_.wait_ge(sdv.h, rs["psB_v"])
                                spe.inc(t_.matmul(psB[:, pp * 512: pp * 512 + GTOK],
                                                  p128t[:],
                                                  qs_b[:, pp * GTOK:(pp + 1) * GTOK],
                                                  start=True, stop=True))
                            T(_)
                            qpm = spe.n

                            def _(s, pp=pp, qpm=qpm):
                                s.wait_ge(spe.h, qpm)
                                s.wait_ge(sdv.h, sdv.n)
                                sac.inc(s.copy(qp_b[:, pp * GTOK:(pp + 1) * GTOK],
                                               psB[:, pp * 512: pp * 512 + GTOK]))
                            S(_)
                            qem = sac.n
                            rs["psB_s"] = qem

                            def _(v, htile=htile, m=m, pp=pp, qem=qem, gs=gs):
                                v.wait_ge(sac.h, qem)
                                v.wait_ge(spe.h, spe.n)
                                v.tensor_tensor(btB[:, 0:GTOK],
                                                qs_b[:, pp * GTOK:(pp + 1) * GTOK],
                                                cosr[:, gs], ALU.mult)
                                v.tensor_tensor(btB[:, GTOK:2 * GTOK],
                                                qp_b[:, pp * GTOK:(pp + 1) * GTOK],
                                                sinr[:, gs], ALU.mult)
                                h0, h1 = 2 * m, 2 * m + 1
                                v.tensor_tensor(htile[:, h0 * GTOK:(h0 + 1) * GTOK],
                                                btB[0:64, 0:GTOK],
                                                btB[0:64, GTOK:2 * GTOK], ALU.add)
                                sdv.inc(v.tensor_tensor(htile[:, h1 * GTOK:(h1 + 1) * GTOK],
                                                        btB[64:128, 0:GTOK],
                                                        btB[64:128, GTOK:2 * GTOK],
                                                        ALU.add))
                            V(_)
                        projpe[nm] = spe.n
                    ctx["qk_m"] = sdv.n
                    wsl_rd[1] = projpe["k"]

                    def _(y, g=g, slot=sat[0]):
                        y.wait_ge(spe.h, projpe["q"])
                        for k in range(DK):
                            y.dma_start(out=wbufA[:, k * 1024:(k + 1) * 1024],
                                        in_=s_att["v"][k * 128:(k + 1) * 128, :]).then_inc(slot.h, 16)
                    Y(_)
                    sat[0].n += 16 * DK
                    vwlm = sat[0].n

                    for c in range(GT):
                        def _(t_, c=c, vwlm=vwlm, xn_m=xn_m):
                            t_.wait_ge(sat[0].h, vwlm)
                            t_.wait_ge(sac.h, xn_m)
                            t_.wait_ge(sac.h, sac.n)
                            t_.wait_ge(sdv.h, psA_m[0])
                            last = None
                            for k in range(DK):
                                lhs = xn_fm[:, k * GTOK + c * 128: k * GTOK + (c + 1) * 128]
                                for hf in range(2):
                                    last = t_.matmul(
                                        psA[:, hf * 512:(hf + 1) * 512], lhs,
                                        wbufA[:, k * 1024 + hf * 512: k * 1024 + (hf + 1) * 512],
                                        start=(k == 0), stop=(k == DK - 1))
                            spe.inc(last)
                        T(_)
                        vm = spe.n

                        def _(s, c=c, vm=vm):
                            s.wait_ge(spe.h, vm)
                            sac.inc(s.copy(v_g[:, c * DM:(c + 1) * DM], psA[:, 0:1024]))
                        S(_)
                    ctx["v_m"] = sac.n
                    wsl_rd[0] = spe.n

                    def _(t_, xn_m=xn_m):
                        t_.wait_ge(sac.h, xn_m)
                        t_.wait_ge(sac.h, wgsmall_m)
                        t_.wait_ge(sac.h, sac.n)
                        t_.wait_ge(sdv.h, rs["psC_v"])
                        last = None
                        for k in range(DK):
                            last = t_.matmul(psC[0:64, 0:GTOK],
                                             wg1b[:, k * HD:(k + 1) * HD],
                                             xn_fm[:, k * GTOK:(k + 1) * GTOK],
                                             start=(k == 0), stop=(k == DK - 1))
                        spe.inc(last)
                    T(_)
                    x1m = spe.n

                    def _(s, x1m=x1m):
                        s.wait_ge(spe.h, x1m)
                        s.wait_ge(spe.h, spe.n)
                        sac.inc(s.copy(xg1g[:, 0:GTOK], psC[0:64, 0:GTOK]))
                    S(_)
                    ctx["xg1_m"] = sac.n
                    gctx[g] = ctx

                def emit_gate(g):
                    ctx = gctx[g]
                    for cl in range(GT):
                        def _(t_, cl=cl, xg1_m=ctx["xg1_m"]):
                            t_.wait_ge(sac.h, xg1_m)
                            t_.wait_ge(sac.h, wgsmall_m)
                            t_.wait_ge(sdv.h, rs["psB_v"])
                            t_.wait_ge(sac.h, rs["psB_s"])
                            last = None
                            for hf in range(2):
                                last = t_.matmul(psB[:, hf * 512:(hf + 1) * 512],
                                                 xg1g[:, cl * 128:(cl + 1) * 128],
                                                 wg2b[:, hf * 512:(hf + 1) * 512],
                                                 start=True, stop=True)
                            spe.inc(last)
                        T(_)
                        gz_m = spe.n

                        def _(s, cl=cl, gz_m=gz_m):
                            s.wait_ge(spe.h, gz_m)
                            s.wait_ge(sdv.h, gate_rd[0])
                            sac.inc(s.activation(
                                gate_sb[:, cl * DM:(cl + 1) * DM],
                                psB[:], AF.Tanh, scale=0.5))
                        S(_)
                        tg_m = sac.n
                        rs["psB_s"] = tg_m

                        def _(v, cl=cl, tg_m=tg_m):
                            v.wait_ge(sac.h, tg_m)
                            gsl = gate_sb[:, cl * DM:(cl + 1) * DM]
                            sdv.inc(v.tensor_scalar(gsl, gsl, 0.5, 0.5,
                                                    ALU.mult, ALU.add))
                        V(_)

                def o_load(g):
                    def _(y, g=g):
                        if g > 0:
                            y.wait_ge(spe.h, cs[2 * g - 1]["wo_m"])
                        for k in range(DK):
                            y.dma_start(out=wbufA[:, 2 * 8192 + k * 1024: 2 * 8192 + (k + 1) * 1024],
                                        in_=s_att["o"][k * 128:(k + 1) * 128, :]).then_inc(sato.h, 16)
                    Y(_)
                    sato.n += 16 * DK
                    gctx[g]["o_m"] = sato.n

                def mixing(c):
                    g, cl = c // GT, c % GT
                    ctx = gctx[g]
                    pc = c % 2

                    out_m = None
                    for b in range(4):
                        pp = b % 2

                        def _(t_, b=b, cl=cl, ctx=ctx):
                            t_.wait_ge(sdv.h, ctx["qk_m"])
                            t_.wait_ge(sac.h, ctx["xg1_m"])
                            t_.wait_ge(sdv.h, rs["psC_v"])
                            last = None
                            for hh in range(4):
                                hc = b * 4 + hh
                                last = t_.matmul(
                                    psC[:, hh * 128:(hh + 1) * 128],
                                    kh[:, hc * GTOK + cl * 128: hc * GTOK + (cl + 1) * 128],
                                    qh[:, hc * GTOK + cl * 128: hc * GTOK + (cl + 1) * 128],
                                    start=True, stop=True)
                            spe.inc(last)
                        T(_)
                        stm = spe.n

                        def _(v, pp=pp, stm=stm):
                            v.wait_ge(spe.h, stm)
                            v.wait_ge(spe.h, sTm_rd[pp])
                            sdv.inc(v.tensor_tensor(sTm[:, pp * 512:(pp + 1) * 512],
                                                    psC[:], mask4[:], ALU.mult))
                        V(_)
                        evm = sdv.n
                        rs["psC_v"] = evm

                        def _(t_, b=b, cl=cl, pp=pp, pc=pc, evm=evm, ctx=ctx, c=c):
                            t_.wait_ge(sdv.h, evm)
                            t_.wait_ge(sac.h, ctx["v_m"])
                            if c >= 1:
                                t_.wait_ge(sdv.h, cs[c - 1]["supd"])
                            t_.wait_ge(sdv.h, psA_m[pc])
                            last = None
                            for hh in range(4):
                                hc = b * 4 + hh
                                t_.matmul(psA[:, pc * 1024 + hc * 64: pc * 1024 + (hc + 1) * 64],
                                          sTm[:, pp * 512 + hh * 128: pp * 512 + (hh + 1) * 128],
                                          v_g[:, cl * DM + hc * 64: cl * DM + (hc + 1) * 64],
                                          start=True, stop=False)
                                last = t_.matmul(psA[:, pc * 1024 + hc * 64: pc * 1024 + (hc + 1) * 64],
                                                 qh[:, hc * GTOK + cl * 128: hc * GTOK + (cl + 1) * 128],
                                                 Sb[:, hc * 64:(hc + 1) * 64],
                                                 start=False, stop=True)
                            spe.inc(last)
                        T(_)
                        sTm_rd[pp] = spe.n
                    out_m = spe.n

                    def _(t_, cl=cl, ctx=ctx):
                        t_.wait_ge(sdv.h, ctx["qk_m"])
                        t_.wait_ge(sdv.h, rs["psD_v"])
                        t_.wait_ge(sac.h, rs["psD_s"])
                        last = None
                        for hc in range(H):
                            last = t_.transpose(
                                psD[:, hc * 64:(hc + 1) * 64],
                                kh[:, hc * GTOK + cl * 128: hc * GTOK + (cl + 1) * 128],
                                identb[0:64, 0:64])
                        spe.inc(last)
                    T(_)
                    ktm_m = spe.n

                    def _(v, ktm_m=ktm_m):
                        v.wait_ge(spe.h, ktm_m)
                        sdv.inc(v.tensor_copy(ktm_sb[:], psD[:]))
                    V(_)
                    ktme_m = sdv.n
                    rs["psD_v"] = ktme_m

                    def _(t_, cl=cl, ktme_m=ktme_m, ctx=ctx):
                        t_.wait_ge(sdv.h, ktme_m)
                        t_.wait_ge(sac.h, ctx["v_m"])
                        t_.wait_ge(sac.h, rs["psB_s"])
                        t_.wait_ge(sdv.h, rs["psB_v"])
                        last = None
                        for hc in range(H):
                            last = t_.matmul(psB[0:64, hc * 64:(hc + 1) * 64],
                                             ktm_sb[:, hc * 64:(hc + 1) * 64],
                                             v_g[:, cl * DM + hc * 64: cl * DM + (hc + 1) * 64],
                                             start=True, stop=True)
                        spe.inc(last)
                    T(_)
                    sup_m = spe.n

                    def _(v, sup_m=sup_m, out_m=out_m):
                        v.wait_ge(spe.h, sup_m)
                        v.wait_ge(spe.h, out_m)
                        v.tensor_tensor(S32[:], S32[:], psB[0:64, :], ALU.add)
                        sdv.inc(v.tensor_copy(Sb[:], S32[:]))
                    V(_)
                    supd_m = sdv.n
                    rs["psB_v"] = supd_m
                    cs[c] = {"supd": supd_m, "out_m": out_m, "pc": pc,
                             "g": g, "cl": cl}

                def epilogue(c):
                    st = cs[c]
                    g, cl, pc = st["g"], st["cl"], st["pc"]
                    ctx = gctx[g]
                    gsl = gate_sb[:, cl * DM:(cl + 1) * DM]

                    def _(v, st=st, gsl=gsl, pc=pc):
                        v.wait_ge(spe.h, st["out_m"])
                        sl = lambda tt: tt[:, 0:1]
                        v.tensor_tensor(wkC[:, DM:2 * DM], gsl,
                                        psA[:, pc * 1024:(pc + 1) * 1024], ALU.mult)
                        v.tensor_reduce(st_s1[:, 0:1],
                                        wkC[:, DM:2 * DM], AX.X, ALU.add)
                        v.tensor_tensor(wkC[:, 0:DM], wkC[:, DM:2 * DM],
                                        wkC[:, DM:2 * DM], ALU.mult)
                        v.tensor_reduce(st_ss[:, 0:1], wkC[:, 0:DM], AX.X, ALU.add)
                        v.tensor_scalar(sl(st_m), sl(st_s1), RD, None, ALU.mult)
                        v.tensor_scalar(sl(st_v), sl(st_ss), RD, None, ALU.mult)
                        v.tensor_tensor(sl(st_t1), sl(st_m), sl(st_m), ALU.mult)
                        v.tensor_tensor(sl(st_v), sl(st_v), sl(st_t1), ALU.subtract)
                        sdv.inc(v.tensor_scalar(sl(st_v), sl(st_v), EPS, None,
                                                ALU.add))
                    V(_)
                    lnv_m = sdv.n
                    gate_rd[0] = lnv_m

                    def _(s, lnv_m=lnv_m):
                        s.wait_ge(sdv.h, lnv_m)
                        sac.inc(s.activation(st_t2[:, 0:1], st_v[:, 0:1], AF.Sqrt))
                    S(_)
                    sqr_m = sac.n

                    def _(v, sqr_m=sqr_m):
                        v.wait_ge(sac.h, sqr_m)
                        sl = lambda tt: tt[:, 0:1]
                        v.reciprocal(sl(st_r), sl(st_t2))
                        v.scalar_tensor_tensor(sl(st_a), sl(st_m), -1.0,
                                               sl(st_r), ALU.mult, ALU.mult)
                        rbc = st_r[:, 0:1].broadcast_to([128, DM])
                        abc = st_a[:, 0:1].broadcast_to([128, DM])
                        v.tensor_tensor(wkD[:, 0:DM], wkC[:, DM:2 * DM], rbc,
                                        ALU.mult)
                        v.tensor_tensor(wkD[:, 0:DM], wkD[:, 0:DM], abc, ALU.add)
                        v.wait_ge(spe.h, rs["btA"])
                        v.tensor_tensor(wkD[:, DM:2 * DM], wkD[:, 0:DM],
                                        gammaT[:], ALU.mult)
                        sdv.inc(v.tensor_tensor(btA[:, 0:DM], wkD[:, DM:2 * DM],
                                                betaT[:], ALU.add))
                    V(_)
                    y2_m = sdv.n

                    def _(t_, y2_m=y2_m):
                        t_.wait_ge(sdv.h, y2_m)
                        t_.wait_ge(sdv.h, rs["psD_v"])
                        t_.wait_ge(sac.h, rs["psD_s"])
                        last = None
                        for k in range(DK):
                            last = t_.transpose(psD[:, k * 128:(k + 1) * 128],
                                                btA[:, k * 128:(k + 1) * 128],
                                                identb[:])
                        spe.inc(last)
                    T(_)
                    ytr_m = spe.n
                    rs["btA"] = ytr_m

                    def _(v, ytr_m=ytr_m):
                        v.wait_ge(spe.h, ytr_m)
                        sdv.inc(v.tensor_copy(y_fm[:], psD[:]))
                    V(_)
                    yfm_m = sdv.n
                    rs["psD_v"] = yfm_m

                    def _(t_, yfm_m=yfm_m, lnv_m=lnv_m, pc=pc, o_m=ctx["o_m"]):
                        t_.wait_ge(sdv.h, yfm_m)
                        t_.wait_ge(sato.h, o_m)
                        t_.wait_ge(sdv.h, lnv_m)
                        last = None
                        for k in range(DK):
                            for hf in range(2):
                                last = t_.matmul(
                                    psA[:, pc * 1024 + hf * 512: pc * 1024 + (hf + 1) * 512],
                                    y_fm[:, k * 128:(k + 1) * 128],
                                    wbufA[:, 2 * 8192 + k * 1024 + hf * 512: 2 * 8192 + k * 1024 + (hf + 1) * 512],
                                    start=(k == 0), stop=(k == DK - 1))
                        spe.inc(last)
                    T(_)
                    wo_m = spe.n
                    st["wo_m"] = wo_m

                    def _(v, c=c, wo_m=wo_m, pc=pc):
                        v.wait_ge(spe.h, wo_m)
                        sdv.inc(v.scalar_tensor_tensor(Rt(c), psA[:, pc * 1024:(pc + 1) * 1024],
                                                       1.0, Rt(c), ALU.mult, ALU.add))
                    V(_)
                    psA_m[pc] = sdv.n

                pending = None
                for g in range(NGRP):
                    sections(g)
                    mixing(2 * g)
                    if pending is not None:
                        epilogue(pending)
                    o_load(g)
                    emit_gate(g)
                    mixing(2 * g + 1)
                    epilogue(2 * g)
                    pending = 2 * g + 1
                epilogue(pending)
        return sdv.n

    emit_phase_barrier()
    r_mark = [(ssu, setup_m)]
    if "ffn1" in PHASES:
        r_mark = [(sdv, emit_ffn_call(0, r_mark))]
        emit_phase_barrier()
    if "attn" in PHASES:
        r_mark = [(sdv, emit_attn(r_mark))]
        emit_phase_barrier()
    if "ffn2" in PHASES:
        r_mark = [(sdv, emit_ffn_call(1, r_mark))]

    def _(g):
        for ctr, m in r_mark:
            g.wait_ge(ctr.h, m)
        for t in range(NT):
            g.dma_start(out=dout[t * TILE:(t + 1) * TILE, :], in_=Rt(t)).then_inc(sgo.h, 16)
        sgo.n += 16 * NT
        g.wait_ge(sgo.h, sgo.n)
    G(_)

    es.close()
    return nc


def _host_consts():
    BASE = 10000.0
    inv_freq = BASE ** (-(np.arange(32, dtype=np.float64) * 2.0) / 64.0)
    pos = np.arange(NTOK, dtype=np.float64)
    ang = pos[:, None] * inv_freq[None, :]
    cos32 = np.cos(ang).T
    sin32 = np.sin(ang).T
    cosr = np.tile(cos32, (4, 1)).astype(ml_dtypes.bfloat16)
    sinr = np.tile(sin32, (4, 1)).astype(ml_dtypes.bfloat16)
    P64 = np.zeros((64, 64), np.float32)
    for i in range(32):
        P64[i, i + 32] = -1.0
        P64[i + 32, i] = 1.0
    P128 = np.zeros((128, 128), np.float32)
    P128[0:64, 0:64] = P64
    P128[64:128, 64:128] = P64
    p128t = P128.T.astype(ml_dtypes.bfloat16)
    ident = np.eye(128, dtype=np.float32).astype(ml_dtypes.bfloat16)
    mask = np.triu(np.ones((128, 128), np.float32))
    mask4 = np.tile(mask, (1, 4)).astype(ml_dtypes.bfloat16)
    return cosr, sinr, p128t, ident, mask4


def kernel(L0_attn_x, L0_attn_Wq, L0_attn_Wk, L0_attn_Wv, L0_attn_Wo,
           L0_attn_Wg1, L0_attn_Wg2, L0_attn_gamma, L0_attn_beta,
           L0_ffn_w_gate, L0_ffn_w_up, L0_ffn_w_down):
    x = np.ascontiguousarray(np.asarray(L0_attn_x, dtype=np.float32))
    B = x.shape[0]
    cosr, sinr, p128t, ident, mask4 = _host_consts()
    gammaT = np.ascontiguousarray(
        np.broadcast_to(np.asarray(L0_attn_gamma, np.float32), (128, DM)))
    betaT = np.ascontiguousarray(
        np.broadcast_to(np.asarray(L0_attn_beta, np.float32), (128, DM)))
    common = {
        "wq": np.ascontiguousarray(np.asarray(L0_attn_Wq, np.float32)),
        "wk": np.ascontiguousarray(np.asarray(L0_attn_Wk, np.float32)),
        "wv": np.ascontiguousarray(np.asarray(L0_attn_Wv, np.float32)),
        "wo": np.ascontiguousarray(np.asarray(L0_attn_Wo, np.float32)),
        "wg1": np.ascontiguousarray(np.asarray(L0_attn_Wg1, np.float32)),
        "wg2": np.ascontiguousarray(np.asarray(L0_attn_Wg2, np.float32)),
        "d_gammaT": gammaT, "d_betaT": betaT,
        "wgate": np.ascontiguousarray(np.asarray(L0_ffn_w_gate, np.float32)),
        "wup": np.ascontiguousarray(np.asarray(L0_ffn_w_up, np.float32)),
        "wdown": np.ascontiguousarray(np.asarray(L0_ffn_w_down, np.float32)),
        "d_cosr": cosr, "d_sinr": sinr, "d_p128t": p128t, "d_ident": ident,
        "d_mask4": mask4,
    }
    nc = build_program()
    in_maps = [dict(common, x=np.ascontiguousarray(x[b])) for b in range(B)]
    res = run_bass_kernel_spmd(nc, in_maps, core_ids=list(range(B)))
    return np.stack([res.results[b]["out"] for b in range(B)], axis=0)
